# version 08
# baseline (speedup 1.0000x reference)
import contextlib

import numpy as np

import concourse.bass as bass
import concourse.mybir as mybir
from concourse.bass_utils import run_bass_kernel_spmd

F32 = mybir.dt.float32
BF16 = mybir.dt.bfloat16
AF = mybir.ActivationFunctionType
OP = mybir.AluOpType
AX = mybir.AxisListType.X

D = 2048
NCH = 16
T = 512
SEQ = 16384
QTOK = 4096
NT_SEQ = SEQ // T
NT_OWN = QTOK // T
HEADS = 16
ISCALE = 1.0 / float(np.sqrt(128.0))
EPS_LN = 1e-5
EPS_T = 1e-10
NW = float(D * D)

ENGS = ("sync", "scalar", "vector", "tensor", "gpsimd")


class Rec:

    def __init__(self):
        self.streams = {e: [] for e in ENGS}
        self.n = {k: 0 for k in ("lx0", "lx1", "sin", "rows", "w0", "w1",
                                 "wt", "misc", "sy2", "sx1", "so", "sx2",
                                 "sxb2", "srows", "sout",
                                 "pck", "pcv", "pcq", "pcu", "pco",
                                 "sy2s", "kvx", "rdz",
                                 "act", "dve", "pe")}
        self.last_wait = {}
        self.buf = {}

    def wait(self, eng, sem, val):
        if not val or val <= 0:
            return
        assert val <= self.n[sem], (eng, sem, val, self.n[sem])
        if self.last_wait.get((eng, sem), 0) >= val:
            return
        self.last_wait[(eng, sem)] = val

        def fn(e, s, sem=sem, val=val):
            e.wait_ge(s[sem], val)
        self.streams[eng].append(fn)

    def raw(self, eng, fn):
        self.streams[eng].append(fn)

    def dma(self, eng, sem, fn):
        self.n[sem] += 16

        def run(e, s, fn=fn, sem=sem):
            fn(e).then_inc(s[sem], 16)
        self.streams[eng].append(run)
        return self.n[sem]

    def act(self, fn):
        self.wait("scalar", "act", self.n["act"])
        self.n["act"] += 1

        def run(e, s, fn=fn):
            fn(e).then_inc(s["act"], 1)
        self.streams["scalar"].append(run)
        return self.n["act"]

    def dve(self, fn):
        self.wait("vector", "dve", self.n["dve"])
        self.n["dve"] += 1

        def run(e, s, fn=fn):
            fn(e).then_inc(s["dve"], 1)
        self.streams["vector"].append(run)
        return self.n["dve"]

    def pe(self, fn, inc=False):
        if inc:
            self.n["pe"] += 1

            def run(e, s, fn=fn):
                fn(e).then_inc(s["pe"], 1)
        else:
            def run(e, s, fn=fn):
                fn(e)
        self.streams["tensor"].append(run)
        return self.n["pe"]

    def use(self, key, sem, cnt):
        d = self.buf.setdefault(key, {})
        d[sem] = max(d.get(sem, 0), cnt)

    def gate(self, eng, key):
        for sem, cnt in sorted(self.buf.get(key, {}).items()):
            self.wait(eng, sem, cnt)

    def merge_use(self, dst, *srcs):
        for s in srcs:
            for sem, cnt in self.buf.get(s, {}).items():
                self.use(dst, sem, cnt)


def build_nc(debug=False, seq=SEQ, qtok=QTOK, race_check=False, no_poll=False):
    nt_seq = seq // T
    nt_own = qtok // T
    nt_a = 2 * nt_own
    half_cols = seq // 2
    nc = bass.Bass("TRN2", target_bir_lowering=False,
                   detect_race_conditions=race_check)

    x_in = nc.dram_tensor("x", [D, 2 * qtok], F32, kind="ExternalInput").ap()
    wT = {}
    for nm in ("wq", "wu", "wo", "wf"):
        wT[nm] = nc.dram_tensor(nm + "T", [D, D], F32, kind="ExternalInput").ap()
    for nm in ("wk", "wv"):
        wT[nm] = nc.dram_tensor(nm + "T", [D, D // 2], F32, kind="ExternalInput").ap()
    salt_in = nc.dram_tensor("salt", [1, 4], mybir.dt.uint32, kind="ExternalInput").ap()
    bias_in = nc.dram_tensor("bias", [128, 16], F32, kind="ExternalInput").ap()
    consts_in = nc.dram_tensor("consts_in", [128, 144], F32, kind="ExternalInput").ap()
    out_d = nc.dram_tensor("out", [D, qtok], F32, kind="ExternalOutput").ap()

    skind = "ExternalOutput" if debug else "Internal"
    y2_s = nc.dram_tensor("y2_s", [D, 2 * qtok], BF16, kind=skind).ap()
    y2_sib = nc.dram_tensor("y2_sib", [D, 2 * qtok], BF16, kind="Internal").ap()
    y2_sh = nc.dram_tensor("y2_sh", [D, seq], BF16, addr_space="Shared").ap()
    kv_sh = nc.dram_tensor("kv_sh", [2, 128, D // 2], BF16, addr_space="Shared").ap()
    fl_sh = nc.dram_tensor("fl_sh", [2, 2, 16], mybir.dt.uint32, addr_space="Shared").ap()
    x1_s = nc.dram_tensor("x1_s", [D, qtok], F32, kind=skind).ap()
    o_s = nc.dram_tensor("o_s", [D, qtok], BF16, kind=skind).ap()
    x2_s = nc.dram_tensor("x2_s", [D, qtok], F32, kind=skind).ap()
    xb2_s = nc.dram_tensor("xb2_s", [D, qtok], BF16, kind=skind).ap()
    wt_s = nc.dram_tensor("wt_s", [D, D], BF16, kind=skind).ap()
    w_bf = {}
    for nm in ("wk", "wv", "wq", "wu", "wo"):
        w_bf[nm] = nc.dram_tensor(nm + "_bf", [D, D], BF16, kind="Internal").ap()
    m3_s = nc.dram_tensor("m3_s", [1, qtok], F32, kind=skind).ap()
    r3_s = nc.dram_tensor("r3_s", [1, qtok], F32, kind=skind).ap()
    dbg_fl = nc.dram_tensor("dbg_fl", [1, 16], mybir.dt.uint32, kind="ExternalOutput").ap()
    if debug:
        kv_dbg = nc.dram_tensor("kv_dbg", [128, D], F32, kind="ExternalOutput").ap()

    rec = Rec()

    with contextlib.ExitStack() as stack:
        E = stack.enter_context
        arena0 = E(nc.sbuf_tensor("arena0", [128, NCH * D], BF16))
        arena1 = E(nc.sbuf_tensor("arena1", [128, NCH * D], BF16))
        sin = E(nc.sbuf_tensor("sin", [128, NCH * T], BF16))
        aux_a = E(nc.sbuf_tensor("aux_a", [128, NCH * T], BF16))
        aux_b = E(nc.sbuf_tensor("aux_b", [128, NCH * T], BF16))
        stg = E(nc.sbuf_tensor("stg", [128, D], F32))
        sqb = E(nc.sbuf_tensor("sqb", [128, 2 * T], BF16))
        wstg = E(nc.sbuf_tensor("wstg", [128, 1024], BF16))
        kvbf = E(nc.sbuf_tensor("kvbf", [128, D], BF16))
        dvA = E(nc.sbuf_tensor("dvA", [128, T], F32))
        dvB = E(nc.sbuf_tensor("dvB", [128, T], F32))
        qh = E(nc.sbuf_tensor("qh", [128, 2 * T], BF16))
        uh = E(nc.sbuf_tensor("uh", [128, 2 * T], F32))
        consts = E(nc.sbuf_tensor("consts", [128, 144], F32))
        consts2 = E(nc.sbuf_tensor("consts2", [128, 4], F32))
        onesb = E(nc.sbuf_tensor("onesb", [128, 2], BF16))
        c_wt = E(nc.sbuf_tensor("c_wt", [128, 16], F32))
        hb = E(nc.sbuf_tensor("hb", [128, 16], F32))
        rows1 = E(nc.sbuf_tensor("rows1", [65, T], F32))
        rows2 = E(nc.sbuf_tensor("rows2", [65, T], F32))
        salt_sb = E(nc.sbuf_tensor("salt_sb", [1, 4], mybir.dt.uint32))
        flag_rd = E(nc.sbuf_tensor("flag_rd", [1, 4], mybir.dt.uint32))

        psb = [E(nc.psum_tensor("p%d" % i, [128, 512], F32)) for i in range(4)]
        p_kv = E(nc.psum_tensor("p_kv", [128, 2048], F32))

        sems = {}
        for s in ("lx0", "lx1", "sin", "rows", "w0", "w1", "wt", "misc",
                  "sy2", "sx1", "so", "sx2", "sxb2", "srows", "sout",
                  "pck", "pcv", "pcq", "pcu", "pco",
                  "sy2s", "kvx", "rdz",
                  "act", "dve", "pe"):
            sems[s] = E(nc.semaphore("s_" + s))
        sems["pol"] = E(nc.semaphore("s_pol"))
        block = E(nc.Block())

        a1f = arena1.ap().bitcast(F32)
        xbuf = [a1f[:, 0:NCH * T], a1f[:, NCH * T:2 * NCH * T]]
        p4 = p_kv[:, 0:512]
        p5 = p_kv[:, 512:1024]
        p6 = p_kv[:, 1024:1536]
        p7 = p_kv[:, 1536:2048]
        bkey = ["b0", "b1", "b2", "b3", "b4", "b5", "b6", "b7"]

        def wsl(arena, k, j0, w):
            return arena[:, k * D + j0:k * D + j0 + w]

        def chk(buf, c, w=T):
            return buf[:, c * w:(c + 1) * w]

        onescol = consts[:, 0:1]
        half11 = consts[0:1, 1:2]
        inv2048 = consts[:, 2:3]
        onesrow = consts[0:1, 4:132]
        thr_col = consts2[:, 0:1]
        half_scale = consts2[0:1, 1:2]
        mean_w = consts2[0:1, 2:3]
        acc_col = consts2[:, 3:4]
        ones_mean = onesb[:, 0:1]
        ones_one = onesb[:, 1:2]

        for _cix, _cval in enumerate((EPS_LN, ISCALE, 1.0 / 128.0, 0.0, 1.0,
                                      1e-12, 0.5, 1.0 / NW, EPS_T)):
            nc.const_aps.register(F32, _cval, consts[:, 132 + _cix:133 + _cix])

        mrow = rows1[0:1, :]
        trow = rows1[32:33, :]
        srow = rows1[64:65, :]
        rrow = rows2[0:1, :]
        r1row = rows2[32:33, :]

        xv = x_in.rearrange("(c p) t -> p c t", p=128)
        y2v = y2_s.rearrange("(c p) t -> p c t", p=128)
        y2sv = y2_sib.rearrange("(c p) t -> p c t", p=128)
        x1v = x1_s.rearrange("(c p) t -> p c t", p=128)
        ov = o_s.rearrange("(c p) t -> p c t", p=128)
        x2v = x2_s.rearrange("(c p) t -> p c t", p=128)
        xb2v = xb2_s.rearrange("(c p) t -> p c t", p=128)
        outv = out_d.rearrange("(c p) t -> p c t", p=128)

        def tile_dma_in(dramv, col0, sbuf_buf, sem):
            for j in range(4):
                dsl = dramv[:, 4 * j:4 * j + 4, col0:col0 + T]
                ssl = sbuf_buf[:, 4 * j * T:(4 * j + 4) * T].rearrange(
                    "p (c t) -> p c t", t=T)
                end = rec.dma("sync", sem,
                              lambda e, a=ssl, b=dsl: e.dma_start(out=a, in_=b))
            return end

        def tile_dma_out(dramv, col0, sbuf_buf, sem, serialize=False):
            if serialize:
                rec.wait("gpsimd", sem, rec.n[sem])
            for j in range(4):
                dsl = dramv[:, 4 * j:4 * j + 4, col0:col0 + T]
                ssl = sbuf_buf[:, 4 * j * T:(4 * j + 4) * T].rearrange(
                    "p (c t) -> p c t", t=T)
                end = rec.dma("gpsimd", sem,
                              lambda e, a=dsl, b=ssl: e.dma_start(out=a, in_=b))
            return end

        c_ld = rec.dma("sync", "rows", lambda e: e.dma_start(out=consts[:, :], in_=consts_in[:, :]))
        rec.dma("sync", "rows", lambda e: e.dma_start(out=salt_sb[0:1, :], in_=salt_in[:, :]))
        c_salt = rec.n["rows"]
        rec.wait("scalar", "rows", c_salt)
        rec.act(lambda e: e.copy(out=ones_mean, in_=inv2048))
        c_ones = rec.act(lambda e: e.copy(out=ones_one, in_=onescol))

        red_dve = []
        for hix in range(2 * NCH):
            k, half = divmod(hix, 2)
            sl = stg[:, half * 1024:(half + 1) * 1024]
            src = wT["wf"][k * 128:(k + 1) * 128, half * 1024:(half + 1) * 1024]
            if hix >= 2:
                rec.wait("gpsimd", "dve", red_dve[hix - 2])
            c = rec.dma("gpsimd", "w%d" % (hix % 2),
                        lambda e, a=sl, b=src: e.dma_start(out=a, in_=b))
            rec.wait("vector", "w%d" % (hix % 2), c)
            if hix == 0:
                rec.dve(lambda e, a=sl: e.tensor_reduce(
                    out=acc_col, in_=a, axis=AX, op=OP.add, apply_absolute_value=True))
            else:
                rec.dve(lambda e, a=sl: e.tensor_reduce(
                    out=dvA[:, 0:1], in_=a, axis=AX, op=OP.add, apply_absolute_value=True))
                rec.dve(lambda e: e.tensor_tensor(
                    out=acc_col, in0=acc_col, in1=dvA[:, 0:1], op=OP.add))
            red_dve.append(rec.n["dve"])
        rec.wait("tensor", "dve", red_dve[-1])
        rec.wait("tensor", "rows", c_salt)
        c_tot = rec.pe(lambda e: e.matmul(psb[0][0:1, 0:1], acc_col, onescol,
                                          start=True, stop=True), inc=True)
        rec.wait("vector", "pe", c_tot)
        rec.dve(lambda e: e.tensor_scalar(out=mean_w, in0=psb[0][0:1, 0:1],
                                          scalar1=1.0 / NW, scalar2=None, op0=OP.mult))
        rec.dve(lambda e: e.tensor_scalar(out=rows1[0:1, 0:1], in0=mean_w,
                                          scalar1=EPS_T, scalar2=0.5,
                                          op0=OP.add, op1=OP.mult))
        c_hs = rec.dve(lambda e: e.tensor_scalar(out=half_scale, in0=mean_w,
                                                 scalar1=0.5, scalar2=None, op0=OP.mult))
        rec.wait("tensor", "dve", c_hs)
        c_thrb = rec.pe(lambda e: e.matmul(psb[1][:, 0:1], onesrow, rows1[0:1, 0:1],
                                           start=True, stop=True), inc=True)
        rec.wait("scalar", "pe", c_thrb)
        c_thr = rec.act(lambda e: e.copy(out=thr_col, in_=psb[1][:, 0:1]))

        rec.wait("gpsimd", "dve", red_dve[-1])
        tern_act = []
        tern_dve = []
        for hix in range(2 * NCH):
            k, half = divmod(hix, 2)
            sl = stg[:, half * 1024:(half + 1) * 1024]
            src = wT["wf"][k * 128:(k + 1) * 128, half * 1024:(half + 1) * 1024]
            if hix >= 2:
                rec.wait("gpsimd", "act", tern_act[hix - 2])
                rec.wait("gpsimd", "dve", tern_dve[hix - 2])
            c = rec.dma("gpsimd", "w%d" % (hix % 2),
                        lambda e, a=sl, b=src: e.dma_start(out=a, in_=b))
            rec.wait("vector", "w%d" % (hix % 2), c)
            rec.wait("vector", "act", c_thr)
            rec.wait("scalar", "w%d" % (hix % 2), c)
            for qx in range(2):
                q = sl[:, qx * 512:(qx + 1) * 512]
                dst = arena0[:, k * D + half * 1024 + qx * 512:
                             k * D + half * 1024 + (qx + 1) * 512]
                c_abs = rec.act(lambda e, q=q: e.activation(
                    out=dvA[:, :], in_=q, func=AF.Abs))
                rec.wait("vector", "act", c_abs)
                rec.dve(lambda e: e.tensor_scalar(
                    out=dvA[:, :], in0=dvA[:, :], scalar1=thr_col, scalar2=None, op0=OP.is_gt))
                c_s = rec.act(lambda e, q=q: e.sign(out=dvB[:, :], in_=q))
                rec.wait("vector", "act", c_s)
                c_w = rec.dve(lambda e: e.tensor_tensor(
                    out=dvB[:, :], in0=dvA[:, :], in1=dvB[:, :], op=OP.mult))
                rec.wait("scalar", "dve", c_w)
                c_cp = rec.act(lambda e, dst=dst: e.copy(out=dst, in_=dvB[:, :]))
            tern_act.append(c_cp)
            tern_dve.append(c_w)
        c_tern = tern_act[-1]
        rec.use("stg", "act", c_tern)
        rec.use("stg", "dve", tern_dve[-1])

        rec.wait("gpsimd", "act", c_tern)
        for k in range(NCH):
            sl = arena0[:, k * D:(k + 1) * D]
            dst = wt_s[k * 128:(k + 1) * 128, :]
            rec.dma("gpsimd", "wt", lambda e, a=dst, b=sl: e.dma_start(out=a, in_=b))
        wt_store_total = rec.n["wt"]
        rec.use("arena0", "wt", wt_store_total)

        rec.wait("tensor", "act", max(c_tern, c_ones))
        for j in range(NCH):
            bank = j % 4
            rec.gate("tensor", bkey[bank])
            for k in range(NCH):
                c_cwt_pe = rec.pe(lambda e, j=j, k=k, bank=bank: e.matmul(
                    psb[bank][:, 0:1], wsl(arena0, k, j * 128, 128), ones_one,
                    start=(k == 0), stop=(k == NCH - 1)),
                    inc=(k == NCH - 1))
            rec.wait("scalar", "pe", c_cwt_pe)
            c_cwt = rec.act(lambda e, j=j, bank=bank: e.copy(
                out=c_wt[:, j:j + 1], in_=psb[bank][:, 0:1]))
            rec.use(bkey[bank], "act", c_cwt)

        c_brow = rec.dma("gpsimd", "misc",
                         lambda e: e.dma_start(out=hb[:, :], in_=bias_in[:, :]))
        rec.wait("scalar", "misc", c_brow)
        c_hb = rec.act(lambda e: e.activation(out=hb[:, :], in_=hb[:, :],
                                              func=AF.Copy, scale=0.5))

        def ln_smalls(pe_stats, mul_half_scale):
            rec.wait("scalar", "pe", pe_stats)
            rec.gate("scalar", "rows")
            a_m = rec.act(lambda e: e.copy(out=mrow, in_=psb[2][0:1, :]))
            rec.use("b2", "act", a_m)
            rec.wait("vector", "act", a_m)
            rec.gate("vector", "rows")
            rec.dve(lambda e: e.tensor_tensor(out=trow, in0=mrow, in1=mrow, op=OP.mult))
            d_var = rec.dve(lambda e: e.tensor_tensor(
                out=trow, in0=psb[3][0:1, :], in1=trow, op=OP.subtract))
            rec.use("b3", "dve", d_var)
            rec.wait("scalar", "dve", d_var)
            a_s = rec.act(lambda e: e.activation(out=srow, in_=trow, func=AF.Sqrt,
                                                 bias=EPS_LN))
            rec.wait("vector", "act", a_s)
            rec.dve(lambda e: e.reciprocal(out=r1row, in_=srow))
            if mul_half_scale:
                d_r = rec.dve(lambda e: e.tensor_scalar(
                    out=rrow, in0=r1row, scalar1=half_scale, scalar2=None, op0=OP.mult))
            else:
                d_r = rec.dve(lambda e: e.tensor_copy(out=rrow, in_=r1row))
            return d_r

        def ln_squares(src_acts, dst_buf, dst_key):
            sq_acts = []
            rec.gate("scalar", dst_key)
            for c, (ap_, cnt) in enumerate(src_acts):
                sq_acts.append(rec.act(lambda e, ap_=ap_, c=c: e.activation(
                    out=chk(dst_buf, c), in_=ap_, func=AF.Square)))
            return sq_acts

        def ln_stats_mms(src_acts, sq_acts, sq_buf, sq_key):
            rec.gate("tensor", "b2")
            rec.gate("tensor", "b3")
            for c, (ap_, cnt) in enumerate(src_acts):
                if cnt is not None:
                    rec.wait("tensor", "act", cnt)
                rec.pe(lambda e, ap_=ap_, c=c: e.matmul(
                    psb[2][0:1, :], ones_mean, ap_,
                    start=(c == 0), stop=(c == NCH - 1)), inc=(c == NCH - 1))
            for c in range(NCH):
                rec.wait("tensor", "act", sq_acts[c])
                pe_c = rec.pe(lambda e, c=c, sq_buf=sq_buf: e.matmul(
                    psb[3][0:1, :], ones_mean, chk(sq_buf, c),
                    start=(c == 0), stop=(c == NCH - 1)), inc=(c == NCH - 1))
            rec.use(sq_key, "pe", pe_c)
            return pe_c

        def bcasts_45(row_a, row_b, dve_ready):
            rec.wait("tensor", "dve", dve_ready)
            rec.gate("tensor", "b4")
            rec.gate("tensor", "b5")
            rec.pe(lambda e: e.matmul(p4, onesrow, row_a, start=True, stop=True))
            ret = rec.pe(lambda e: e.matmul(p5, onesrow, row_b,
                                            start=True, stop=True), inc=True)
            rec.use("rows", "pe", ret)
            return ret

        def load_weight_fp32(arena, arena_key, src, half=False):
            done = [0] * NCH
            first = True
            nh = NCH if half else 2 * NCH
            for hix in range(nh):
                if half:
                    k, hf = hix, 0
                    srcsl = src[k * 128:(k + 1) * 128, 0:1024]
                    dstsl = arena[:, k * 1024:(k + 1) * 1024]
                else:
                    k, hf = divmod(hix, 2)
                    srcsl = src[k * 128:(k + 1) * 128, hf * 1024:(hf + 1) * 1024]
                    dstsl = arena[:, k * D + hf * 1024:k * D + (hf + 1) * 1024]
                slot = hix % 2
                sl = stg[:, slot * 1024:(slot + 1) * 1024]
                skey = "stgh%d" % slot
                rec.gate("gpsimd", skey)
                if first:
                    rec.gate("gpsimd", "stg")
                c = rec.dma("gpsimd", "w%d" % slot,
                            lambda e, a=sl, b=srcsl: e.dma_start(out=a, in_=b))
                if first:
                    rec.gate("scalar", arena_key)
                    first = False
                rec.wait("scalar", "w%d" % slot, c)
                cc = rec.act(lambda e, a=dstsl, b=sl: e.copy(out=a, in_=b))
                rec.use(skey, "act", cc)
                if half or hf == 1:
                    done[k] = cc
            return done

        def load_weight_bf16(arena, arena_key, src, pc_sem=None):
            rec.gate("gpsimd", arena_key)
            if pc_sem is not None:
                rec.wait("gpsimd", pc_sem, rec.n[pc_sem])
            rec.wait("gpsimd", "wt", rec.n["wt"])
            for k in range(NCH):
                done = rec.dma("gpsimd", "wt",
                               lambda e, a=arena[:, k * D:(k + 1) * D],
                               b=src[k * 128:(k + 1) * 128, :]:
                               e.dma_start(out=a, in_=b))
            return done

        POOL_ENG = [mybir.EngineType.Pool]

        def emit_switch_pool(arm_fn, sem_incs):
            def closure(e, s):
                for arm in nc.switch_core_id(engines=[e], n=128):
                    arm_fn(int(arm) % 2, e, s)
            rec.raw("gpsimd", closure)
            for sem, inc in sem_incs.items():
                rec.n[sem] += inc

        _pollid = [0]

        def emit_poll(phase, dep_sem, dep_val):
            def closure(e, s, phase=phase):
                e.wait_ge(s[dep_sem], dep_val)
                pid = _pollid[0]
                _pollid[0] += 1
                lbl = "poll_%d" % pid
                c2 = lbl + "_c2"
                done = lbl + "_done"
                e.br(lbl)
                with nc.body(lbl, valid_engines=POOL_ENG):
                    e.dma_start(out=flag_rd[0:1, 0:1],
                                in_=fl_sh[0:1, phase, 0:1]).then_inc(s["pol"], 16)
                    e.dma_start(out=flag_rd[0:1, 1:2],
                                in_=fl_sh[1:2, phase, 0:1]).then_inc(s["pol"], 16)
                    e.nop(cycle_cnt=60000)
                    v0 = e.value_load(flag_rd[0:1, 0:1])
                    vs = e.value_load(salt_sb[0:1, 0:1])
                    e.br_ne(v0, vs, on_true=lbl, on_false=c2)
                with nc.body(c2, valid_engines=POOL_ENG):
                    v1 = e.value_load(flag_rd[0:1, 1:2])
                    vs2 = e.value_load(salt_sb[0:1, 0:1])
                    e.br_ne(v1, vs2, on_true=lbl, on_false=done)
                nc.switch_bb(done)
                e.sem_inc(s["rdz"], 1)
            rec.raw("gpsimd", closure)
            rec.n["rdz"] += 1
            return rec.n["rdz"]

        wt_cast = [c_tern] * NCH

        y2st = [0] * nt_a
        x1st = [0] * nt_own

        pc_jobs = []
        for nm, sem in (("wk", "pck"), ("wv", "pcv"), ("wq", "pcq"),
                        ("wu", "pcu"), ("wo", "pco")):
            for k in range(NCH):
                for half in range(2):
                    pc_jobs.append((nm, sem, k, half))
        pc_done = {}

        def emit_precast(njobs):
            for _ in range(njobs):
                if not pc_jobs:
                    return
                nm, sem, k, half = pc_jobs.pop(0)
                if nm == "wk" and k == 0 and half == 0:
                    rec.gate("gpsimd", "stg")
                slot = (k * 2 + half) % 2
                sl = stg[:, slot * 1024:(slot + 1) * 1024]
                bsl = wstg[:, 0:1024]
                rec.gate("gpsimd", "stgh%d" % slot)
                c = rec.dma("gpsimd", "w%d" % slot,
                            lambda e, a=sl, b=wT[nm][k * 128:(k + 1) * 128,
                                                     half * 1024:(half + 1) * 1024]:
                            e.dma_start(out=a, in_=b))
                rec.wait("scalar", "w%d" % slot, c)
                rec.gate("scalar", "wstg")
                cc = rec.act(lambda e, a=bsl, b=sl: e.copy(out=a, in_=b))
                rec.use("stgh%d" % slot, "act", cc)
                rec.wait("gpsimd", "act", cc)
                st = rec.dma("gpsimd", sem,
                             lambda e, a=w_bf[nm][k * 128:(k + 1) * 128,
                                                  half * 1024:(half + 1) * 1024],
                             b=bsl: e.dma_start(out=a, in_=b))
                rec.use("wstg", sem, st)
                pc_done[nm] = (sem, st)

        pc_jobs.clear()
        pc_per_tile = 0

        def pass_A():
            for i in range(nt_a):
                xb = xbuf[i % 2]
                xkey = "xbuf%d" % (i % 2)
                lxs = "lx%d" % (i % 2)
                rec.gate("sync", xkey)
                lds = tile_dma_in(xv, i * T, xb, lxs)
                cast_a = []
                for c in range(NCH):
                    rec.wait("scalar", lxs, lds)
                    rec.gate("scalar", "sin")
                    cast_a.append(rec.act(lambda e, c=c, xb=xb: e.copy(
                        out=chk(sin, c), in_=chk(xb, c))))
                rec.use(xkey, "act", cast_a[-1])
                src1 = [(chk(sin, c), cast_a[c]) for c in range(NCH)]
                sq1 = ln_squares(src1, aux_a, "aux_a")
                pe_st1 = ln_stats_mms(src1, sq1, aux_a, "aux_a")
                rec.use("sin", "pe", pe_st1)
                rec.wait("vector", "pe", pe_st1)
                d_r1 = ln_smalls(pe_st1, mul_half_scale=True)
                ffn_pe = [0] * NCH
                bc1 = None

                def ffn_chunk(e_ix):
                    bank = e_ix % 4
                    rec.gate("tensor", bkey[bank])
                    for k in range(NCH):
                        rec.wait("tensor", "act", wt_cast[k])
                        rec.wait("tensor", "act", cast_a[k])
                        ffn_pe[e_ix] = rec.pe(
                            lambda e, k=k, e_ix=e_ix, bank=bank: e.matmul(
                                psb[bank][:, :], wsl(arena0, k, e_ix * 128, 128),
                                chk(sin, k), start=(k == 0), stop=(k == NCH - 1)),
                            inc=(k == NCH - 1))

                evac_d = [0] * NCH

                def evac_chunk(e_ix):
                    bank = e_ix % 4
                    rec.wait("vector", "pe", ffn_pe[e_ix])
                    rec.wait("vector", "pe", bc1)
                    rec.dve(lambda e, e_ix=e_ix: e.tensor_scalar(
                        out=dvA[:, :], in0=p4, scalar1=c_wt[:, e_ix:e_ix + 1],
                        scalar2=None, op0=OP.mult))
                    rec.dve(lambda e, bank=bank: e.tensor_tensor(
                        out=dvB[:, :], in0=psb[bank][:, :], in1=dvA[:, :], op=OP.subtract))
                    rec.dve(lambda e: e.tensor_tensor(
                        out=dvA[:, :], in0=dvB[:, :], in1=p5, op=OP.mult))
                    evac_d[e_ix] = rec.dve(lambda e, e_ix=e_ix, xb=xb: e.scalar_tensor_tensor(
                        out=chk(xb, e_ix), in0=dvA[:, :], scalar=hb[:, e_ix:e_ix + 1],
                        in1=chk(xb, e_ix), op0=OP.add, op1=OP.add))
                    rec.use(bkey[bank], "dve", evac_d[e_ix])
                    rec.use("b4", "dve", evac_d[e_ix])
                    rec.use("b5", "dve", evac_d[e_ix])

                for e_ix in range(4):
                    ffn_chunk(e_ix)
                bc1 = bcasts_45(mrow, rrow, d_r1)
                for e_ix in range(4, NCH):
                    evac_chunk(e_ix - 4)
                    ffn_chunk(e_ix)
                for e_ix in range(NCH - 4, NCH):
                    evac_chunk(e_ix)
                rec.use("sin", "pe", ffn_pe[NCH - 1])
                if i < nt_own:
                    rec.wait("gpsimd", "dve", evac_d[NCH - 1])
                    st = tile_dma_out(x1v, i * T, xb, "sx1", serialize=True)
                    rec.use(xkey, "sx1", st)
                    x1st[i] = st
                cast1_a = []
                for c in range(NCH):
                    rec.wait("scalar", "dve", evac_d[c])
                    rec.gate("scalar", "aux_a")
                    cast1_a.append(rec.act(lambda e, c=c, xb=xb: e.copy(
                        out=chk(aux_a, c), in_=chk(xb, c))))
                rec.use(xkey, "act", cast1_a[-1])
                src2 = [(chk(aux_a, c), cast1_a[c]) for c in range(NCH)]
                sq2 = ln_squares(src2, aux_b, "aux_b")
                pe_st2 = ln_stats_mms(src2, sq2, aux_b, "aux_b")
                rec.use("aux_a", "pe", pe_st2)
                rec.wait("vector", "pe", pe_st2)
                d_r2 = ln_smalls(pe_st2, mul_half_scale=False)
                bc2 = bcasts_45(mrow, rrow, d_r2)
                app_d = [0] * NCH
                for c in range(NCH):
                    rec.wait("vector", "pe", bc2)
                    rec.gate("vector", "aux_b")
                    rec.dve(lambda e, c=c, xb=xb: e.tensor_tensor(
                        out=dvB[:, :], in0=chk(xb, c), in1=p4, op=OP.subtract))
                    app_d[c] = rec.dve(lambda e, c=c: e.tensor_tensor(
                        out=chk(aux_b, c), in0=dvB[:, :], in1=p5, op=OP.mult))
                rec.use("b4", "dve", app_d[-1])
                rec.use("b5", "dve", app_d[-1])
                rec.use(xkey, "dve", app_d[-1])
                rec.wait("gpsimd", "dve", app_d[NCH - 1])
                st2 = tile_dma_out(y2v, i * T, aux_b, "sy2")
                rec.use("aux_b", "sy2", st2)
                y2st[i] = st2
            rec.use("arena0", "pe", ffn_pe[NCH - 1])
            rec.merge_use("arena1", "xbuf0", "xbuf1")
            return

        pass_A()

        def pass_BC():
            rec.wait("gpsimd", "sy2", y2st[nt_a - 1])
            rec.wait("gpsimd", "rows", c_salt)

            def arm_out(par, e, s):
                for j in range(4):
                    srcv = y2_s[j * 512:(j + 1) * 512, :]
                    dstv = y2_sh[j * 512:(j + 1) * 512,
                                 par * half_cols:par * half_cols + 2 * qtok]
                    e.dma_start(out=dstv, in_=srcv).then_inc(s["sy2s"], 16)
            emit_switch_pool(arm_out, {"sy2s": 64})
            rec.wait("gpsimd", "sy2s", rec.n["sy2s"])

            def arm_flag1(par, e, s):
                e.dma_start(out=fl_sh[par:par + 1, 0, 0:1],
                            in_=salt_sb[0:1, 0:1]).then_inc(s["sy2s"], 16)
            emit_switch_pool(arm_flag1, {"sy2s": 16})
            wk_cast = load_weight_fp32(arena0, "arena0", wT["wk"], half=True)
            wv_cast = load_weight_fp32(arena1, "arena1", wT["wv"], half=True)
            rdz1 = emit_poll(0, "sy2s", rec.n["sy2s"])

            def arm_in(par, e, s):
                sib = 1 - par
                for j in range(4):
                    srcv = y2_sh[j * 512:(j + 1) * 512,
                                 sib * half_cols:sib * half_cols + 2 * qtok]
                    dstv = y2_sib[j * 512:(j + 1) * 512, :]
                    e.dma_start(out=dstv, in_=srcv).then_inc(s["sy2s"], 16)
            emit_switch_pool(arm_in, {"sy2s": 64})
            y2sib_done = rec.n["sy2s"]
            for b in range(4, 8):
                rec.gate("scalar", bkey[b])
            c_zkv = rec.act(lambda e: e.memzero(p_kv[:, :]))
            rec.wait("tensor", "act", c_zkv)
            kv_last = 0
            for i in range(nt_seq):
                rec.gate("sync", "sin")
                if i < nt_a:
                    rec.wait("sync", "sy2", y2st[nt_a - 1])
                    lds = tile_dma_in(y2v, i * T, sin, "sin")
                else:
                    rec.wait("sync", "rdz", rdz1)
                    rec.wait("sync", "sy2s", y2sib_done)
                    lds = tile_dma_in(y2sv, (i - nt_a) * T, sin, "sin")
                proj_pe = {}
                evac_a = {}
                for pj, (arena, wcast, dstbuf, func) in enumerate((
                        (arena0, wk_cast, aux_a, "K"),
                        (arena1, wv_cast, aux_b, "V"))):
                    for s in range(4):
                        for strip in range(2):
                            bank = 2 * pj + (strip % 2)
                            rec.gate("tensor", bkey[bank])
                            rec.wait("tensor", "sin", lds)
                            for k in range(NCH):
                                rec.wait("tensor", "act", wcast[k])
                                pc = rec.pe(lambda e, k=k, s=s, strip=strip, bank=bank,
                                            arena=arena: e.matmul(
                                    psb[bank][:, :],
                                    sin[:, k * T + s * 128:k * T + (s + 1) * 128],
                                    arena[:, k * 1024 + strip * 512:
                                          k * 1024 + (strip + 1) * 512],
                                    start=(k == 0), stop=(k == NCH - 1)),
                                    inc=(k == NCH - 1))
                            proj_pe[(pj, s, strip)] = pc
                            rec.wait("scalar", "pe", pc)
                            rec.gate("scalar", "aux_a" if pj == 0 else "aux_b")
                            dst = dstbuf[:, s * 1024 + strip * 512:
                                         s * 1024 + (strip + 1) * 512]
                            if func == "K":
                                ca = rec.act(lambda e, dst=dst, bank=bank: e.activation(
                                    out=dst, in_=psb[bank][:, :], func=AF.Relu,
                                    scale=ISCALE))
                            else:
                                ca = rec.act(lambda e, dst=dst, bank=bank: e.copy(
                                    out=dst, in_=psb[bank][:, :]))
                            evac_a[(pj, s, strip)] = ca
                            rec.use(bkey[bank], "act", ca)
                rec.use("sin", "pe", pc)
                for s in range(4):
                    for strip in range(2):
                        rec.wait("tensor", "act", evac_a[(0, s, strip)])
                        rec.wait("tensor", "act", evac_a[(1, s, strip)])
                    for h in range(HEADS // 2):
                        kv_last = rec.pe(lambda e, s=s, h=h: e.matmul(
                            p_kv[:, h * 128:(h + 1) * 128],
                            aux_a[:, s * 1024 + h * 128:s * 1024 + (h + 1) * 128],
                            aux_b[:, s * 1024 + h * 128:s * 1024 + (h + 1) * 128],
                            start=False, stop=True, skip_group_check=True),
                            inc=(h == HEADS // 2 - 1))
                rec.use("aux_a", "pe", kv_last)
                rec.use("aux_b", "pe", kv_last)
            rec.use("arena0", "pe", kv_last)
            rec.use("arena1", "pe", kv_last)
            rec.wait("scalar", "pe", kv_last)
            kv_acts = []
            for j in range(2):
                kv_acts.append(rec.act(lambda e, j=j: e.copy(
                    out=kvbf[:, j * 512:(j + 1) * 512], in_=p_kv[:, j * 512:(j + 1) * 512])))
                rec.use(bkey[4 + j], "act", kv_acts[-1])
            rec.wait("gpsimd", "act", kv_acts[-1])

            def arm_kv(par, e, s):
                e.dma_start(out=kv_sh[par, :, :],
                            in_=kvbf[:, 0:1024]).then_inc(s["kvx"], 16)
            emit_switch_pool(arm_kv, {"kvx": 16})
            rec.wait("gpsimd", "kvx", rec.n["kvx"])

            def arm_flag2(par, e, s):
                e.dma_start(out=fl_sh[par:par + 1, 1, 0:1],
                            in_=salt_sb[0:1, 0:1]).then_inc(s["kvx"], 16)
            emit_switch_pool(arm_flag2, {"kvx": 16})
            rec.use("kvbf", "kvx", rec.n["kvx"])
            rdz2 = emit_poll(1, "kvx", rec.n["kvx"])
            rec.wait("gpsimd", "rdz", rdz2)
            rec.gate("gpsimd", "kvbf")
            for j in range(2):
                kvl = rec.dma("gpsimd", "kvx", lambda e, j=j: e.dma_start(
                    out=kvbf[:, j * 1024:(j + 1) * 1024], in_=kv_sh[j, :, :]))
            kv_acts = [("kvx", kvl)]
            if debug:
                rec.gate("gpsimd", "stg")
                for j in range(4):
                    rec.wait("vector", "pe", kv_last)
                    cd = rec.dve(lambda e, j=j: e.tensor_copy(
                        out=stg[:, j * 512:(j + 1) * 512], in_=p_kv[:, j * 512:(j + 1) * 512]))
                rec.wait("gpsimd", "dve", cd)
                c_kvd = rec.dma("gpsimd", "misc", lambda e: e.dma_start(out=kv_dbg[:, :], in_=stg[:, :]))
                rec.use("stg", "misc", c_kvd)
                rec.use("stg", "dve", cd)
            return kv_acts[-1]

        kv_ready = pass_BC()

        def pass_DE():
            wq_cast = load_weight_fp32(arena0, "arena0", wT["wq"])
            wu_cast = load_weight_fp32(arena1, "arena1", wT["wu"])
            last_pe = 0
            ost = [0] * nt_own
            for t in range(nt_own):
                rec.gate("sync", "sin")
                lds = tile_dma_in(y2v, t * T, sin, "sin")
                q_pe = [0] * HEADS
                u_pe = [0] * HEADS
                o_pe = [0] * HEADS
                ss_pe = [0] * HEADS
                bc_pe = [0] * HEADS
                qev = [0] * HEADS
                uev = [0] * HEADS
                sqo = [0] * HEADS
                of_d = [0] * HEADS

                def emit_q(h):
                    bank = h % 2
                    rec.gate("tensor", bkey[bank])
                    rec.wait("tensor", "sin", lds)
                    for k in range(NCH):
                        rec.wait("tensor", "act", wq_cast[k])
                        q_pe[h] = rec.pe(lambda e, k=k, h=h, bank=bank: e.matmul(
                            psb[bank][:, :], wsl(arena0, k, h * 128, 128),
                            chk(sin, k), start=(k == 0), stop=(k == NCH - 1)),
                            inc=(k == NCH - 1))
                    rec.wait("scalar", "pe", q_pe[h])
                    rec.gate("scalar", "qh%d" % (h % 2))
                    qev[h] = rec.act(lambda e, h=h, bank=bank: e.activation(
                        out=chk(qh, h % 2), in_=psb[bank][:, :], func=AF.Relu,
                        scale=ISCALE))
                    rec.use(bkey[bank], "act", qev[h])

                def emit_u(h):
                    bank = 2 + h % 2
                    rec.gate("tensor", bkey[bank])
                    rec.wait("tensor", "sin", lds)
                    for k in range(NCH):
                        rec.wait("tensor", "act", wu_cast[k])
                        u_pe[h] = rec.pe(lambda e, k=k, h=h, bank=bank: e.matmul(
                            psb[bank][:, :], wsl(arena1, k, h * 128, 128),
                            chk(sin, k), start=(k == 0), stop=(k == NCH - 1)),
                            inc=(k == NCH - 1))
                    rec.wait("scalar", "pe", u_pe[h])
                    rec.gate("scalar", "uh%d" % (h % 2))
                    uev[h] = rec.act(lambda e, h=h, bank=bank: e.activation(
                        out=chk(uh, h % 2), in_=psb[bank][:, :], func=AF.Silu))
                    rec.use(bkey[bank], "act", uev[h])

                def emit_o(h):
                    pb = p4 if h % 2 == 0 else p5
                    bk = "b4" if h % 2 == 0 else "b5"
                    rec.wait("tensor", "act", qev[h])
                    rec.wait("tensor", kv_ready[0], kv_ready[1])
                    rec.gate("tensor", bk)
                    o_pe[h] = rec.pe(lambda e, h=h, pb=pb: e.matmul(
                        pb, kvbf[:, h * 128:(h + 1) * 128], chk(qh, h % 2),
                        start=True, stop=True), inc=True)
                    rec.use("qh%d" % (h % 2), "pe", o_pe[h])
                    rec.wait("scalar", "pe", o_pe[h])
                    rec.gate("scalar", "sq%d" % (h % 2))
                    sqo[h] = rec.act(lambda e, h=h, pb=pb: e.activation(
                        out=chk(sqb, h % 2), in_=pb, func=AF.Square))

                def emit_ss(h):
                    rec.wait("tensor", "act", sqo[h])
                    rec.gate("tensor", "b6")
                    ss_pe[h] = rec.pe(lambda e, h=h: e.matmul(
                        p6[0:1, :], ones_one, chk(sqb, h % 2),
                        start=True, stop=True), inc=True)
                    rec.use("sq%d" % (h % 2), "pe", ss_pe[h])
                    rec.wait("scalar", "pe", ss_pe[h])
                    rec.gate("scalar", "rows")
                    a_a = rec.act(lambda e: e.activation(
                        out=srow, in_=p6[0:1, :], func=AF.Sqrt, scale=1.0 / 128.0))
                    rec.use("b6", "act", a_a)
                    rec.wait("vector", "act", a_a)
                    rec.gate("vector", "rows")
                    rec.dve(lambda e: e.tensor_scalar(
                        out=trow, in0=srow, scalar1=1e-12, scalar2=None, op0=OP.max))
                    return rec.dve(lambda e: e.reciprocal(out=rrow, in_=trow))

                def emit_bc(h, d_recip):
                    rec.wait("tensor", "dve", d_recip)
                    rec.gate("tensor", "b7")
                    bc_pe[h] = rec.pe(lambda e: e.matmul(
                        p7, onesrow, rrow, start=True, stop=True), inc=True)
                    rec.use("rows", "pe", bc_pe[h])

                def emit_gate(h):
                    pb = p4 if h % 2 == 0 else p5
                    bk = "b4" if h % 2 == 0 else "b5"
                    g1buf = dvA[:, :] if h % 2 == 0 else dvB[:, :]
                    rec.wait("vector", "pe", o_pe[h])
                    rec.wait("vector", "act", uev[h])
                    rec.dve(lambda e, pb=pb, g1buf=g1buf, h=h: e.tensor_tensor(
                        out=g1buf, in0=pb, in1=chk(uh, h % 2), op=OP.mult))
                    rec.use("uh%d" % (h % 2), "dve", rec.n["dve"])
                    rec.wait("vector", "pe", bc_pe[h])
                    of_d[h] = rec.dve(lambda e, h=h, g1buf=g1buf: e.tensor_tensor(
                        out=chk(aux_a, h), in0=g1buf, in1=p7, op=OP.mult))
                    rec.use(bk, "dve", of_d[h])
                    rec.use("b7", "dve", of_d[h])

                d_recip = [0] * HEADS
                for h in range(HEADS):
                    emit_q(h)
                    if h >= 2:
                        emit_bc(h - 2, d_recip[h - 2])
                        emit_gate(h - 2)
                    if h >= 1:
                        d_recip[h - 1] = emit_ss(h - 1)
                    if h == 0:
                        rec.gate("vector", "aux_a")
                    emit_u(h)
                    emit_o(h)
                emit_bc(HEADS - 2, d_recip[HEADS - 2])
                emit_gate(HEADS - 2)
                d_recip[HEADS - 1] = emit_ss(HEADS - 1)
                emit_bc(HEADS - 1, d_recip[HEADS - 1])
                emit_gate(HEADS - 1)
                rec.wait("gpsimd", "dve", of_d[HEADS - 1])
                st = tile_dma_out(ov, t * T, aux_a, "so")
                rec.use("aux_a", "so", st)
                ost[t] = st
                rec.use("sin", "pe", max(q_pe[HEADS - 1], u_pe[HEADS - 1]))
                last_pe = max(q_pe[HEADS - 1], u_pe[HEADS - 1])
            rec.use("arena0", "pe", last_pe)
            rec.use("arena1", "pe", last_pe)
            rec.merge_use("xbuf0", "arena1")
            rec.merge_use("xbuf1", "arena1")
            return ost

        ost = pass_DE()

        x2st = [0] * nt_own
        xb2st = [0] * nt_own
        rowst = [0] * nt_own

        def pass_F():
            wo_cast = load_weight_fp32(arena0, "arena0", wT["wo"])
            for t in range(nt_own):
                xb = xbuf[t % 2]
                xkey = "xbuf%d" % (t % 2)
                lxs = "lx%d" % (t % 2)
                rec.gate("sync", "sin")
                rec.wait("sync", "so", ost[nt_own - 1])
                lds = tile_dma_in(ov, t * T, sin, "sin")
                rec.gate("sync", xkey)
                rec.wait("sync", "sx1", x1st[nt_own - 1])
                lds_x = tile_dma_in(x1v, t * T, xb, lxs)
                at_pe = [0] * NCH
                x2_d = [0] * NCH

                def attn_chunk(e_ix):
                    bank = e_ix % 4
                    rec.gate("tensor", bkey[bank])
                    rec.wait("tensor", "sin", lds)
                    for k in range(NCH):
                        rec.wait("tensor", "act", wo_cast[k])
                        at_pe[e_ix] = rec.pe(lambda e, k=k, e_ix=e_ix, bank=bank:
                                             e.matmul(
                            psb[bank][:, :], wsl(arena0, k, e_ix * 128, 128),
                            chk(sin, k), start=(k == 0), stop=(k == NCH - 1)),
                            inc=(k == NCH - 1))

                def x2_chunk(e_ix):
                    bank = e_ix % 4
                    rec.wait("vector", "pe", at_pe[e_ix])
                    rec.wait("vector", lxs, lds_x)
                    x2_d[e_ix] = rec.dve(lambda e, e_ix=e_ix, bank=bank, xb=xb: e.tensor_tensor(
                        out=chk(xb, e_ix), in0=psb[bank][:, :], in1=chk(xb, e_ix),
                        op=OP.add))
                    rec.use(bkey[bank], "dve", x2_d[e_ix])

                for e_ix in range(4):
                    attn_chunk(e_ix)
                for e_ix in range(4, NCH):
                    x2_chunk(e_ix - 4)
                    attn_chunk(e_ix)
                for e_ix in range(NCH - 4, NCH):
                    x2_chunk(e_ix)
                rec.use("sin", "pe", at_pe[NCH - 1])
                rec.wait("gpsimd", "dve", x2_d[NCH - 1])
                stx = tile_dma_out(x2v, t * T, xb, "sx2", serialize=True)
                cast2_a = []
                for c in range(NCH):
                    rec.wait("scalar", "dve", x2_d[c])
                    rec.gate("scalar", "aux_a")
                    cast2_a.append(rec.act(lambda e, c=c, xb=xb: e.copy(
                        out=chk(aux_a, c), in_=chk(xb, c))))
                rec.wait("gpsimd", "act", cast2_a[NCH - 1])
                stb = tile_dma_out(xb2v, t * T, aux_a, "sxb2")
                rec.use("aux_a", "sxb2", stb)
                rec.use(xkey, "sx2", stx)
                rec.use(xkey, "act", cast2_a[NCH - 1])
                x2st[t] = stx
                xb2st[t] = stb
            rec.use("arena0", "pe", at_pe[NCH - 1])

        pass_F()

        def pass_G():
            wt_pool = load_weight_bf16(arena0, "arena0", wt_s)
            out_last = 0
            for t in range(nt_own):
                xb = xbuf[t % 2]
                xkey = "xbuf%d" % (t % 2)
                lxs = "lx%d" % (t % 2)
                rec.gate("sync", "sin")
                rec.wait("sync", "sxb2", xb2st[nt_own - 1])
                lds = tile_dma_in(xb2v, t * T, sin, "sin")
                rec.gate("sync", xkey)
                rec.wait("sync", "sx2", x2st[nt_own - 1])
                lds_x = tile_dma_in(x2v, t * T, xb, lxs)
                srcg = [(chk(sin, c), lds) for c in range(NCH)]
                for c in range(NCH):
                    rec.wait("scalar", "sin", lds)
                sqg = ln_squares(srcg, aux_a, "aux_a")
                rec.wait("tensor", "sin", lds)
                pe_stg = ln_stats_mms([(chk(sin, c), None) for c in range(NCH)],
                                      sqg, aux_a, "aux_a")
                rec.use("sin", "pe", pe_stg)
                d_r3 = ln_smalls(pe_stg, mul_half_scale=True)
                bc3 = bcasts_45(mrow, rrow, d_r3)
                ffn_pe = [0] * NCH
                out_dv = [0] * NCH

                def ffn2_chunk(e_ix):
                    bank = e_ix % 4
                    rec.gate("tensor", bkey[bank])
                    rec.wait("tensor", "wt", wt_pool)
                    rec.wait("tensor", "sin", lds)
                    for k in range(NCH):
                        ffn_pe[e_ix] = rec.pe(lambda e, k=k, e_ix=e_ix, bank=bank:
                                              e.matmul(
                            psb[bank][:, :], wsl(arena0, k, e_ix * 128, 128),
                            chk(sin, k), start=(k == 0), stop=(k == NCH - 1)),
                            inc=(k == NCH - 1))

                def oev_chunk(e_ix):
                    bank = e_ix % 4
                    rec.wait("vector", "pe", ffn_pe[e_ix])
                    rec.wait("vector", "pe", bc3)
                    rec.wait("vector", lxs, lds_x)
                    rec.dve(lambda e, e_ix=e_ix: e.tensor_scalar(
                        out=dvA[:, :], in0=p4, scalar1=c_wt[:, e_ix:e_ix + 1],
                        scalar2=None, op0=OP.mult))
                    rec.dve(lambda e, bank=bank: e.tensor_tensor(
                        out=dvB[:, :], in0=psb[bank][:, :], in1=dvA[:, :], op=OP.subtract))
                    rec.dve(lambda e: e.tensor_tensor(
                        out=dvA[:, :], in0=dvB[:, :], in1=p5, op=OP.mult))
                    out_dv[e_ix] = rec.dve(lambda e, e_ix=e_ix, xb=xb: e.scalar_tensor_tensor(
                        out=chk(xb, e_ix), in0=dvA[:, :], scalar=hb[:, e_ix:e_ix + 1],
                        in1=chk(xb, e_ix), op0=OP.add, op1=OP.add))
                    rec.use(bkey[bank], "dve", out_dv[e_ix])
                    rec.use("b4", "dve", out_dv[e_ix])
                    rec.use("b5", "dve", out_dv[e_ix])

                for e_ix in range(4):
                    ffn2_chunk(e_ix)
                for e_ix in range(4, NCH):
                    oev_chunk(e_ix - 4)
                    ffn2_chunk(e_ix)
                for e_ix in range(NCH - 4, NCH):
                    oev_chunk(e_ix)
                rec.use("sin", "pe", ffn_pe[NCH - 1])
                rec.wait("gpsimd", "dve", out_dv[NCH - 1])
                st = tile_dma_out(outv, t * T, xb, "sout")
                rec.use(xkey, "sout", st)
                out_last = st
            rec.raw("gpsimd", lambda e, s, v=out_last: e.wait_ge(s["sout"], v))

        pass_G()

        @block.sync
        def _(eng):
            for fn in rec.streams["sync"]:
                fn(eng, sems)

        @block.scalar
        def _(eng):
            for fn in rec.streams["scalar"]:
                fn(eng, sems)

        @block.vector
        def _(eng):
            for fn in rec.streams["vector"]:
                fn(eng, sems)

        @block.tensor
        def _(eng):
            for fn in rec.streams["tensor"]:
                fn(eng, sems)

        @block.gpsimd
        def _(eng):
            for fn in rec.streams["gpsimd"]:
                fn(eng, sems)

    return nc


_NC_CACHE = {}


def _get_nc(debug=False):
    if debug not in _NC_CACHE:
        _NC_CACHE[debug] = build_nc(debug)
    return _NC_CACHE[debug]


def _consts_array():
    c = np.zeros((128, 144), np.float32)
    c[:, 0] = 1.0
    c[:, 1] = 0.5
    c[:, 2] = 1.0 / 2048.0
    c[:, 4:132] = 1.0
    for i, v in enumerate((EPS_LN, ISCALE, 1.0 / 128.0, 0.0, 1.0,
                           1e-12, 0.5, 1.0 / NW, EPS_T)):
        c[:, 132 + i] = v
    return c


def kernel(L0_attn_x, L0_attn_wq, L0_attn_wk, L0_attn_wv, L0_attn_wu,
           L0_attn_wo, L0_ffn_weight, L0_ffn_bias, _debug=False):
    x = np.asarray(L0_attn_x, dtype=np.float32)
    x_fm = np.ascontiguousarray(x.reshape(2 * SEQ, D).T)
    wkT_full = np.asarray(L0_attn_wk, np.float32).T
    wvT_full = np.asarray(L0_attn_wv, np.float32).T
    wts = {
        "wqT": np.ascontiguousarray(np.asarray(L0_attn_wq, np.float32).T),
        "wuT": np.ascontiguousarray(np.asarray(L0_attn_wu, np.float32).T),
        "woT": np.ascontiguousarray(np.asarray(L0_attn_wo, np.float32).T),
        "wfT": np.ascontiguousarray(np.asarray(L0_ffn_weight, np.float32).T),
    }
    bias = np.ascontiguousarray(np.asarray(L0_ffn_bias, np.float32).reshape(16, 128).T)
    consts = _consts_array()

    salt = np.random.default_rng().integers(1, 2 ** 31, size=(1, 4)).astype(np.uint32)
    in_maps = []
    for c in range(8):
        sq = c // 4
        q = c % 4
        par = c % 2
        base = sq * SEQ
        own = x_fm[:, base + q * QTOK: base + (q + 1) * QTOK]
        fq = (q + 2) % 4
        foreign = x_fm[:, base + fq * QTOK: base + (fq + 1) * QTOK]
        xc = np.ascontiguousarray(np.concatenate([own, foreign], axis=1))
        m = {"x": xc, "bias": bias, "consts_in": consts, "salt": salt}
        m["wkT"] = np.ascontiguousarray(wkT_full[:, par * 1024:(par + 1) * 1024])
        m["wvT"] = np.ascontiguousarray(wvT_full[:, par * 1024:(par + 1) * 1024])
        m.update(wts)
        in_maps.append(m)

    nc = _get_nc(_debug)
    res = run_bass_kernel_spmd(nc, in_maps, core_ids=list(range(8)))
    shards = [res.results[c]["out"] for c in range(8)]
    full = np.concatenate(shards, axis=1)
    out = np.ascontiguousarray(full.T).reshape(2, SEQ, D)
    if _debug:
        return out, res
    return out


# version 09
# speedup vs baseline: 1.7992x; 1.7992x over previous
import contextlib

import numpy as np

import concourse.bass as bass
import concourse.mybir as mybir
from concourse.bass_utils import run_bass_kernel_spmd

F32 = mybir.dt.float32
BF16 = mybir.dt.bfloat16
AF = mybir.ActivationFunctionType
OP = mybir.AluOpType
AX = mybir.AxisListType.X

D = 2048
NCH = 16
T = 512
SEQ = 16384
QTOK = 4096
NT_SEQ = SEQ // T
NT_OWN = QTOK // T
HEADS = 16
ISCALE = 1.0 / float(np.sqrt(128.0))
EPS_LN = 1e-5
EPS_T = 1e-10
NW = float(D * D)

ENGS = ("sync", "scalar", "vector", "tensor", "gpsimd")


class Rec:

    def __init__(self):
        self.streams = {e: [] for e in ENGS}
        self.n = {k: 0 for k in ("lx0", "lx1", "sin", "rows", "w0", "w1",
                                 "wt", "misc", "sy2", "sx1", "so", "sx2",
                                 "sxb2", "srows", "sout",
                                 "pck", "pcv", "pcq", "pcu", "pco",
                                 "sy2s", "kvx", "rdz",
                                 "act", "dve", "pe")}
        self.last_wait = {}
        self.buf = {}

    def wait(self, eng, sem, val):
        if not val or val <= 0:
            return
        assert val <= self.n[sem], (eng, sem, val, self.n[sem])
        if self.last_wait.get((eng, sem), 0) >= val:
            return
        self.last_wait[(eng, sem)] = val

        def fn(e, s, sem=sem, val=val):
            e.wait_ge(s[sem], val)
        self.streams[eng].append(fn)

    def raw(self, eng, fn):
        self.streams[eng].append(fn)

    def dma(self, eng, sem, fn):
        self.n[sem] += 16

        def run(e, s, fn=fn, sem=sem):
            fn(e).then_inc(s[sem], 16)
        self.streams[eng].append(run)
        return self.n[sem]

    def act(self, fn):
        self.wait("scalar", "act", self.n["act"])
        self.n["act"] += 1

        def run(e, s, fn=fn):
            fn(e).then_inc(s["act"], 1)
        self.streams["scalar"].append(run)
        return self.n["act"]

    def dve(self, fn):
        self.wait("vector", "dve", self.n["dve"])
        self.n["dve"] += 1

        def run(e, s, fn=fn):
            fn(e).then_inc(s["dve"], 1)
        self.streams["vector"].append(run)
        return self.n["dve"]

    def pe(self, fn, inc=False):
        if inc:
            self.n["pe"] += 1

            def run(e, s, fn=fn):
                fn(e).then_inc(s["pe"], 1)
        else:
            def run(e, s, fn=fn):
                fn(e)
        self.streams["tensor"].append(run)
        return self.n["pe"]

    def use(self, key, sem, cnt):
        d = self.buf.setdefault(key, {})
        d[sem] = max(d.get(sem, 0), cnt)

    def gate(self, eng, key):
        for sem, cnt in sorted(self.buf.get(key, {}).items()):
            self.wait(eng, sem, cnt)

    def merge_use(self, dst, *srcs):
        for s in srcs:
            for sem, cnt in self.buf.get(s, {}).items():
                self.use(dst, sem, cnt)


def build_nc(debug=False, seq=SEQ, qtok=QTOK, race_check=False, no_poll=False):
    nt_seq = seq // T
    nt_own = qtok // T
    nt_a = 2 * nt_own
    half_cols = seq // 2
    nc = bass.Bass("TRN2", target_bir_lowering=False,
                   detect_race_conditions=race_check)

    x_in = nc.dram_tensor("x", [D, 2 * qtok], F32, kind="ExternalInput").ap()
    wT = {}
    for nm in ("wq", "wu", "wo", "wf"):
        wT[nm] = nc.dram_tensor(nm + "T", [D, D], F32, kind="ExternalInput").ap()
    for nm in ("wk", "wv"):
        wT[nm] = nc.dram_tensor(nm + "T", [D, D // 2], F32, kind="ExternalInput").ap()
    salt_in = nc.dram_tensor("salt", [1, 4], mybir.dt.uint32, kind="ExternalInput").ap()
    bias_in = nc.dram_tensor("bias", [128, 16], F32, kind="ExternalInput").ap()
    consts_in = nc.dram_tensor("consts_in", [128, 144], F32, kind="ExternalInput").ap()
    out_d = nc.dram_tensor("out", [D, qtok], F32, kind="ExternalOutput").ap()

    skind = "ExternalOutput" if debug else "Internal"
    y2_s = nc.dram_tensor("y2_s", [D, 2 * qtok], BF16, kind=skind).ap()
    y2_sib = nc.dram_tensor("y2_sib", [D, 2 * qtok], BF16, kind="Internal").ap()
    y2_sh = nc.dram_tensor("y2_sh", [D, seq], BF16, addr_space="Shared").ap()
    kv_sh = nc.dram_tensor("kv_sh", [2, 128, D // 2], BF16, addr_space="Shared").ap()
    fl_sh = nc.dram_tensor("fl_sh", [2, 2, 16], mybir.dt.uint32, addr_space="Shared").ap()
    x1_s = nc.dram_tensor("x1_s", [D, qtok], F32, kind=skind).ap()
    o_s = nc.dram_tensor("o_s", [D, qtok], BF16, kind=skind).ap()
    x2_s = nc.dram_tensor("x2_s", [D, qtok], F32, kind=skind).ap()
    xb2_s = nc.dram_tensor("xb2_s", [D, qtok], BF16, kind=skind).ap()
    wt_s = nc.dram_tensor("wt_s", [D, D], BF16, kind=skind).ap()
    w_bf = {}
    for nm in ("wk", "wv", "wq", "wu", "wo"):
        w_bf[nm] = nc.dram_tensor(nm + "_bf", [D, D], BF16, kind="Internal").ap()
    m3_s = nc.dram_tensor("m3_s", [1, qtok], F32, kind=skind).ap()
    r3_s = nc.dram_tensor("r3_s", [1, qtok], F32, kind=skind).ap()
    dbg_fl = nc.dram_tensor("dbg_fl", [1, 16], mybir.dt.uint32, kind="ExternalOutput").ap()
    if debug:
        kv_dbg = nc.dram_tensor("kv_dbg", [128, D], F32, kind="ExternalOutput").ap()

    rec = Rec()

    with contextlib.ExitStack() as stack:
        E = stack.enter_context
        arena0 = E(nc.sbuf_tensor("arena0", [128, NCH * D], BF16))
        arena1 = E(nc.sbuf_tensor("arena1", [128, NCH * D], BF16))
        sin = E(nc.sbuf_tensor("sin", [128, NCH * T], BF16))
        aux_a = E(nc.sbuf_tensor("aux_a", [128, NCH * T], BF16))
        aux_b = E(nc.sbuf_tensor("aux_b", [128, NCH * T], BF16))
        stg = E(nc.sbuf_tensor("stg", [128, D], F32))
        sqb = E(nc.sbuf_tensor("sqb", [128, 2 * T], BF16))
        wstg = E(nc.sbuf_tensor("wstg", [128, 1024], BF16))
        kvbf = E(nc.sbuf_tensor("kvbf", [128, D], BF16))
        dvA = E(nc.sbuf_tensor("dvA", [128, T], F32))
        dvB = E(nc.sbuf_tensor("dvB", [128, T], F32))
        qh = E(nc.sbuf_tensor("qh", [128, 2 * T], BF16))
        uh = E(nc.sbuf_tensor("uh", [128, 2 * T], F32))
        consts = E(nc.sbuf_tensor("consts", [128, 144], F32))
        consts2 = E(nc.sbuf_tensor("consts2", [128, 4], F32))
        onesb = E(nc.sbuf_tensor("onesb", [128, 2], BF16))
        c_wt = E(nc.sbuf_tensor("c_wt", [128, 16], F32))
        hb = E(nc.sbuf_tensor("hb", [128, 16], F32))
        rows1 = E(nc.sbuf_tensor("rows1", [65, T], F32))
        rows2 = E(nc.sbuf_tensor("rows2", [65, T], F32))
        salt_sb = E(nc.sbuf_tensor("salt_sb", [1, 4], mybir.dt.uint32))
        flag_rd = E(nc.sbuf_tensor("flag_rd", [1, 4], mybir.dt.uint32))

        psb = [E(nc.psum_tensor("p%d" % i, [128, 512], F32)) for i in range(4)]
        p_kv = E(nc.psum_tensor("p_kv", [128, 2048], F32))

        sems = {}
        for s in ("lx0", "lx1", "sin", "rows", "w0", "w1", "wt", "misc",
                  "sy2", "sx1", "so", "sx2", "sxb2", "srows", "sout",
                  "pck", "pcv", "pcq", "pcu", "pco",
                  "sy2s", "kvx", "rdz",
                  "act", "dve", "pe"):
            sems[s] = E(nc.semaphore("s_" + s))
        sems["pol"] = E(nc.semaphore("s_pol"))
        block = E(nc.Block())

        a1f = arena1.ap().bitcast(F32)
        xbuf = [a1f[:, 0:NCH * T], a1f[:, NCH * T:2 * NCH * T]]
        p4 = p_kv[:, 0:512]
        p5 = p_kv[:, 512:1024]
        p6 = p_kv[:, 1024:1536]
        p7 = p_kv[:, 1536:2048]
        bkey = ["b0", "b1", "b2", "b3", "b4", "b5", "b6", "b7"]

        def wsl(arena, k, j0, w):
            return arena[:, k * D + j0:k * D + j0 + w]

        def chk(buf, c, w=T):
            return buf[:, c * w:(c + 1) * w]

        onescol = consts[:, 0:1]
        half11 = consts[0:1, 1:2]
        inv2048 = consts[:, 2:3]
        onesrow = consts[0:1, 4:132]
        thr_col = consts2[:, 0:1]
        half_scale = consts2[0:1, 1:2]
        mean_w = consts2[0:1, 2:3]
        acc_col = consts2[:, 3:4]
        ones_mean = onesb[:, 0:1]
        ones_one = onesb[:, 1:2]

        for _cix, _cval in enumerate((EPS_LN, ISCALE, 1.0 / 128.0, 0.0, 1.0,
                                      1e-12, 0.5, 1.0 / NW, EPS_T)):
            nc.const_aps.register(F32, _cval, consts[:, 132 + _cix:133 + _cix])

        mrow = rows1[0:1, :]
        trow = rows1[32:33, :]
        srow = rows1[64:65, :]
        rrow = rows2[0:1, :]
        r1row = rows2[32:33, :]

        xv = x_in.rearrange("(c p) t -> p c t", p=128)
        y2v = y2_s.rearrange("(c p) t -> p c t", p=128)
        y2sv = y2_sib.rearrange("(c p) t -> p c t", p=128)
        x1v = x1_s.rearrange("(c p) t -> p c t", p=128)
        ov = o_s.rearrange("(c p) t -> p c t", p=128)
        x2v = x2_s.rearrange("(c p) t -> p c t", p=128)
        xb2v = xb2_s.rearrange("(c p) t -> p c t", p=128)
        outv = out_d.rearrange("(c p) t -> p c t", p=128)

        def tile_dma_in(dramv, col0, sbuf_buf, sem):
            for j in range(4):
                dsl = dramv[:, 4 * j:4 * j + 4, col0:col0 + T]
                ssl = sbuf_buf[:, 4 * j * T:(4 * j + 4) * T].rearrange(
                    "p (c t) -> p c t", t=T)
                end = rec.dma("sync", sem,
                              lambda e, a=ssl, b=dsl: e.dma_start(out=a, in_=b))
            return end

        def tile_dma_out(dramv, col0, sbuf_buf, sem, serialize=False):
            if serialize:
                rec.wait("gpsimd", sem, rec.n[sem])
            for j in range(4):
                dsl = dramv[:, 4 * j:4 * j + 4, col0:col0 + T]
                ssl = sbuf_buf[:, 4 * j * T:(4 * j + 4) * T].rearrange(
                    "p (c t) -> p c t", t=T)
                end = rec.dma("gpsimd", sem,
                              lambda e, a=dsl, b=ssl: e.dma_start(out=a, in_=b))
            return end

        c_ld = rec.dma("sync", "rows", lambda e: e.dma_start(out=consts[:, :], in_=consts_in[:, :]))
        rec.dma("sync", "rows", lambda e: e.dma_start(out=salt_sb[0:1, :], in_=salt_in[:, :]))
        c_salt = rec.n["rows"]
        rec.wait("scalar", "rows", c_salt)
        c_fz = rec.act(lambda e: e.memzero(flag_rd[0:1, :]))
        rec.act(lambda e: e.memzero(wstg[:, :]))
        rec.act(lambda e: e.copy(out=ones_mean, in_=inv2048))
        c_ones = rec.act(lambda e: e.copy(out=ones_one, in_=onescol))

        red_dve = []
        for hix in range(2 * NCH):
            k, half = divmod(hix, 2)
            sl = stg[:, half * 1024:(half + 1) * 1024]
            src = wT["wf"][k * 128:(k + 1) * 128, half * 1024:(half + 1) * 1024]
            if hix >= 2:
                rec.wait("gpsimd", "dve", red_dve[hix - 2])
            c = rec.dma("gpsimd", "w%d" % (hix % 2),
                        lambda e, a=sl, b=src: e.dma_start(out=a, in_=b))
            rec.wait("vector", "w%d" % (hix % 2), c)
            if hix == 0:
                rec.dve(lambda e, a=sl: e.tensor_reduce(
                    out=acc_col, in_=a, axis=AX, op=OP.add, apply_absolute_value=True))
            else:
                rec.dve(lambda e, a=sl: e.tensor_reduce(
                    out=dvA[:, 0:1], in_=a, axis=AX, op=OP.add, apply_absolute_value=True))
                rec.dve(lambda e: e.tensor_tensor(
                    out=acc_col, in0=acc_col, in1=dvA[:, 0:1], op=OP.add))
            red_dve.append(rec.n["dve"])
        rec.wait("tensor", "dve", red_dve[-1])
        rec.wait("tensor", "rows", c_salt)
        c_tot = rec.pe(lambda e: e.matmul(psb[0][0:1, 0:1], acc_col, onescol,
                                          start=True, stop=True), inc=True)
        rec.wait("vector", "pe", c_tot)
        rec.dve(lambda e: e.tensor_scalar(out=mean_w, in0=psb[0][0:1, 0:1],
                                          scalar1=1.0 / NW, scalar2=None, op0=OP.mult))
        rec.dve(lambda e: e.tensor_scalar(out=rows1[0:1, 0:1], in0=mean_w,
                                          scalar1=EPS_T, scalar2=0.5,
                                          op0=OP.add, op1=OP.mult))
        c_hs = rec.dve(lambda e: e.tensor_scalar(out=half_scale, in0=mean_w,
                                                 scalar1=0.5, scalar2=None, op0=OP.mult))
        rec.wait("tensor", "dve", c_hs)
        c_thrb = rec.pe(lambda e: e.matmul(psb[1][:, 0:1], onesrow, rows1[0:1, 0:1],
                                           start=True, stop=True), inc=True)
        rec.wait("scalar", "pe", c_thrb)
        c_thr = rec.act(lambda e: e.copy(out=thr_col, in_=psb[1][:, 0:1]))

        rec.wait("gpsimd", "dve", red_dve[-1])
        tern_act = []
        tern_dve = []
        for hix in range(2 * NCH):
            k, half = divmod(hix, 2)
            sl = stg[:, half * 1024:(half + 1) * 1024]
            src = wT["wf"][k * 128:(k + 1) * 128, half * 1024:(half + 1) * 1024]
            if hix >= 2:
                rec.wait("gpsimd", "act", tern_act[hix - 2])
                rec.wait("gpsimd", "dve", tern_dve[hix - 2])
            c = rec.dma("gpsimd", "w%d" % (hix % 2),
                        lambda e, a=sl, b=src: e.dma_start(out=a, in_=b))
            rec.wait("vector", "w%d" % (hix % 2), c)
            rec.wait("vector", "act", c_thr)
            rec.wait("scalar", "w%d" % (hix % 2), c)
            for qx in range(2):
                q = sl[:, qx * 512:(qx + 1) * 512]
                dst = arena0[:, k * D + half * 1024 + qx * 512:
                             k * D + half * 1024 + (qx + 1) * 512]
                c_abs = rec.act(lambda e, q=q: e.activation(
                    out=dvA[:, :], in_=q, func=AF.Abs))
                rec.wait("vector", "act", c_abs)
                rec.dve(lambda e: e.tensor_scalar(
                    out=dvA[:, :], in0=dvA[:, :], scalar1=thr_col, scalar2=None, op0=OP.is_gt))
                c_s = rec.act(lambda e, q=q: e.sign(out=dvB[:, :], in_=q))
                rec.wait("vector", "act", c_s)
                c_w = rec.dve(lambda e: e.tensor_tensor(
                    out=dvB[:, :], in0=dvA[:, :], in1=dvB[:, :], op=OP.mult))
                rec.wait("scalar", "dve", c_w)
                c_cp = rec.act(lambda e, dst=dst: e.copy(out=dst, in_=dvB[:, :]))
            tern_act.append(c_cp)
            tern_dve.append(c_w)
        c_tern = tern_act[-1]
        rec.use("stg", "act", c_tern)
        rec.use("stg", "dve", tern_dve[-1])

        rec.wait("gpsimd", "act", c_tern)
        for k in range(NCH):
            sl = arena0[:, k * D:(k + 1) * D]
            dst = wt_s[k * 128:(k + 1) * 128, :]
            rec.dma("gpsimd", "wt", lambda e, a=dst, b=sl: e.dma_start(out=a, in_=b))
        wt_store_total = rec.n["wt"]
        rec.use("arena0", "wt", wt_store_total)

        rec.wait("tensor", "act", max(c_tern, c_ones))
        for j in range(NCH):
            bank = j % 4
            rec.gate("tensor", bkey[bank])
            for k in range(NCH):
                c_cwt_pe = rec.pe(lambda e, j=j, k=k, bank=bank: e.matmul(
                    psb[bank][:, 0:1], wsl(arena0, k, j * 128, 128), ones_one,
                    start=(k == 0), stop=(k == NCH - 1)),
                    inc=(k == NCH - 1))
            rec.wait("scalar", "pe", c_cwt_pe)
            c_cwt = rec.act(lambda e, j=j, bank=bank: e.copy(
                out=c_wt[:, j:j + 1], in_=psb[bank][:, 0:1]))
            rec.use(bkey[bank], "act", c_cwt)

        c_brow = rec.dma("gpsimd", "misc",
                         lambda e: e.dma_start(out=hb[:, :], in_=bias_in[:, :]))
        rec.wait("scalar", "misc", c_brow)
        c_hb = rec.act(lambda e: e.activation(out=hb[:, :], in_=hb[:, :],
                                              func=AF.Copy, scale=0.5))

        def ln_smalls(pe_stats, mul_half_scale):
            rec.wait("scalar", "pe", pe_stats)
            rec.gate("scalar", "rows")
            a_m = rec.act(lambda e: e.copy(out=mrow, in_=psb[2][0:1, :]))
            rec.use("b2", "act", a_m)
            rec.wait("vector", "act", a_m)
            rec.gate("vector", "rows")
            rec.dve(lambda e: e.tensor_tensor(out=trow, in0=mrow, in1=mrow, op=OP.mult))
            d_var = rec.dve(lambda e: e.tensor_tensor(
                out=trow, in0=psb[3][0:1, :], in1=trow, op=OP.subtract))
            rec.use("b3", "dve", d_var)
            rec.wait("scalar", "dve", d_var)
            a_s = rec.act(lambda e: e.activation(out=srow, in_=trow, func=AF.Sqrt,
                                                 bias=EPS_LN))
            rec.wait("vector", "act", a_s)
            rec.dve(lambda e: e.reciprocal(out=r1row, in_=srow))
            if mul_half_scale:
                d_r = rec.dve(lambda e: e.tensor_scalar(
                    out=rrow, in0=r1row, scalar1=half_scale, scalar2=None, op0=OP.mult))
            else:
                d_r = rec.dve(lambda e: e.tensor_copy(out=rrow, in_=r1row))
            return d_r

        def ln_squares(src_acts, dst_buf, dst_key):
            sq_acts = []
            rec.gate("scalar", dst_key)
            for c, (ap_, cnt) in enumerate(src_acts):
                sq_acts.append(rec.act(lambda e, ap_=ap_, c=c: e.activation(
                    out=chk(dst_buf, c), in_=ap_, func=AF.Square)))
            return sq_acts

        def ln_stats_mms(src_acts, sq_acts, sq_buf, sq_key):
            rec.gate("tensor", "b2")
            rec.gate("tensor", "b3")
            for c, (ap_, cnt) in enumerate(src_acts):
                if cnt is not None:
                    rec.wait("tensor", "act", cnt)
                rec.pe(lambda e, ap_=ap_, c=c: e.matmul(
                    psb[2][0:1, :], ones_mean, ap_,
                    start=(c == 0), stop=(c == NCH - 1)), inc=(c == NCH - 1))
            for c in range(NCH):
                rec.wait("tensor", "act", sq_acts[c])
                pe_c = rec.pe(lambda e, c=c, sq_buf=sq_buf: e.matmul(
                    psb[3][0:1, :], ones_mean, chk(sq_buf, c),
                    start=(c == 0), stop=(c == NCH - 1)), inc=(c == NCH - 1))
            rec.use(sq_key, "pe", pe_c)
            return pe_c

        def bcasts_45(row_a, row_b, dve_ready):
            rec.wait("tensor", "dve", dve_ready)
            rec.gate("tensor", "b4")
            rec.gate("tensor", "b5")
            rec.pe(lambda e: e.matmul(p4, onesrow, row_a, start=True, stop=True))
            ret = rec.pe(lambda e: e.matmul(p5, onesrow, row_b,
                                            start=True, stop=True), inc=True)
            rec.use("rows", "pe", ret)
            return ret

        def load_weight_fp32(arena, arena_key, src, half=False):
            done = [0] * NCH
            first = True
            nh = NCH if half else 2 * NCH
            for hix in range(nh):
                if half:
                    k, hf = hix, 0
                    srcsl = src[k * 128:(k + 1) * 128, 0:1024]
                    dstsl = arena[:, k * 1024:(k + 1) * 1024]
                else:
                    k, hf = divmod(hix, 2)
                    srcsl = src[k * 128:(k + 1) * 128, hf * 1024:(hf + 1) * 1024]
                    dstsl = arena[:, k * D + hf * 1024:k * D + (hf + 1) * 1024]
                slot = hix % 2
                sl = stg[:, slot * 1024:(slot + 1) * 1024]
                skey = "stgh%d" % slot
                rec.gate("gpsimd", skey)
                if first:
                    rec.gate("gpsimd", "stg")
                c = rec.dma("gpsimd", "w%d" % slot,
                            lambda e, a=sl, b=srcsl: e.dma_start(out=a, in_=b))
                if first:
                    rec.gate("scalar", arena_key)
                    first = False
                rec.wait("scalar", "w%d" % slot, c)
                cc = rec.act(lambda e, a=dstsl, b=sl: e.copy(out=a, in_=b))
                rec.use(skey, "act", cc)
                if half or hf == 1:
                    done[k] = cc
            return done

        def load_weight_bf16(arena, arena_key, src, pc_sem=None):
            rec.gate("gpsimd", arena_key)
            if pc_sem is not None:
                rec.wait("gpsimd", pc_sem, rec.n[pc_sem])
            rec.wait("gpsimd", "wt", rec.n["wt"])
            for k in range(NCH):
                done = rec.dma("gpsimd", "wt",
                               lambda e, a=arena[:, k * D:(k + 1) * D],
                               b=src[k * 128:(k + 1) * 128, :]:
                               e.dma_start(out=a, in_=b))
            return done

        POOL_ENG = [mybir.EngineType.Pool]

        def emit_switch_pool(arm_fn, sem_incs):
            def closure(e, s):
                for arm in nc.switch_core_id(engines=[e], n=128):
                    arm_fn(int(arm) % 2, e, s)
            rec.raw("gpsimd", closure)
            for sem, inc in sem_incs.items():
                rec.n[sem] += inc

        _pollid = [0]

        def emit_poll(phase, dep_sem, dep_val):
            _cfz = c_fz

            def closure(e, s, phase=phase, _cfz=_cfz):
                e.wait_ge(s[dep_sem], dep_val)
                e.wait_ge(s["act"], _cfz)
                pid = _pollid[0]
                _pollid[0] += 1
                lbl = "poll_%d" % pid
                c2 = lbl + "_c2"
                done = lbl + "_done"
                e.br(lbl)
                with nc.body(lbl, valid_engines=POOL_ENG):
                    e.dma_start(out=flag_rd[0:1, 2 * phase:2 * phase + 1],
                                in_=fl_sh[0:1, phase, 0:1]).then_inc(s["pol"], 16)
                    e.dma_start(out=flag_rd[0:1, 2 * phase + 1:2 * phase + 2],
                                in_=fl_sh[1:2, phase, 0:1]).then_inc(s["pol"], 16)
                    e.tensor_copy(out=wstg[:, 0:512], in_=wstg[:, 512:1024])
                    e.tensor_copy(out=wstg[:, 512:1024], in_=wstg[:, 0:512])
                    v0 = e.value_load(flag_rd[0:1, 2 * phase:2 * phase + 1])
                    vs = e.value_load(salt_sb[0:1, 0:1])
                    e.br_ne(v0, vs, on_true=lbl, on_false=c2)
                with nc.body(c2, valid_engines=POOL_ENG):
                    v1 = e.value_load(flag_rd[0:1, 2 * phase + 1:2 * phase + 2])
                    vs2 = e.value_load(salt_sb[0:1, 0:1])
                    e.br_ne(v1, vs2, on_true=lbl, on_false=done)
                nc.switch_bb(done)
                e.sem_inc(s["rdz"], 1)
            rec.raw("gpsimd", closure)
            rec.n["rdz"] += 1
            return rec.n["rdz"]

        wt_cast = [c_tern] * NCH

        y2st = [0] * nt_a
        x1st = [0] * nt_own

        pc_jobs = []
        for nm, sem in (("wk", "pck"), ("wv", "pcv"), ("wq", "pcq"),
                        ("wu", "pcu"), ("wo", "pco")):
            for k in range(NCH):
                for half in range(2):
                    pc_jobs.append((nm, sem, k, half))
        pc_done = {}

        def emit_precast(njobs):
            for _ in range(njobs):
                if not pc_jobs:
                    return
                nm, sem, k, half = pc_jobs.pop(0)
                if nm == "wk" and k == 0 and half == 0:
                    rec.gate("gpsimd", "stg")
                slot = (k * 2 + half) % 2
                sl = stg[:, slot * 1024:(slot + 1) * 1024]
                bsl = wstg[:, 0:1024]
                rec.gate("gpsimd", "stgh%d" % slot)
                c = rec.dma("gpsimd", "w%d" % slot,
                            lambda e, a=sl, b=wT[nm][k * 128:(k + 1) * 128,
                                                     half * 1024:(half + 1) * 1024]:
                            e.dma_start(out=a, in_=b))
                rec.wait("scalar", "w%d" % slot, c)
                rec.gate("scalar", "wstg")
                cc = rec.act(lambda e, a=bsl, b=sl: e.copy(out=a, in_=b))
                rec.use("stgh%d" % slot, "act", cc)
                rec.wait("gpsimd", "act", cc)
                st = rec.dma("gpsimd", sem,
                             lambda e, a=w_bf[nm][k * 128:(k + 1) * 128,
                                                  half * 1024:(half + 1) * 1024],
                             b=bsl: e.dma_start(out=a, in_=b))
                rec.use("wstg", sem, st)
                pc_done[nm] = (sem, st)

        pc_jobs.clear()
        pc_per_tile = 0

        def pass_A():
            for i in range(nt_a):
                xb = xbuf[i % 2]
                xkey = "xbuf%d" % (i % 2)
                lxs = "lx%d" % (i % 2)
                rec.gate("sync", xkey)
                lds = tile_dma_in(xv, i * T, xb, lxs)
                cast_a = []
                for c in range(NCH):
                    rec.wait("scalar", lxs, lds)
                    rec.gate("scalar", "sin")
                    cast_a.append(rec.act(lambda e, c=c, xb=xb: e.copy(
                        out=chk(sin, c), in_=chk(xb, c))))
                rec.use(xkey, "act", cast_a[-1])
                src1 = [(chk(sin, c), cast_a[c]) for c in range(NCH)]
                sq1 = ln_squares(src1, aux_a, "aux_a")
                pe_st1 = ln_stats_mms(src1, sq1, aux_a, "aux_a")
                rec.use("sin", "pe", pe_st1)
                rec.wait("vector", "pe", pe_st1)
                d_r1 = ln_smalls(pe_st1, mul_half_scale=True)
                ffn_pe = [0] * NCH
                bc1 = None

                def ffn_chunk(e_ix):
                    bank = e_ix % 4
                    rec.gate("tensor", bkey[bank])
                    for k in range(NCH):
                        rec.wait("tensor", "act", wt_cast[k])
                        rec.wait("tensor", "act", cast_a[k])
                        ffn_pe[e_ix] = rec.pe(
                            lambda e, k=k, e_ix=e_ix, bank=bank: e.matmul(
                                psb[bank][:, :], wsl(arena0, k, e_ix * 128, 128),
                                chk(sin, k), start=(k == 0), stop=(k == NCH - 1)),
                            inc=(k == NCH - 1))

                evac_d = [0] * NCH

                def evac_chunk(e_ix):
                    bank = e_ix % 4
                    rec.wait("vector", "pe", ffn_pe[e_ix])
                    rec.wait("vector", "pe", bc1)
                    rec.dve(lambda e, e_ix=e_ix: e.tensor_scalar(
                        out=dvA[:, :], in0=p4, scalar1=c_wt[:, e_ix:e_ix + 1],
                        scalar2=None, op0=OP.mult))
                    rec.dve(lambda e, bank=bank: e.tensor_tensor(
                        out=dvB[:, :], in0=psb[bank][:, :], in1=dvA[:, :], op=OP.subtract))
                    rec.dve(lambda e: e.tensor_tensor(
                        out=dvA[:, :], in0=dvB[:, :], in1=p5, op=OP.mult))
                    evac_d[e_ix] = rec.dve(lambda e, e_ix=e_ix, xb=xb: e.scalar_tensor_tensor(
                        out=chk(xb, e_ix), in0=dvA[:, :], scalar=hb[:, e_ix:e_ix + 1],
                        in1=chk(xb, e_ix), op0=OP.add, op1=OP.add))
                    rec.use(bkey[bank], "dve", evac_d[e_ix])
                    rec.use("b4", "dve", evac_d[e_ix])
                    rec.use("b5", "dve", evac_d[e_ix])

                for e_ix in range(4):
                    ffn_chunk(e_ix)
                bc1 = bcasts_45(mrow, rrow, d_r1)
                for e_ix in range(4, NCH):
                    evac_chunk(e_ix - 4)
                    ffn_chunk(e_ix)
                for e_ix in range(NCH - 4, NCH):
                    evac_chunk(e_ix)
                rec.use("sin", "pe", ffn_pe[NCH - 1])
                if i < nt_own:
                    rec.wait("gpsimd", "dve", evac_d[NCH - 1])
                    st = tile_dma_out(x1v, i * T, xb, "sx1", serialize=True)
                    rec.use(xkey, "sx1", st)
                    x1st[i] = st
                cast1_a = []
                for c in range(NCH):
                    rec.wait("scalar", "dve", evac_d[c])
                    rec.gate("scalar", "aux_a")
                    cast1_a.append(rec.act(lambda e, c=c, xb=xb: e.copy(
                        out=chk(aux_a, c), in_=chk(xb, c))))
                rec.use(xkey, "act", cast1_a[-1])
                src2 = [(chk(aux_a, c), cast1_a[c]) for c in range(NCH)]
                sq2 = ln_squares(src2, aux_b, "aux_b")
                pe_st2 = ln_stats_mms(src2, sq2, aux_b, "aux_b")
                rec.use("aux_a", "pe", pe_st2)
                rec.wait("vector", "pe", pe_st2)
                d_r2 = ln_smalls(pe_st2, mul_half_scale=False)
                bc2 = bcasts_45(mrow, rrow, d_r2)
                app_d = [0] * NCH
                for c in range(NCH):
                    rec.wait("vector", "pe", bc2)
                    rec.gate("vector", "aux_b")
                    rec.dve(lambda e, c=c, xb=xb: e.tensor_tensor(
                        out=dvB[:, :], in0=chk(xb, c), in1=p4, op=OP.subtract))
                    app_d[c] = rec.dve(lambda e, c=c: e.tensor_tensor(
                        out=chk(aux_b, c), in0=dvB[:, :], in1=p5, op=OP.mult))
                rec.use("b4", "dve", app_d[-1])
                rec.use("b5", "dve", app_d[-1])
                rec.use(xkey, "dve", app_d[-1])
                rec.wait("gpsimd", "dve", app_d[NCH - 1])
                st2 = tile_dma_out(y2v, i * T, aux_b, "sy2")
                rec.use("aux_b", "sy2", st2)
                y2st[i] = st2
            rec.use("arena0", "pe", ffn_pe[NCH - 1])
            rec.merge_use("arena1", "xbuf0", "xbuf1")
            return

        pass_A()

        def pass_BC():
            rec.wait("gpsimd", "sy2", y2st[nt_a - 1])
            rec.wait("gpsimd", "rows", c_salt)

            def arm_out(par, e, s):
                for j in range(4):
                    srcv = y2_s[j * 512:(j + 1) * 512, :]
                    dstv = y2_sh[j * 512:(j + 1) * 512,
                                 par * half_cols:par * half_cols + 2 * qtok]
                    e.dma_start(out=dstv, in_=srcv).then_inc(s["sy2s"], 16)
            emit_switch_pool(arm_out, {"sy2s": 64})
            rec.wait("gpsimd", "sy2s", rec.n["sy2s"])

            def arm_flag1(par, e, s):
                e.dma_start(out=fl_sh[par:par + 1, 0, 0:1],
                            in_=salt_sb[0:1, 0:1]).then_inc(s["sy2s"], 16)
            emit_switch_pool(arm_flag1, {"sy2s": 16})
            wk_cast = load_weight_fp32(arena0, "arena0", wT["wk"], half=True)
            wv_cast = load_weight_fp32(arena1, "arena1", wT["wv"], half=True)
            rdz1 = emit_poll(0, "sy2s", rec.n["sy2s"])

            def arm_in(par, e, s):
                sib = 1 - par
                for j in range(4):
                    srcv = y2_sh[j * 512:(j + 1) * 512,
                                 sib * half_cols:sib * half_cols + 2 * qtok]
                    dstv = y2_sib[j * 512:(j + 1) * 512, :]
                    e.dma_start(out=dstv, in_=srcv).then_inc(s["sy2s"], 16)
            emit_switch_pool(arm_in, {"sy2s": 64})
            y2sib_done = rec.n["sy2s"]
            for b in range(4, 8):
                rec.gate("scalar", bkey[b])
            c_zkv = rec.act(lambda e: e.memzero(p_kv[:, :]))
            rec.wait("tensor", "act", c_zkv)
            kv_last = 0
            for i in range(nt_seq):
                rec.gate("sync", "sin")
                if i < nt_a:
                    rec.wait("sync", "sy2", y2st[nt_a - 1])
                    lds = tile_dma_in(y2v, i * T, sin, "sin")
                else:
                    rec.wait("sync", "rdz", rdz1)
                    rec.wait("sync", "sy2s", y2sib_done)
                    lds = tile_dma_in(y2sv, (i - nt_a) * T, sin, "sin")
                proj_pe = {}
                evac_a = {}
                for pj, (arena, wcast, dstbuf, func) in enumerate((
                        (arena0, wk_cast, aux_a, "K"),
                        (arena1, wv_cast, aux_b, "V"))):
                    for s in range(4):
                        for strip in range(2):
                            bank = 2 * pj + (strip % 2)
                            rec.gate("tensor", bkey[bank])
                            rec.wait("tensor", "sin", lds)
                            for k in range(NCH):
                                rec.wait("tensor", "act", wcast[k])
                                pc = rec.pe(lambda e, k=k, s=s, strip=strip, bank=bank,
                                            arena=arena: e.matmul(
                                    psb[bank][:, :],
                                    sin[:, k * T + s * 128:k * T + (s + 1) * 128],
                                    arena[:, k * 1024 + strip * 512:
                                          k * 1024 + (strip + 1) * 512],
                                    start=(k == 0), stop=(k == NCH - 1)),
                                    inc=(k == NCH - 1))
                            proj_pe[(pj, s, strip)] = pc
                            rec.wait("scalar", "pe", pc)
                            rec.gate("scalar", "aux_a" if pj == 0 else "aux_b")
                            dst = dstbuf[:, s * 1024 + strip * 512:
                                         s * 1024 + (strip + 1) * 512]
                            if func == "K":
                                ca = rec.act(lambda e, dst=dst, bank=bank: e.activation(
                                    out=dst, in_=psb[bank][:, :], func=AF.Relu,
                                    scale=ISCALE))
                            else:
                                ca = rec.act(lambda e, dst=dst, bank=bank: e.copy(
                                    out=dst, in_=psb[bank][:, :]))
                            evac_a[(pj, s, strip)] = ca
                            rec.use(bkey[bank], "act", ca)
                rec.use("sin", "pe", pc)
                for s in range(4):
                    for strip in range(2):
                        rec.wait("tensor", "act", evac_a[(0, s, strip)])
                        rec.wait("tensor", "act", evac_a[(1, s, strip)])
                    for h in range(HEADS // 2):
                        kv_last = rec.pe(lambda e, s=s, h=h: e.matmul(
                            p_kv[:, h * 128:(h + 1) * 128],
                            aux_a[:, s * 1024 + h * 128:s * 1024 + (h + 1) * 128],
                            aux_b[:, s * 1024 + h * 128:s * 1024 + (h + 1) * 128],
                            start=False, stop=True, skip_group_check=True),
                            inc=(h == HEADS // 2 - 1))
                rec.use("aux_a", "pe", kv_last)
                rec.use("aux_b", "pe", kv_last)
            rec.use("arena0", "pe", kv_last)
            rec.use("arena1", "pe", kv_last)
            rec.wait("scalar", "pe", kv_last)
            kv_acts = []
            for j in range(2):
                kv_acts.append(rec.act(lambda e, j=j: e.copy(
                    out=kvbf[:, j * 512:(j + 1) * 512], in_=p_kv[:, j * 512:(j + 1) * 512])))
                rec.use(bkey[4 + j], "act", kv_acts[-1])
            rec.wait("gpsimd", "act", kv_acts[-1])

            def arm_kv(par, e, s):
                e.dma_start(out=kv_sh[par, :, :],
                            in_=kvbf[:, 0:1024]).then_inc(s["kvx"], 16)
            emit_switch_pool(arm_kv, {"kvx": 16})
            rec.wait("gpsimd", "kvx", rec.n["kvx"])

            def arm_flag2(par, e, s):
                e.dma_start(out=fl_sh[par:par + 1, 1, 0:1],
                            in_=salt_sb[0:1, 0:1]).then_inc(s["kvx"], 16)
            emit_switch_pool(arm_flag2, {"kvx": 16})
            rec.use("kvbf", "kvx", rec.n["kvx"])
            rdz2 = emit_poll(1, "kvx", rec.n["kvx"])
            rec.wait("gpsimd", "rdz", rdz2)
            rec.gate("gpsimd", "kvbf")
            for j in range(2):
                kvl = rec.dma("gpsimd", "kvx", lambda e, j=j: e.dma_start(
                    out=kvbf[:, j * 1024:(j + 1) * 1024], in_=kv_sh[j, :, :]))
            kv_acts = [("kvx", kvl)]
            if debug:
                rec.gate("gpsimd", "stg")
                for j in range(4):
                    rec.wait("vector", "pe", kv_last)
                    cd = rec.dve(lambda e, j=j: e.tensor_copy(
                        out=stg[:, j * 512:(j + 1) * 512], in_=p_kv[:, j * 512:(j + 1) * 512]))
                rec.wait("gpsimd", "dve", cd)
                c_kvd = rec.dma("gpsimd", "misc", lambda e: e.dma_start(out=kv_dbg[:, :], in_=stg[:, :]))
                rec.use("stg", "misc", c_kvd)
                rec.use("stg", "dve", cd)
            return kv_acts[-1]

        kv_ready = pass_BC()

        def pass_DE():
            wq_cast = load_weight_fp32(arena0, "arena0", wT["wq"])
            wu_cast = load_weight_fp32(arena1, "arena1", wT["wu"])
            last_pe = 0
            ost = [0] * nt_own
            for t in range(nt_own):
                rec.gate("sync", "sin")
                lds = tile_dma_in(y2v, t * T, sin, "sin")
                q_pe = [0] * HEADS
                u_pe = [0] * HEADS
                o_pe = [0] * HEADS
                ss_pe = [0] * HEADS
                bc_pe = [0] * HEADS
                qev = [0] * HEADS
                uev = [0] * HEADS
                sqo = [0] * HEADS
                of_d = [0] * HEADS

                def emit_q(h):
                    bank = h % 2
                    rec.gate("tensor", bkey[bank])
                    rec.wait("tensor", "sin", lds)
                    for k in range(NCH):
                        rec.wait("tensor", "act", wq_cast[k])
                        q_pe[h] = rec.pe(lambda e, k=k, h=h, bank=bank: e.matmul(
                            psb[bank][:, :], wsl(arena0, k, h * 128, 128),
                            chk(sin, k), start=(k == 0), stop=(k == NCH - 1)),
                            inc=(k == NCH - 1))
                    rec.wait("scalar", "pe", q_pe[h])
                    rec.gate("scalar", "qh%d" % (h % 2))
                    qev[h] = rec.act(lambda e, h=h, bank=bank: e.activation(
                        out=chk(qh, h % 2), in_=psb[bank][:, :], func=AF.Relu,
                        scale=ISCALE))
                    rec.use(bkey[bank], "act", qev[h])

                def emit_u(h):
                    bank = 2 + h % 2
                    rec.gate("tensor", bkey[bank])
                    rec.wait("tensor", "sin", lds)
                    for k in range(NCH):
                        rec.wait("tensor", "act", wu_cast[k])
                        u_pe[h] = rec.pe(lambda e, k=k, h=h, bank=bank: e.matmul(
                            psb[bank][:, :], wsl(arena1, k, h * 128, 128),
                            chk(sin, k), start=(k == 0), stop=(k == NCH - 1)),
                            inc=(k == NCH - 1))
                    rec.wait("scalar", "pe", u_pe[h])
                    rec.gate("scalar", "uh%d" % (h % 2))
                    uev[h] = rec.act(lambda e, h=h, bank=bank: e.activation(
                        out=chk(uh, h % 2), in_=psb[bank][:, :], func=AF.Silu))
                    rec.use(bkey[bank], "act", uev[h])

                def emit_o(h):
                    pb = p4 if h % 2 == 0 else p5
                    bk = "b4" if h % 2 == 0 else "b5"
                    rec.wait("tensor", "act", qev[h])
                    rec.wait("tensor", kv_ready[0], kv_ready[1])
                    rec.gate("tensor", bk)
                    o_pe[h] = rec.pe(lambda e, h=h, pb=pb: e.matmul(
                        pb, kvbf[:, h * 128:(h + 1) * 128], chk(qh, h % 2),
                        start=True, stop=True), inc=True)
                    rec.use("qh%d" % (h % 2), "pe", o_pe[h])
                    rec.wait("scalar", "pe", o_pe[h])
                    rec.gate("scalar", "sq%d" % (h % 2))
                    sqo[h] = rec.act(lambda e, h=h, pb=pb: e.activation(
                        out=chk(sqb, h % 2), in_=pb, func=AF.Square))

                def emit_ss(h):
                    rec.wait("tensor", "act", sqo[h])
                    rec.gate("tensor", "b6")
                    ss_pe[h] = rec.pe(lambda e, h=h: e.matmul(
                        p6[0:1, :], ones_one, chk(sqb, h % 2),
                        start=True, stop=True), inc=True)
                    rec.use("sq%d" % (h % 2), "pe", ss_pe[h])
                    rec.wait("scalar", "pe", ss_pe[h])
                    rec.gate("scalar", "rows")
                    a_a = rec.act(lambda e: e.activation(
                        out=srow, in_=p6[0:1, :], func=AF.Sqrt, scale=1.0 / 128.0))
                    rec.use("b6", "act", a_a)
                    rec.wait("vector", "act", a_a)
                    rec.gate("vector", "rows")
                    rec.dve(lambda e: e.tensor_scalar(
                        out=trow, in0=srow, scalar1=1e-12, scalar2=None, op0=OP.max))
                    return rec.dve(lambda e: e.reciprocal(out=rrow, in_=trow))

                def emit_bc(h, d_recip):
                    rec.wait("tensor", "dve", d_recip)
                    rec.gate("tensor", "b7")
                    bc_pe[h] = rec.pe(lambda e: e.matmul(
                        p7, onesrow, rrow, start=True, stop=True), inc=True)
                    rec.use("rows", "pe", bc_pe[h])

                def emit_gate(h):
                    pb = p4 if h % 2 == 0 else p5
                    bk = "b4" if h % 2 == 0 else "b5"
                    g1buf = dvA[:, :] if h % 2 == 0 else dvB[:, :]
                    rec.wait("vector", "pe", o_pe[h])
                    rec.wait("vector", "act", uev[h])
                    rec.dve(lambda e, pb=pb, g1buf=g1buf, h=h: e.tensor_tensor(
                        out=g1buf, in0=pb, in1=chk(uh, h % 2), op=OP.mult))
                    rec.use("uh%d" % (h % 2), "dve", rec.n["dve"])
                    rec.wait("vector", "pe", bc_pe[h])
                    of_d[h] = rec.dve(lambda e, h=h, g1buf=g1buf: e.tensor_tensor(
                        out=chk(aux_a, h), in0=g1buf, in1=p7, op=OP.mult))
                    rec.use(bk, "dve", of_d[h])
                    rec.use("b7", "dve", of_d[h])

                d_recip = [0] * HEADS
                for h in range(HEADS):
                    emit_q(h)
                    if h >= 2:
                        emit_bc(h - 2, d_recip[h - 2])
                        emit_gate(h - 2)
                    if h >= 1:
                        d_recip[h - 1] = emit_ss(h - 1)
                    if h == 0:
                        rec.gate("vector", "aux_a")
                    emit_u(h)
                    emit_o(h)
                emit_bc(HEADS - 2, d_recip[HEADS - 2])
                emit_gate(HEADS - 2)
                d_recip[HEADS - 1] = emit_ss(HEADS - 1)
                emit_bc(HEADS - 1, d_recip[HEADS - 1])
                emit_gate(HEADS - 1)
                rec.wait("gpsimd", "dve", of_d[HEADS - 1])
                st = tile_dma_out(ov, t * T, aux_a, "so")
                rec.use("aux_a", "so", st)
                ost[t] = st
                rec.use("sin", "pe", max(q_pe[HEADS - 1], u_pe[HEADS - 1]))
                last_pe = max(q_pe[HEADS - 1], u_pe[HEADS - 1])
            rec.use("arena0", "pe", last_pe)
            rec.use("arena1", "pe", last_pe)
            rec.merge_use("xbuf0", "arena1")
            rec.merge_use("xbuf1", "arena1")
            return ost

        ost = pass_DE()

        x2st = [0] * nt_own
        xb2st = [0] * nt_own
        rowst = [0] * nt_own

        def pass_F():
            wo_cast = load_weight_fp32(arena0, "arena0", wT["wo"])
            for t in range(nt_own):
                xb = xbuf[t % 2]
                xkey = "xbuf%d" % (t % 2)
                lxs = "lx%d" % (t % 2)
                rec.gate("sync", "sin")
                rec.wait("sync", "so", ost[nt_own - 1])
                lds = tile_dma_in(ov, t * T, sin, "sin")
                rec.gate("sync", xkey)
                rec.wait("sync", "sx1", x1st[nt_own - 1])
                lds_x = tile_dma_in(x1v, t * T, xb, lxs)
                at_pe = [0] * NCH
                x2_d = [0] * NCH

                def attn_chunk(e_ix):
                    bank = e_ix % 4
                    rec.gate("tensor", bkey[bank])
                    rec.wait("tensor", "sin", lds)
                    for k in range(NCH):
                        rec.wait("tensor", "act", wo_cast[k])
                        at_pe[e_ix] = rec.pe(lambda e, k=k, e_ix=e_ix, bank=bank:
                                             e.matmul(
                            psb[bank][:, :], wsl(arena0, k, e_ix * 128, 128),
                            chk(sin, k), start=(k == 0), stop=(k == NCH - 1)),
                            inc=(k == NCH - 1))

                def x2_chunk(e_ix):
                    bank = e_ix % 4
                    rec.wait("vector", "pe", at_pe[e_ix])
                    rec.wait("vector", lxs, lds_x)
                    x2_d[e_ix] = rec.dve(lambda e, e_ix=e_ix, bank=bank, xb=xb: e.tensor_tensor(
                        out=chk(xb, e_ix), in0=psb[bank][:, :], in1=chk(xb, e_ix),
                        op=OP.add))
                    rec.use(bkey[bank], "dve", x2_d[e_ix])

                for e_ix in range(4):
                    attn_chunk(e_ix)
                for e_ix in range(4, NCH):
                    x2_chunk(e_ix - 4)
                    attn_chunk(e_ix)
                for e_ix in range(NCH - 4, NCH):
                    x2_chunk(e_ix)
                rec.use("sin", "pe", at_pe[NCH - 1])
                rec.wait("gpsimd", "dve", x2_d[NCH - 1])
                stx = tile_dma_out(x2v, t * T, xb, "sx2", serialize=True)
                cast2_a = []
                for c in range(NCH):
                    rec.wait("scalar", "dve", x2_d[c])
                    rec.gate("scalar", "aux_a")
                    cast2_a.append(rec.act(lambda e, c=c, xb=xb: e.copy(
                        out=chk(aux_a, c), in_=chk(xb, c))))
                rec.wait("gpsimd", "act", cast2_a[NCH - 1])
                stb = tile_dma_out(xb2v, t * T, aux_a, "sxb2")
                rec.use("aux_a", "sxb2", stb)
                rec.use(xkey, "sx2", stx)
                rec.use(xkey, "act", cast2_a[NCH - 1])
                x2st[t] = stx
                xb2st[t] = stb
            rec.use("arena0", "pe", at_pe[NCH - 1])

        pass_F()

        def pass_G():
            wt_pool = load_weight_bf16(arena0, "arena0", wt_s)
            out_last = 0
            for t in range(nt_own):
                xb = xbuf[t % 2]
                xkey = "xbuf%d" % (t % 2)
                lxs = "lx%d" % (t % 2)
                rec.gate("sync", "sin")
                rec.wait("sync", "sxb2", xb2st[nt_own - 1])
                lds = tile_dma_in(xb2v, t * T, sin, "sin")
                rec.gate("sync", xkey)
                rec.wait("sync", "sx2", x2st[nt_own - 1])
                lds_x = tile_dma_in(x2v, t * T, xb, lxs)
                srcg = [(chk(sin, c), lds) for c in range(NCH)]
                for c in range(NCH):
                    rec.wait("scalar", "sin", lds)
                sqg = ln_squares(srcg, aux_a, "aux_a")
                rec.wait("tensor", "sin", lds)
                pe_stg = ln_stats_mms([(chk(sin, c), None) for c in range(NCH)],
                                      sqg, aux_a, "aux_a")
                rec.use("sin", "pe", pe_stg)
                d_r3 = ln_smalls(pe_stg, mul_half_scale=True)
                bc3 = bcasts_45(mrow, rrow, d_r3)
                ffn_pe = [0] * NCH
                out_dv = [0] * NCH

                def ffn2_chunk(e_ix):
                    bank = e_ix % 4
                    rec.gate("tensor", bkey[bank])
                    rec.wait("tensor", "wt", wt_pool)
                    rec.wait("tensor", "sin", lds)
                    for k in range(NCH):
                        ffn_pe[e_ix] = rec.pe(lambda e, k=k, e_ix=e_ix, bank=bank:
                                              e.matmul(
                            psb[bank][:, :], wsl(arena0, k, e_ix * 128, 128),
                            chk(sin, k), start=(k == 0), stop=(k == NCH - 1)),
                            inc=(k == NCH - 1))

                def oev_chunk(e_ix):
                    bank = e_ix % 4
                    rec.wait("vector", "pe", ffn_pe[e_ix])
                    rec.wait("vector", "pe", bc3)
                    rec.wait("vector", lxs, lds_x)
                    rec.dve(lambda e, e_ix=e_ix: e.tensor_scalar(
                        out=dvA[:, :], in0=p4, scalar1=c_wt[:, e_ix:e_ix + 1],
                        scalar2=None, op0=OP.mult))
                    rec.dve(lambda e, bank=bank: e.tensor_tensor(
                        out=dvB[:, :], in0=psb[bank][:, :], in1=dvA[:, :], op=OP.subtract))
                    rec.dve(lambda e: e.tensor_tensor(
                        out=dvA[:, :], in0=dvB[:, :], in1=p5, op=OP.mult))
                    out_dv[e_ix] = rec.dve(lambda e, e_ix=e_ix, xb=xb: e.scalar_tensor_tensor(
                        out=chk(xb, e_ix), in0=dvA[:, :], scalar=hb[:, e_ix:e_ix + 1],
                        in1=chk(xb, e_ix), op0=OP.add, op1=OP.add))
                    rec.use(bkey[bank], "dve", out_dv[e_ix])
                    rec.use("b4", "dve", out_dv[e_ix])
                    rec.use("b5", "dve", out_dv[e_ix])

                for e_ix in range(4):
                    ffn2_chunk(e_ix)
                for e_ix in range(4, NCH):
                    oev_chunk(e_ix - 4)
                    ffn2_chunk(e_ix)
                for e_ix in range(NCH - 4, NCH):
                    oev_chunk(e_ix)
                rec.use("sin", "pe", ffn_pe[NCH - 1])
                rec.wait("gpsimd", "dve", out_dv[NCH - 1])
                st = tile_dma_out(outv, t * T, xb, "sout")
                rec.use(xkey, "sout", st)
                out_last = st
            rec.raw("gpsimd", lambda e, s, v=out_last: e.wait_ge(s["sout"], v))

        pass_G()

        @block.sync
        def _(eng):
            for fn in rec.streams["sync"]:
                fn(eng, sems)

        @block.scalar
        def _(eng):
            for fn in rec.streams["scalar"]:
                fn(eng, sems)

        @block.vector
        def _(eng):
            for fn in rec.streams["vector"]:
                fn(eng, sems)

        @block.tensor
        def _(eng):
            for fn in rec.streams["tensor"]:
                fn(eng, sems)

        @block.gpsimd
        def _(eng):
            for fn in rec.streams["gpsimd"]:
                fn(eng, sems)

    return nc


_NC_CACHE = {}


def _get_nc(debug=False):
    if debug not in _NC_CACHE:
        _NC_CACHE[debug] = build_nc(debug)
    return _NC_CACHE[debug]


def _consts_array():
    c = np.zeros((128, 144), np.float32)
    c[:, 0] = 1.0
    c[:, 1] = 0.5
    c[:, 2] = 1.0 / 2048.0
    c[:, 4:132] = 1.0
    for i, v in enumerate((EPS_LN, ISCALE, 1.0 / 128.0, 0.0, 1.0,
                           1e-12, 0.5, 1.0 / NW, EPS_T)):
        c[:, 132 + i] = v
    return c


def kernel(L0_attn_x, L0_attn_wq, L0_attn_wk, L0_attn_wv, L0_attn_wu,
           L0_attn_wo, L0_ffn_weight, L0_ffn_bias, _debug=False):
    x = np.asarray(L0_attn_x, dtype=np.float32)
    x_fm = np.ascontiguousarray(x.reshape(2 * SEQ, D).T)
    wkT_full = np.asarray(L0_attn_wk, np.float32).T
    wvT_full = np.asarray(L0_attn_wv, np.float32).T
    wts = {
        "wqT": np.ascontiguousarray(np.asarray(L0_attn_wq, np.float32).T),
        "wuT": np.ascontiguousarray(np.asarray(L0_attn_wu, np.float32).T),
        "woT": np.ascontiguousarray(np.asarray(L0_attn_wo, np.float32).T),
        "wfT": np.ascontiguousarray(np.asarray(L0_ffn_weight, np.float32).T),
    }
    bias = np.ascontiguousarray(np.asarray(L0_ffn_bias, np.float32).reshape(16, 128).T)
    consts = _consts_array()

    salt = np.random.default_rng().integers(1, 2 ** 31, size=(1, 4)).astype(np.uint32)
    in_maps = []
    for c in range(8):
        sq = c // 4
        q = c % 4
        par = c % 2
        base = sq * SEQ
        own = x_fm[:, base + q * QTOK: base + (q + 1) * QTOK]
        fq = (q + 2) % 4
        foreign = x_fm[:, base + fq * QTOK: base + (fq + 1) * QTOK]
        xc = np.ascontiguousarray(np.concatenate([own, foreign], axis=1))
        m = {"x": xc, "bias": bias, "consts_in": consts, "salt": salt}
        m["wkT"] = np.ascontiguousarray(wkT_full[:, par * 1024:(par + 1) * 1024])
        m["wvT"] = np.ascontiguousarray(wvT_full[:, par * 1024:(par + 1) * 1024])
        m.update(wts)
        in_maps.append(m)

    nc = _get_nc(_debug)
    res = run_bass_kernel_spmd(nc, in_maps, core_ids=list(range(8)))
    shards = [res.results[c]["out"] for c in range(8)]
    full = np.concatenate(shards, axis=1)
    out = np.ascontiguousarray(full.T).reshape(2, SEQ, D)
    if _debug:
        return out, res
    return out


# version 10
# speedup vs baseline: 1.8375x; 1.0213x over previous
import contextlib

import numpy as np

import concourse.bass as bass
import concourse.mybir as mybir
from concourse.bass_utils import run_bass_kernel_spmd

F32 = mybir.dt.float32
BF16 = mybir.dt.bfloat16
AF = mybir.ActivationFunctionType
OP = mybir.AluOpType
AX = mybir.AxisListType.X

D = 2048
NCH = 16
T = 512
SEQ = 16384
QTOK = 4096
NT_SEQ = SEQ // T
NT_OWN = QTOK // T
HEADS = 16
ISCALE = 1.0 / float(np.sqrt(128.0))
EPS_LN = 1e-5
EPS_T = 1e-10
NW = float(D * D)

ENGS = ("sync", "scalar", "vector", "tensor", "gpsimd")


class Rec:

    def __init__(self):
        self.streams = {e: [] for e in ENGS}
        self.n = {k: 0 for k in ("lx0", "lx1", "sin", "rows", "w0", "w1",
                                 "wt", "misc", "sy2", "sx1", "so", "sx2",
                                 "sxb2", "srows", "sout",
                                 "pck", "pcv", "pcq", "pcu", "pco",
                                 "sy2s", "kvx", "rdz", "sinp",
                                 "act", "dve", "pe")}
        self.last_wait = {}
        self.buf = {}

    def wait(self, eng, sem, val):
        if not val or val <= 0:
            return
        assert val <= self.n[sem], (eng, sem, val, self.n[sem])
        if self.last_wait.get((eng, sem), 0) >= val:
            return
        self.last_wait[(eng, sem)] = val

        def fn(e, s, sem=sem, val=val):
            e.wait_ge(s[sem], val)
        self.streams[eng].append(fn)

    def raw(self, eng, fn):
        self.streams[eng].append(fn)

    def dma(self, eng, sem, fn):
        self.n[sem] += 16

        def run(e, s, fn=fn, sem=sem):
            fn(e).then_inc(s[sem], 16)
        self.streams[eng].append(run)
        return self.n[sem]

    def act(self, fn):
        self.wait("scalar", "act", self.n["act"])
        self.n["act"] += 1

        def run(e, s, fn=fn):
            fn(e).then_inc(s["act"], 1)
        self.streams["scalar"].append(run)
        return self.n["act"]

    def dve(self, fn):
        self.wait("vector", "dve", self.n["dve"])
        self.n["dve"] += 1

        def run(e, s, fn=fn):
            fn(e).then_inc(s["dve"], 1)
        self.streams["vector"].append(run)
        return self.n["dve"]

    def pe(self, fn, inc=False):
        if inc:
            self.n["pe"] += 1

            def run(e, s, fn=fn):
                fn(e).then_inc(s["pe"], 1)
        else:
            def run(e, s, fn=fn):
                fn(e)
        self.streams["tensor"].append(run)
        return self.n["pe"]

    def use(self, key, sem, cnt):
        d = self.buf.setdefault(key, {})
        d[sem] = max(d.get(sem, 0), cnt)

    def gate(self, eng, key):
        for sem, cnt in sorted(self.buf.get(key, {}).items()):
            self.wait(eng, sem, cnt)

    def merge_use(self, dst, *srcs):
        for s in srcs:
            for sem, cnt in self.buf.get(s, {}).items():
                self.use(dst, sem, cnt)


def build_nc(debug=False, seq=SEQ, qtok=QTOK, race_check=False, no_poll=False):
    nt_seq = seq // T
    nt_own = qtok // T
    nt_a = 2 * nt_own
    half_cols = seq // 2
    nc = bass.Bass("TRN2", target_bir_lowering=False,
                   detect_race_conditions=race_check)

    x_in = nc.dram_tensor("x", [D, 2 * qtok], F32, kind="ExternalInput").ap()
    wT = {}
    for nm in ("wq", "wu", "wo", "wf"):
        wT[nm] = nc.dram_tensor(nm + "T", [D, D], F32, kind="ExternalInput").ap()
    for nm in ("wk", "wv"):
        wT[nm] = nc.dram_tensor(nm + "T", [D, D // 2], F32, kind="ExternalInput").ap()
    salt_in = nc.dram_tensor("salt", [1, 4], mybir.dt.uint32, kind="ExternalInput").ap()
    bias_in = nc.dram_tensor("bias", [128, 16], F32, kind="ExternalInput").ap()
    consts_in = nc.dram_tensor("consts_in", [128, 144], F32, kind="ExternalInput").ap()
    out_d = nc.dram_tensor("out", [D, qtok], F32, kind="ExternalOutput").ap()

    skind = "ExternalOutput" if debug else "Internal"
    y2_s = nc.dram_tensor("y2_s", [D, 2 * qtok], BF16, kind=skind).ap()
    y2_sh = nc.dram_tensor("y2_sh", [D, seq], BF16, addr_space="Shared").ap()
    kv_sh = nc.dram_tensor("kv_sh", [2, 128, D // 2], BF16, addr_space="Shared").ap()
    fl_sh = nc.dram_tensor("fl_sh", [2, 2, 16], mybir.dt.uint32, addr_space="Shared").ap()
    x1_s = nc.dram_tensor("x1_s", [D, qtok], F32, kind=skind).ap()
    o_s = nc.dram_tensor("o_s", [D, qtok], BF16, kind=skind).ap()
    x2_s = nc.dram_tensor("x2_s", [D, qtok], F32, kind=skind).ap()
    xb2_s = nc.dram_tensor("xb2_s", [D, qtok], BF16, kind=skind).ap()
    wt_s = nc.dram_tensor("wt_s", [D, D], BF16, kind=skind).ap()
    w_bf = {}
    for nm in ("wk", "wv", "wq", "wu", "wo"):
        w_bf[nm] = nc.dram_tensor(nm + "_bf", [D, D], BF16, kind="Internal").ap()
    m3_s = nc.dram_tensor("m3_s", [1, qtok], F32, kind=skind).ap()
    r3_s = nc.dram_tensor("r3_s", [1, qtok], F32, kind=skind).ap()
    dbg_fl = nc.dram_tensor("dbg_fl", [1, 16], mybir.dt.uint32, kind="ExternalOutput").ap()
    if debug:
        kv_dbg = nc.dram_tensor("kv_dbg", [128, D], F32, kind="ExternalOutput").ap()

    rec = Rec()

    with contextlib.ExitStack() as stack:
        E = stack.enter_context
        arena0 = E(nc.sbuf_tensor("arena0", [128, NCH * D], BF16))
        arena1 = E(nc.sbuf_tensor("arena1", [128, NCH * D], BF16))
        sin = E(nc.sbuf_tensor("sin", [128, NCH * T], BF16))
        aux_a = E(nc.sbuf_tensor("aux_a", [128, NCH * T], BF16))
        aux_b = E(nc.sbuf_tensor("aux_b", [128, NCH * T], BF16))
        stg = E(nc.sbuf_tensor("stg", [128, D], F32))
        sqb = E(nc.sbuf_tensor("sqb", [128, 2 * T], BF16))
        wstg = E(nc.sbuf_tensor("wstg", [128, 1024], BF16))
        kvbf = E(nc.sbuf_tensor("kvbf", [128, D], BF16))
        dvA = E(nc.sbuf_tensor("dvA", [128, T], F32))
        dvB = E(nc.sbuf_tensor("dvB", [128, T], F32))
        qh = E(nc.sbuf_tensor("qh", [128, 2 * T], BF16))
        uh = E(nc.sbuf_tensor("uh", [128, 2 * T], F32))
        consts = E(nc.sbuf_tensor("consts", [128, 144], F32))
        consts2 = E(nc.sbuf_tensor("consts2", [128, 4], F32))
        onesb = E(nc.sbuf_tensor("onesb", [128, 2], BF16))
        c_wt = E(nc.sbuf_tensor("c_wt", [128, 16], F32))
        hb = E(nc.sbuf_tensor("hb", [128, 16], F32))
        rows1 = E(nc.sbuf_tensor("rows1", [65, T], F32))
        rows2 = E(nc.sbuf_tensor("rows2", [65, T], F32))
        salt_sb = E(nc.sbuf_tensor("salt_sb", [1, 4], mybir.dt.uint32))
        flag_rd = E(nc.sbuf_tensor("flag_rd", [1, 4], mybir.dt.uint32))

        psb = [E(nc.psum_tensor("p%d" % i, [128, 512], F32)) for i in range(4)]
        p_kv = E(nc.psum_tensor("p_kv", [128, 2048], F32))

        sems = {}
        for s in ("lx0", "lx1", "sin", "rows", "w0", "w1", "wt", "misc",
                  "sy2", "sx1", "so", "sx2", "sxb2", "srows", "sout",
                  "pck", "pcv", "pcq", "pcu", "pco",
                  "sy2s", "kvx", "rdz", "sinp",
                  "act", "dve", "pe"):
            sems[s] = E(nc.semaphore("s_" + s))
        sems["pol"] = E(nc.semaphore("s_pol"))
        block = E(nc.Block())

        a1f = arena1.ap().bitcast(F32)
        xbuf = [a1f[:, 0:NCH * T], a1f[:, NCH * T:2 * NCH * T]]
        p4 = p_kv[:, 0:512]
        p5 = p_kv[:, 512:1024]
        p6 = p_kv[:, 1024:1536]
        p7 = p_kv[:, 1536:2048]
        bkey = ["b0", "b1", "b2", "b3", "b4", "b5", "b6", "b7"]

        def wsl(arena, k, j0, w):
            return arena[:, k * D + j0:k * D + j0 + w]

        def chk(buf, c, w=T):
            return buf[:, c * w:(c + 1) * w]

        onescol = consts[:, 0:1]
        half11 = consts[0:1, 1:2]
        inv2048 = consts[:, 2:3]
        onesrow = consts[0:1, 4:132]
        thr_col = consts2[:, 0:1]
        half_scale = consts2[0:1, 1:2]
        mean_w = consts2[0:1, 2:3]
        acc_col = consts2[:, 3:4]
        ones_mean = onesb[:, 0:1]
        ones_one = onesb[:, 1:2]

        for _cix, _cval in enumerate((EPS_LN, ISCALE, 1.0 / 128.0, 0.0, 1.0,
                                      1e-12, 0.5, 1.0 / NW, EPS_T)):
            nc.const_aps.register(F32, _cval, consts[:, 132 + _cix:133 + _cix])

        mrow = rows1[0:1, :]
        trow = rows1[32:33, :]
        srow = rows1[64:65, :]
        rrow = rows2[0:1, :]
        r1row = rows2[32:33, :]

        xv = x_in.rearrange("(c p) t -> p c t", p=128)
        y2v = y2_s.rearrange("(c p) t -> p c t", p=128)
        y2shv = y2_sh.rearrange("(c p) t -> p c t", p=128)
        x1v = x1_s.rearrange("(c p) t -> p c t", p=128)
        ov = o_s.rearrange("(c p) t -> p c t", p=128)
        x2v = x2_s.rearrange("(c p) t -> p c t", p=128)
        xb2v = xb2_s.rearrange("(c p) t -> p c t", p=128)
        outv = out_d.rearrange("(c p) t -> p c t", p=128)

        def tile_dma_in(dramv, col0, sbuf_buf, sem):
            for j in range(4):
                dsl = dramv[:, 4 * j:4 * j + 4, col0:col0 + T]
                ssl = sbuf_buf[:, 4 * j * T:(4 * j + 4) * T].rearrange(
                    "p (c t) -> p c t", t=T)
                end = rec.dma("sync", sem,
                              lambda e, a=ssl, b=dsl: e.dma_start(out=a, in_=b))
            return end

        def tile_dma_out(dramv, col0, sbuf_buf, sem, serialize=False):
            if serialize:
                rec.wait("gpsimd", sem, rec.n[sem])
            for j in range(4):
                dsl = dramv[:, 4 * j:4 * j + 4, col0:col0 + T]
                ssl = sbuf_buf[:, 4 * j * T:(4 * j + 4) * T].rearrange(
                    "p (c t) -> p c t", t=T)
                end = rec.dma("gpsimd", sem,
                              lambda e, a=dsl, b=ssl: e.dma_start(out=a, in_=b))
            return end

        c_ld = rec.dma("sync", "rows", lambda e: e.dma_start(out=consts[:, :], in_=consts_in[:, :]))
        rec.dma("sync", "rows", lambda e: e.dma_start(out=salt_sb[0:1, :], in_=salt_in[:, :]))
        c_salt = rec.n["rows"]
        rec.wait("scalar", "rows", c_salt)
        c_fz = rec.act(lambda e: e.memzero(flag_rd[0:1, :]))
        rec.act(lambda e: e.memzero(wstg[:, :]))
        rec.act(lambda e: e.copy(out=ones_mean, in_=inv2048))
        c_ones = rec.act(lambda e: e.copy(out=ones_one, in_=onescol))

        red_dve = []
        for hix in range(2 * NCH):
            k, half = divmod(hix, 2)
            sl = stg[:, half * 1024:(half + 1) * 1024]
            src = wT["wf"][k * 128:(k + 1) * 128, half * 1024:(half + 1) * 1024]
            if hix >= 2:
                rec.wait("gpsimd", "dve", red_dve[hix - 2])
            c = rec.dma("gpsimd", "w%d" % (hix % 2),
                        lambda e, a=sl, b=src: e.dma_start(out=a, in_=b))
            rec.wait("vector", "w%d" % (hix % 2), c)
            if hix == 0:
                rec.dve(lambda e, a=sl: e.tensor_reduce(
                    out=acc_col, in_=a, axis=AX, op=OP.add, apply_absolute_value=True))
            else:
                rec.dve(lambda e, a=sl: e.tensor_reduce(
                    out=dvA[:, 0:1], in_=a, axis=AX, op=OP.add, apply_absolute_value=True))
                rec.dve(lambda e: e.tensor_tensor(
                    out=acc_col, in0=acc_col, in1=dvA[:, 0:1], op=OP.add))
            red_dve.append(rec.n["dve"])
        rec.wait("tensor", "dve", red_dve[-1])
        rec.wait("tensor", "rows", c_salt)
        c_tot = rec.pe(lambda e: e.matmul(psb[0][0:1, 0:1], acc_col, onescol,
                                          start=True, stop=True), inc=True)
        rec.wait("vector", "pe", c_tot)
        rec.dve(lambda e: e.tensor_scalar(out=mean_w, in0=psb[0][0:1, 0:1],
                                          scalar1=1.0 / NW, scalar2=None, op0=OP.mult))
        rec.dve(lambda e: e.tensor_scalar(out=rows1[0:1, 0:1], in0=mean_w,
                                          scalar1=EPS_T, scalar2=0.5,
                                          op0=OP.add, op1=OP.mult))
        c_hs = rec.dve(lambda e: e.tensor_scalar(out=half_scale, in0=mean_w,
                                                 scalar1=0.5, scalar2=None, op0=OP.mult))
        rec.wait("tensor", "dve", c_hs)
        c_thrb = rec.pe(lambda e: e.matmul(psb[1][:, 0:1], onesrow, rows1[0:1, 0:1],
                                           start=True, stop=True), inc=True)
        rec.wait("scalar", "pe", c_thrb)
        c_thr = rec.act(lambda e: e.copy(out=thr_col, in_=psb[1][:, 0:1]))

        rec.wait("gpsimd", "dve", red_dve[-1])
        tern_act = []
        tern_dve = []
        for hix in range(2 * NCH):
            k, half = divmod(hix, 2)
            sl = stg[:, half * 1024:(half + 1) * 1024]
            src = wT["wf"][k * 128:(k + 1) * 128, half * 1024:(half + 1) * 1024]
            if hix >= 2:
                rec.wait("gpsimd", "act", tern_act[hix - 2])
                rec.wait("gpsimd", "dve", tern_dve[hix - 2])
            c = rec.dma("gpsimd", "w%d" % (hix % 2),
                        lambda e, a=sl, b=src: e.dma_start(out=a, in_=b))
            rec.wait("vector", "w%d" % (hix % 2), c)
            rec.wait("vector", "act", c_thr)
            rec.wait("scalar", "w%d" % (hix % 2), c)
            for qx in range(2):
                q = sl[:, qx * 512:(qx + 1) * 512]
                dst = arena0[:, k * D + half * 1024 + qx * 512:
                             k * D + half * 1024 + (qx + 1) * 512]
                c_abs = rec.act(lambda e, q=q: e.activation(
                    out=dvA[:, :], in_=q, func=AF.Abs))
                rec.wait("vector", "act", c_abs)
                rec.dve(lambda e: e.tensor_scalar(
                    out=dvA[:, :], in0=dvA[:, :], scalar1=thr_col, scalar2=None, op0=OP.is_gt))
                c_s = rec.act(lambda e, q=q: e.sign(out=dvB[:, :], in_=q))
                rec.wait("vector", "act", c_s)
                c_w = rec.dve(lambda e: e.tensor_tensor(
                    out=dvB[:, :], in0=dvA[:, :], in1=dvB[:, :], op=OP.mult))
                rec.wait("scalar", "dve", c_w)
                c_cp = rec.act(lambda e, dst=dst: e.copy(out=dst, in_=dvB[:, :]))
            tern_act.append(c_cp)
            tern_dve.append(c_w)
        c_tern = tern_act[-1]
        rec.use("stg", "act", c_tern)
        rec.use("stg", "dve", tern_dve[-1])

        rec.wait("gpsimd", "act", c_tern)
        for k in range(NCH):
            sl = arena0[:, k * D:(k + 1) * D]
            dst = wt_s[k * 128:(k + 1) * 128, :]
            rec.dma("gpsimd", "wt", lambda e, a=dst, b=sl: e.dma_start(out=a, in_=b))
        wt_store_total = rec.n["wt"]
        rec.use("arena0", "wt", wt_store_total)

        rec.wait("tensor", "act", max(c_tern, c_ones))
        for j in range(NCH):
            bank = j % 4
            rec.gate("tensor", bkey[bank])
            for k in range(NCH):
                c_cwt_pe = rec.pe(lambda e, j=j, k=k, bank=bank: e.matmul(
                    psb[bank][:, 0:1], wsl(arena0, k, j * 128, 128), ones_one,
                    start=(k == 0), stop=(k == NCH - 1)),
                    inc=(k == NCH - 1))
            rec.wait("scalar", "pe", c_cwt_pe)
            c_cwt = rec.act(lambda e, j=j, bank=bank: e.copy(
                out=c_wt[:, j:j + 1], in_=psb[bank][:, 0:1]))
            rec.use(bkey[bank], "act", c_cwt)

        c_brow = rec.dma("gpsimd", "misc",
                         lambda e: e.dma_start(out=hb[:, :], in_=bias_in[:, :]))
        rec.wait("scalar", "misc", c_brow)
        c_hb = rec.act(lambda e: e.activation(out=hb[:, :], in_=hb[:, :],
                                              func=AF.Copy, scale=0.5))

        def ln_smalls(pe_stats, mul_half_scale):
            rec.wait("scalar", "pe", pe_stats)
            rec.gate("scalar", "rows")
            a_m = rec.act(lambda e: e.copy(out=mrow, in_=psb[2][0:1, :]))
            rec.use("b2", "act", a_m)
            rec.wait("vector", "act", a_m)
            rec.gate("vector", "rows")
            rec.dve(lambda e: e.tensor_tensor(out=trow, in0=mrow, in1=mrow, op=OP.mult))
            d_var = rec.dve(lambda e: e.tensor_tensor(
                out=trow, in0=psb[3][0:1, :], in1=trow, op=OP.subtract))
            rec.use("b3", "dve", d_var)
            rec.wait("scalar", "dve", d_var)
            a_s = rec.act(lambda e: e.activation(out=srow, in_=trow, func=AF.Sqrt,
                                                 bias=EPS_LN))
            rec.wait("vector", "act", a_s)
            rec.dve(lambda e: e.reciprocal(out=r1row, in_=srow))
            if mul_half_scale:
                d_r = rec.dve(lambda e: e.tensor_scalar(
                    out=rrow, in0=r1row, scalar1=half_scale, scalar2=None, op0=OP.mult))
            else:
                d_r = rec.dve(lambda e: e.tensor_copy(out=rrow, in_=r1row))
            return d_r

        def ln_squares(src_acts, dst_buf, dst_key):
            sq_acts = []
            rec.gate("scalar", dst_key)
            for c, (ap_, cnt) in enumerate(src_acts):
                sq_acts.append(rec.act(lambda e, ap_=ap_, c=c: e.activation(
                    out=chk(dst_buf, c), in_=ap_, func=AF.Square)))
            return sq_acts

        def ln_stats_mms(src_acts, sq_acts, sq_buf, sq_key):
            rec.gate("tensor", "b2")
            rec.gate("tensor", "b3")
            for c, (ap_, cnt) in enumerate(src_acts):
                if cnt is not None:
                    rec.wait("tensor", "act", cnt)
                rec.pe(lambda e, ap_=ap_, c=c: e.matmul(
                    psb[2][0:1, :], ones_mean, ap_,
                    start=(c == 0), stop=(c == NCH - 1)), inc=(c == NCH - 1))
            for c in range(NCH):
                rec.wait("tensor", "act", sq_acts[c])
                pe_c = rec.pe(lambda e, c=c, sq_buf=sq_buf: e.matmul(
                    psb[3][0:1, :], ones_mean, chk(sq_buf, c),
                    start=(c == 0), stop=(c == NCH - 1)), inc=(c == NCH - 1))
            rec.use(sq_key, "pe", pe_c)
            return pe_c

        def bcasts_45(row_a, row_b, dve_ready):
            rec.wait("tensor", "dve", dve_ready)
            rec.gate("tensor", "b4")
            rec.gate("tensor", "b5")
            rec.pe(lambda e: e.matmul(p4, onesrow, row_a, start=True, stop=True))
            ret = rec.pe(lambda e: e.matmul(p5, onesrow, row_b,
                                            start=True, stop=True), inc=True)
            rec.use("rows", "pe", ret)
            return ret

        def load_weight_fp32(arena, arena_key, src, half=False):
            done = [0] * NCH
            first = True
            nh = NCH if half else 2 * NCH
            for hix in range(nh):
                if half:
                    k, hf = hix, 0
                    srcsl = src[k * 128:(k + 1) * 128, 0:1024]
                    dstsl = arena[:, k * 1024:(k + 1) * 1024]
                else:
                    k, hf = divmod(hix, 2)
                    srcsl = src[k * 128:(k + 1) * 128, hf * 1024:(hf + 1) * 1024]
                    dstsl = arena[:, k * D + hf * 1024:k * D + (hf + 1) * 1024]
                slot = hix % 2
                sl = stg[:, slot * 1024:(slot + 1) * 1024]
                skey = "stgh%d" % slot
                rec.gate("gpsimd", skey)
                if first:
                    rec.gate("gpsimd", "stg")
                c = rec.dma("gpsimd", "w%d" % slot,
                            lambda e, a=sl, b=srcsl: e.dma_start(out=a, in_=b))
                if first:
                    rec.gate("scalar", arena_key)
                    first = False
                rec.wait("scalar", "w%d" % slot, c)
                cc = rec.act(lambda e, a=dstsl, b=sl: e.copy(out=a, in_=b))
                rec.use(skey, "act", cc)
                if half or hf == 1:
                    done[k] = cc
            return done

        def load_weight_bf16(arena, arena_key, src, pc_sem=None):
            rec.gate("gpsimd", arena_key)
            if pc_sem is not None:
                rec.wait("gpsimd", pc_sem, rec.n[pc_sem])
            rec.wait("gpsimd", "wt", rec.n["wt"])
            for k in range(NCH):
                done = rec.dma("gpsimd", "wt",
                               lambda e, a=arena[:, k * D:(k + 1) * D],
                               b=src[k * 128:(k + 1) * 128, :]:
                               e.dma_start(out=a, in_=b))
            return done

        POOL_ENG = [mybir.EngineType.Pool]

        def emit_switch_pool(arm_fn, sem_incs):
            def closure(e, s):
                for arm in nc.switch_core_id(engines=[e], n=128):
                    arm_fn(int(arm) % 2, e, s)
            rec.raw("gpsimd", closure)
            for sem, inc in sem_incs.items():
                rec.n[sem] += inc

        _pollid = [0]

        def emit_poll(phase, dep_sem, dep_val):
            _cfz = c_fz

            def closure(e, s, phase=phase, _cfz=_cfz):
                e.wait_ge(s[dep_sem], dep_val)
                e.wait_ge(s["act"], _cfz)
                pid = _pollid[0]
                _pollid[0] += 1
                lbl = "poll_%d" % pid
                c2 = lbl + "_c2"
                done = lbl + "_done"
                e.br(lbl)
                with nc.body(lbl, valid_engines=POOL_ENG):
                    e.dma_start(out=flag_rd[0:1, 2 * phase:2 * phase + 1],
                                in_=fl_sh[0:1, phase, 0:1]).then_inc(s["pol"], 16)
                    e.dma_start(out=flag_rd[0:1, 2 * phase + 1:2 * phase + 2],
                                in_=fl_sh[1:2, phase, 0:1]).then_inc(s["pol"], 16)
                    e.tensor_copy(out=wstg[:, 0:256], in_=wstg[:, 256:512])
                    v0 = e.value_load(flag_rd[0:1, 2 * phase:2 * phase + 1])
                    vs = e.value_load(salt_sb[0:1, 0:1])
                    e.br_ne(v0, vs, on_true=lbl, on_false=c2)
                with nc.body(c2, valid_engines=POOL_ENG):
                    v1 = e.value_load(flag_rd[0:1, 2 * phase + 1:2 * phase + 2])
                    vs2 = e.value_load(salt_sb[0:1, 0:1])
                    e.br_ne(v1, vs2, on_true=lbl, on_false=done)
                nc.switch_bb(done)
                e.sem_inc(s["rdz"], 1)
            rec.raw("gpsimd", closure)
            rec.n["rdz"] += 1
            return rec.n["rdz"]

        wt_cast = [c_tern] * NCH

        y2st = [0] * nt_a
        x1st = [0] * nt_own

        pc_jobs = []
        for nm, sem in (("wk", "pck"), ("wv", "pcv"), ("wq", "pcq"),
                        ("wu", "pcu"), ("wo", "pco")):
            for k in range(NCH):
                for half in range(2):
                    pc_jobs.append((nm, sem, k, half))
        pc_done = {}

        def emit_precast(njobs):
            for _ in range(njobs):
                if not pc_jobs:
                    return
                nm, sem, k, half = pc_jobs.pop(0)
                if nm == "wk" and k == 0 and half == 0:
                    rec.gate("gpsimd", "stg")
                slot = (k * 2 + half) % 2
                sl = stg[:, slot * 1024:(slot + 1) * 1024]
                bsl = wstg[:, 0:1024]
                rec.gate("gpsimd", "stgh%d" % slot)
                c = rec.dma("gpsimd", "w%d" % slot,
                            lambda e, a=sl, b=wT[nm][k * 128:(k + 1) * 128,
                                                     half * 1024:(half + 1) * 1024]:
                            e.dma_start(out=a, in_=b))
                rec.wait("scalar", "w%d" % slot, c)
                rec.gate("scalar", "wstg")
                cc = rec.act(lambda e, a=bsl, b=sl: e.copy(out=a, in_=b))
                rec.use("stgh%d" % slot, "act", cc)
                rec.wait("gpsimd", "act", cc)
                st = rec.dma("gpsimd", sem,
                             lambda e, a=w_bf[nm][k * 128:(k + 1) * 128,
                                                  half * 1024:(half + 1) * 1024],
                             b=bsl: e.dma_start(out=a, in_=b))
                rec.use("wstg", sem, st)
                pc_done[nm] = (sem, st)

        pc_jobs.clear()
        pc_per_tile = 0

        def pass_A():
            for i in range(nt_a):
                xb = xbuf[i % 2]
                xkey = "xbuf%d" % (i % 2)
                lxs = "lx%d" % (i % 2)
                rec.gate("sync", xkey)
                lds = tile_dma_in(xv, i * T, xb, lxs)
                cast_a = []
                for c in range(NCH):
                    rec.wait("scalar", lxs, lds)
                    rec.gate("scalar", "sin")
                    cast_a.append(rec.act(lambda e, c=c, xb=xb: e.copy(
                        out=chk(sin, c), in_=chk(xb, c))))
                rec.use(xkey, "act", cast_a[-1])
                src1 = [(chk(sin, c), cast_a[c]) for c in range(NCH)]
                sq1 = ln_squares(src1, aux_a, "aux_a")
                pe_st1 = ln_stats_mms(src1, sq1, aux_a, "aux_a")
                rec.use("sin", "pe", pe_st1)
                rec.wait("vector", "pe", pe_st1)
                d_r1 = ln_smalls(pe_st1, mul_half_scale=True)
                ffn_pe = [0] * NCH
                bc1 = None

                def ffn_chunk(e_ix):
                    bank = e_ix % 4
                    rec.gate("tensor", bkey[bank])
                    for k in range(NCH):
                        rec.wait("tensor", "act", wt_cast[k])
                        rec.wait("tensor", "act", cast_a[k])
                        ffn_pe[e_ix] = rec.pe(
                            lambda e, k=k, e_ix=e_ix, bank=bank: e.matmul(
                                psb[bank][:, :], wsl(arena0, k, e_ix * 128, 128),
                                chk(sin, k), start=(k == 0), stop=(k == NCH - 1)),
                            inc=(k == NCH - 1))

                evac_d = [0] * NCH

                def evac_chunk(e_ix):
                    bank = e_ix % 4
                    rec.wait("vector", "pe", ffn_pe[e_ix])
                    rec.wait("vector", "pe", bc1)
                    rec.dve(lambda e, e_ix=e_ix: e.tensor_scalar(
                        out=dvA[:, :], in0=p4, scalar1=c_wt[:, e_ix:e_ix + 1],
                        scalar2=None, op0=OP.mult))
                    rec.dve(lambda e, bank=bank: e.tensor_tensor(
                        out=dvB[:, :], in0=psb[bank][:, :], in1=dvA[:, :], op=OP.subtract))
                    rec.dve(lambda e: e.tensor_tensor(
                        out=dvA[:, :], in0=dvB[:, :], in1=p5, op=OP.mult))
                    evac_d[e_ix] = rec.dve(lambda e, e_ix=e_ix, xb=xb: e.scalar_tensor_tensor(
                        out=chk(xb, e_ix), in0=dvA[:, :], scalar=hb[:, e_ix:e_ix + 1],
                        in1=chk(xb, e_ix), op0=OP.add, op1=OP.add))
                    rec.use(bkey[bank], "dve", evac_d[e_ix])
                    rec.use("b4", "dve", evac_d[e_ix])
                    rec.use("b5", "dve", evac_d[e_ix])

                for e_ix in range(4):
                    ffn_chunk(e_ix)
                bc1 = bcasts_45(mrow, rrow, d_r1)
                for e_ix in range(4, NCH):
                    evac_chunk(e_ix - 4)
                    ffn_chunk(e_ix)
                for e_ix in range(NCH - 4, NCH):
                    evac_chunk(e_ix)
                rec.use("sin", "pe", ffn_pe[NCH - 1])
                if i < nt_own:
                    rec.wait("gpsimd", "dve", evac_d[NCH - 1])
                    st = tile_dma_out(x1v, i * T, xb, "sx1", serialize=True)
                    rec.use(xkey, "sx1", st)
                    x1st[i] = st
                cast1_a = []
                for c in range(NCH):
                    rec.wait("scalar", "dve", evac_d[c])
                    rec.gate("scalar", "aux_a")
                    cast1_a.append(rec.act(lambda e, c=c, xb=xb: e.copy(
                        out=chk(aux_a, c), in_=chk(xb, c))))
                rec.use(xkey, "act", cast1_a[-1])
                src2 = [(chk(aux_a, c), cast1_a[c]) for c in range(NCH)]
                sq2 = ln_squares(src2, aux_b, "aux_b")
                pe_st2 = ln_stats_mms(src2, sq2, aux_b, "aux_b")
                rec.use("aux_a", "pe", pe_st2)
                rec.wait("vector", "pe", pe_st2)
                d_r2 = ln_smalls(pe_st2, mul_half_scale=False)
                bc2 = bcasts_45(mrow, rrow, d_r2)
                app_d = [0] * NCH
                for c in range(NCH):
                    rec.wait("vector", "pe", bc2)
                    rec.gate("vector", "aux_b")
                    rec.dve(lambda e, c=c, xb=xb: e.tensor_tensor(
                        out=dvB[:, :], in0=chk(xb, c), in1=p4, op=OP.subtract))
                    app_d[c] = rec.dve(lambda e, c=c: e.tensor_tensor(
                        out=chk(aux_b, c), in0=dvB[:, :], in1=p5, op=OP.mult))
                rec.use("b4", "dve", app_d[-1])
                rec.use("b5", "dve", app_d[-1])
                rec.use(xkey, "dve", app_d[-1])
                rec.wait("gpsimd", "dve", app_d[NCH - 1])
                st2 = tile_dma_out(y2v, i * T, aux_b, "sy2")
                rec.use("aux_b", "sy2", st2)
                y2st[i] = st2

                def arm_sh(par, e, s, i=i):
                    for j in range(4):
                        ssl = aux_b[:, 4 * j * T:(4 * j + 4) * T].rearrange(
                            "p (c t) -> p c t", t=T)
                        dsl = y2shv[:, 4 * j:4 * j + 4,
                                    par * half_cols + i * T:
                                    par * half_cols + (i + 1) * T]
                        e.dma_start(out=dsl, in_=ssl).then_inc(s["sy2s"], 16)
                emit_switch_pool(arm_sh, {"sy2s": 64})
                rec.use("aux_b", "sy2s", rec.n["sy2s"])
            rec.use("arena0", "pe", ffn_pe[NCH - 1])
            rec.merge_use("arena1", "xbuf0", "xbuf1")
            return

        pass_A()

        def pass_BC():
            rec.wait("gpsimd", "sy2", y2st[nt_a - 1])
            rec.wait("gpsimd", "rows", c_salt)

            rec.wait("gpsimd", "sy2s", rec.n["sy2s"])

            def arm_flag1(par, e, s):
                e.dma_start(out=fl_sh[par:par + 1, 0, 0:1],
                            in_=salt_sb[0:1, 0:1]).then_inc(s["sy2s"], 16)
            emit_switch_pool(arm_flag1, {"sy2s": 16})
            wk_cast = load_weight_fp32(arena0, "arena0", wT["wk"], half=True)
            wv_cast = load_weight_fp32(arena1, "arena1", wT["wv"], half=True)
            rdz1 = emit_poll(0, "sy2s", rec.n["sy2s"])
            for b in range(4, 8):
                rec.gate("scalar", bkey[b])
            c_zkv = rec.act(lambda e: e.memzero(p_kv[:, :]))
            rec.wait("tensor", "act", c_zkv)
            kv_last = 0
            for i in range(nt_seq):
                rec.gate("sync", "sin")
                if i < nt_a:
                    rec.wait("sync", "sy2", y2st[i])
                    lds = ("sin", tile_dma_in(y2v, i * T, sin, "sin"))
                else:
                    rec.wait("gpsimd", "rdz", rdz1)
                    rec.gate("gpsimd", "sin")

                    def arm_ld(par, e, s, i=i):
                        sib = 1 - par
                        for j in range(4):
                            ssl = sin[:, 4 * j * T:(4 * j + 4) * T].rearrange(
                                "p (c t) -> p c t", t=T)
                            dsl = y2shv[:, 4 * j:4 * j + 4,
                                        sib * half_cols + (i - nt_a) * T:
                                        sib * half_cols + (i - nt_a + 1) * T]
                            e.dma_start(out=ssl, in_=dsl).then_inc(s["sinp"], 16)
                    emit_switch_pool(arm_ld, {"sinp": 64})
                    lds = ("sinp", rec.n["sinp"])
                proj_pe = {}
                evac_a = {}
                for pj, (arena, wcast, dstbuf, func) in enumerate((
                        (arena0, wk_cast, aux_a, "K"),
                        (arena1, wv_cast, aux_b, "V"))):
                    for s in range(4):
                        for strip in range(2):
                            bank = 2 * pj + (strip % 2)
                            rec.gate("tensor", bkey[bank])
                            rec.wait("tensor", lds[0], lds[1])
                            for k in range(NCH):
                                rec.wait("tensor", "act", wcast[k])
                                pc = rec.pe(lambda e, k=k, s=s, strip=strip, bank=bank,
                                            arena=arena: e.matmul(
                                    psb[bank][:, :],
                                    sin[:, k * T + s * 128:k * T + (s + 1) * 128],
                                    arena[:, k * 1024 + strip * 512:
                                          k * 1024 + (strip + 1) * 512],
                                    start=(k == 0), stop=(k == NCH - 1)),
                                    inc=(k == NCH - 1))
                            proj_pe[(pj, s, strip)] = pc
                            rec.wait("scalar", "pe", pc)
                            rec.gate("scalar", "aux_a" if pj == 0 else "aux_b")
                            dst = dstbuf[:, s * 1024 + strip * 512:
                                         s * 1024 + (strip + 1) * 512]
                            if func == "K":
                                ca = rec.act(lambda e, dst=dst, bank=bank: e.activation(
                                    out=dst, in_=psb[bank][:, :], func=AF.Relu,
                                    scale=ISCALE))
                            else:
                                ca = rec.act(lambda e, dst=dst, bank=bank: e.copy(
                                    out=dst, in_=psb[bank][:, :]))
                            evac_a[(pj, s, strip)] = ca
                            rec.use(bkey[bank], "act", ca)
                rec.use("sin", "pe", pc)
                for s in range(4):
                    for strip in range(2):
                        rec.wait("tensor", "act", evac_a[(0, s, strip)])
                        rec.wait("tensor", "act", evac_a[(1, s, strip)])
                    for h in range(HEADS // 2):
                        kv_last = rec.pe(lambda e, s=s, h=h: e.matmul(
                            p_kv[:, h * 128:(h + 1) * 128],
                            aux_a[:, s * 1024 + h * 128:s * 1024 + (h + 1) * 128],
                            aux_b[:, s * 1024 + h * 128:s * 1024 + (h + 1) * 128],
                            start=False, stop=True, skip_group_check=True),
                            inc=(h == HEADS // 2 - 1))
                rec.use("aux_a", "pe", kv_last)
                rec.use("aux_b", "pe", kv_last)
            rec.use("arena0", "pe", kv_last)
            rec.use("arena1", "pe", kv_last)
            rec.wait("scalar", "pe", kv_last)
            kv_acts = []
            for j in range(2):
                kv_acts.append(rec.act(lambda e, j=j: e.copy(
                    out=kvbf[:, j * 512:(j + 1) * 512], in_=p_kv[:, j * 512:(j + 1) * 512])))
                rec.use(bkey[4 + j], "act", kv_acts[-1])
            rec.wait("gpsimd", "act", kv_acts[-1])

            def arm_kv(par, e, s):
                e.dma_start(out=kv_sh[par, :, :],
                            in_=kvbf[:, 0:1024]).then_inc(s["kvx"], 16)
            emit_switch_pool(arm_kv, {"kvx": 16})
            rec.wait("gpsimd", "kvx", rec.n["kvx"])

            def arm_flag2(par, e, s):
                e.dma_start(out=fl_sh[par:par + 1, 1, 0:1],
                            in_=salt_sb[0:1, 0:1]).then_inc(s["kvx"], 16)
            emit_switch_pool(arm_flag2, {"kvx": 16})
            rec.use("kvbf", "kvx", rec.n["kvx"])
            rdz2 = emit_poll(1, "kvx", rec.n["kvx"])
            rec.wait("gpsimd", "rdz", rdz2)
            rec.gate("gpsimd", "kvbf")
            for j in range(2):
                kvl = rec.dma("gpsimd", "kvx", lambda e, j=j: e.dma_start(
                    out=kvbf[:, j * 1024:(j + 1) * 1024], in_=kv_sh[j, :, :]))
            kv_acts = [("kvx", kvl)]
            if debug:
                rec.gate("gpsimd", "stg")
                for j in range(4):
                    rec.wait("vector", "pe", kv_last)
                    cd = rec.dve(lambda e, j=j: e.tensor_copy(
                        out=stg[:, j * 512:(j + 1) * 512], in_=p_kv[:, j * 512:(j + 1) * 512]))
                rec.wait("gpsimd", "dve", cd)
                c_kvd = rec.dma("gpsimd", "misc", lambda e: e.dma_start(out=kv_dbg[:, :], in_=stg[:, :]))
                rec.use("stg", "misc", c_kvd)
                rec.use("stg", "dve", cd)
            return kv_acts[-1]

        pass_BC()

        def pass_DE():
            wq_cast = load_weight_fp32(arena0, "arena0", wT["wq"])
            wu_cast = load_weight_fp32(arena1, "arena1", wT["wu"])
            rdz2 = emit_poll(1, "kvx", rec.n["kvx"])
            rec.wait("gpsimd", "rdz", rdz2)
            rec.gate("gpsimd", "kvbf")
            for j in range(2):
                kvl = rec.dma("gpsimd", "kvx", lambda e, j=j: e.dma_start(
                    out=kvbf[:, j * 1024:(j + 1) * 1024], in_=kv_sh[j, :, :]))
            kv_ready = ("kvx", kvl)
            last_pe = 0
            ost = [0] * nt_own
            for t in range(nt_own):
                rec.gate("sync", "sin")
                lds = tile_dma_in(y2v, t * T, sin, "sin")
                q_pe = [0] * HEADS
                u_pe = [0] * HEADS
                o_pe = [0] * HEADS
                ss_pe = [0] * HEADS
                bc_pe = [0] * HEADS
                qev = [0] * HEADS
                uev = [0] * HEADS
                sqo = [0] * HEADS
                of_d = [0] * HEADS

                def emit_q(h):
                    bank = h % 2
                    rec.gate("tensor", bkey[bank])
                    rec.wait("tensor", "sin", lds)
                    for k in range(NCH):
                        rec.wait("tensor", "act", wq_cast[k])
                        q_pe[h] = rec.pe(lambda e, k=k, h=h, bank=bank: e.matmul(
                            psb[bank][:, :], wsl(arena0, k, h * 128, 128),
                            chk(sin, k), start=(k == 0), stop=(k == NCH - 1)),
                            inc=(k == NCH - 1))
                    rec.wait("scalar", "pe", q_pe[h])
                    rec.gate("scalar", "qh%d" % (h % 2))
                    qev[h] = rec.act(lambda e, h=h, bank=bank: e.activation(
                        out=chk(qh, h % 2), in_=psb[bank][:, :], func=AF.Relu,
                        scale=ISCALE))
                    rec.use(bkey[bank], "act", qev[h])

                def emit_u(h):
                    bank = 2 + h % 2
                    rec.gate("tensor", bkey[bank])
                    rec.wait("tensor", "sin", lds)
                    for k in range(NCH):
                        rec.wait("tensor", "act", wu_cast[k])
                        u_pe[h] = rec.pe(lambda e, k=k, h=h, bank=bank: e.matmul(
                            psb[bank][:, :], wsl(arena1, k, h * 128, 128),
                            chk(sin, k), start=(k == 0), stop=(k == NCH - 1)),
                            inc=(k == NCH - 1))
                    rec.wait("scalar", "pe", u_pe[h])
                    rec.gate("scalar", "uh%d" % (h % 2))
                    uev[h] = rec.act(lambda e, h=h, bank=bank: e.activation(
                        out=chk(uh, h % 2), in_=psb[bank][:, :], func=AF.Silu))
                    rec.use(bkey[bank], "act", uev[h])

                def emit_o(h):
                    pb = p4 if h % 2 == 0 else p5
                    bk = "b4" if h % 2 == 0 else "b5"
                    rec.wait("tensor", "act", qev[h])
                    rec.wait("tensor", kv_ready[0], kv_ready[1])
                    rec.gate("tensor", bk)
                    o_pe[h] = rec.pe(lambda e, h=h, pb=pb: e.matmul(
                        pb, kvbf[:, h * 128:(h + 1) * 128], chk(qh, h % 2),
                        start=True, stop=True), inc=True)
                    rec.use("qh%d" % (h % 2), "pe", o_pe[h])
                    rec.wait("scalar", "pe", o_pe[h])
                    rec.gate("scalar", "sq%d" % (h % 2))
                    sqo[h] = rec.act(lambda e, h=h, pb=pb: e.activation(
                        out=chk(sqb, h % 2), in_=pb, func=AF.Square))

                def emit_ss(h):
                    rec.wait("tensor", "act", sqo[h])
                    rec.gate("tensor", "b6")
                    ss_pe[h] = rec.pe(lambda e, h=h: e.matmul(
                        p6[0:1, :], ones_one, chk(sqb, h % 2),
                        start=True, stop=True), inc=True)
                    rec.use("sq%d" % (h % 2), "pe", ss_pe[h])
                    rec.wait("scalar", "pe", ss_pe[h])
                    rec.gate("scalar", "rows")
                    a_a = rec.act(lambda e: e.activation(
                        out=srow, in_=p6[0:1, :], func=AF.Sqrt, scale=1.0 / 128.0))
                    rec.use("b6", "act", a_a)
                    rec.wait("vector", "act", a_a)
                    rec.gate("vector", "rows")
                    rec.dve(lambda e: e.tensor_scalar(
                        out=trow, in0=srow, scalar1=1e-12, scalar2=None, op0=OP.max))
                    return rec.dve(lambda e: e.reciprocal(out=rrow, in_=trow))

                def emit_bc(h, d_recip):
                    rec.wait("tensor", "dve", d_recip)
                    rec.gate("tensor", "b7")
                    bc_pe[h] = rec.pe(lambda e: e.matmul(
                        p7, onesrow, rrow, start=True, stop=True), inc=True)
                    rec.use("rows", "pe", bc_pe[h])

                def emit_gate(h):
                    pb = p4 if h % 2 == 0 else p5
                    bk = "b4" if h % 2 == 0 else "b5"
                    g1buf = dvA[:, :] if h % 2 == 0 else dvB[:, :]
                    rec.wait("vector", "pe", o_pe[h])
                    rec.wait("vector", "act", uev[h])
                    rec.dve(lambda e, pb=pb, g1buf=g1buf, h=h: e.tensor_tensor(
                        out=g1buf, in0=pb, in1=chk(uh, h % 2), op=OP.mult))
                    rec.use("uh%d" % (h % 2), "dve", rec.n["dve"])
                    rec.wait("vector", "pe", bc_pe[h])
                    of_d[h] = rec.dve(lambda e, h=h, g1buf=g1buf: e.tensor_tensor(
                        out=chk(aux_a, h), in0=g1buf, in1=p7, op=OP.mult))
                    rec.use(bk, "dve", of_d[h])
                    rec.use("b7", "dve", of_d[h])

                d_recip = [0] * HEADS
                for h in range(HEADS):
                    emit_q(h)
                    if h >= 2:
                        emit_bc(h - 2, d_recip[h - 2])
                        emit_gate(h - 2)
                    if h >= 1:
                        d_recip[h - 1] = emit_ss(h - 1)
                    if h == 0:
                        rec.gate("vector", "aux_a")
                    emit_u(h)
                    emit_o(h)
                emit_bc(HEADS - 2, d_recip[HEADS - 2])
                emit_gate(HEADS - 2)
                d_recip[HEADS - 1] = emit_ss(HEADS - 1)
                emit_bc(HEADS - 1, d_recip[HEADS - 1])
                emit_gate(HEADS - 1)
                rec.wait("gpsimd", "dve", of_d[HEADS - 1])
                st = tile_dma_out(ov, t * T, aux_a, "so")
                rec.use("aux_a", "so", st)
                ost[t] = st
                rec.use("sin", "pe", max(q_pe[HEADS - 1], u_pe[HEADS - 1]))
                last_pe = max(q_pe[HEADS - 1], u_pe[HEADS - 1])
            rec.use("arena0", "pe", last_pe)
            rec.use("arena1", "pe", last_pe)
            rec.merge_use("xbuf0", "arena1")
            rec.merge_use("xbuf1", "arena1")
            return ost

        ost = pass_DE()

        x2st = [0] * nt_own
        xb2st = [0] * nt_own
        rowst = [0] * nt_own

        def pass_F():
            wo_cast = load_weight_fp32(arena0, "arena0", wT["wo"])
            for t in range(nt_own):
                xb = xbuf[t % 2]
                xkey = "xbuf%d" % (t % 2)
                lxs = "lx%d" % (t % 2)
                rec.gate("sync", "sin")
                rec.wait("sync", "so", ost[nt_own - 1])
                lds = tile_dma_in(ov, t * T, sin, "sin")
                rec.gate("sync", xkey)
                rec.wait("sync", "sx1", x1st[nt_own - 1])
                lds_x = tile_dma_in(x1v, t * T, xb, lxs)
                at_pe = [0] * NCH
                x2_d = [0] * NCH

                def attn_chunk(e_ix):
                    bank = e_ix % 4
                    rec.gate("tensor", bkey[bank])
                    rec.wait("tensor", "sin", lds)
                    for k in range(NCH):
                        rec.wait("tensor", "act", wo_cast[k])
                        at_pe[e_ix] = rec.pe(lambda e, k=k, e_ix=e_ix, bank=bank:
                                             e.matmul(
                            psb[bank][:, :], wsl(arena0, k, e_ix * 128, 128),
                            chk(sin, k), start=(k == 0), stop=(k == NCH - 1)),
                            inc=(k == NCH - 1))

                def x2_chunk(e_ix):
                    bank = e_ix % 4
                    rec.wait("vector", "pe", at_pe[e_ix])
                    rec.wait("vector", lxs, lds_x)
                    x2_d[e_ix] = rec.dve(lambda e, e_ix=e_ix, bank=bank, xb=xb: e.tensor_tensor(
                        out=chk(xb, e_ix), in0=psb[bank][:, :], in1=chk(xb, e_ix),
                        op=OP.add))
                    rec.use(bkey[bank], "dve", x2_d[e_ix])

                for e_ix in range(4):
                    attn_chunk(e_ix)
                for e_ix in range(4, NCH):
                    x2_chunk(e_ix - 4)
                    attn_chunk(e_ix)
                for e_ix in range(NCH - 4, NCH):
                    x2_chunk(e_ix)
                rec.use("sin", "pe", at_pe[NCH - 1])
                rec.wait("gpsimd", "dve", x2_d[NCH - 1])
                stx = tile_dma_out(x2v, t * T, xb, "sx2", serialize=True)
                cast2_a = []
                for c in range(NCH):
                    rec.wait("scalar", "dve", x2_d[c])
                    rec.gate("scalar", "aux_a")
                    cast2_a.append(rec.act(lambda e, c=c, xb=xb: e.copy(
                        out=chk(aux_a, c), in_=chk(xb, c))))
                rec.wait("gpsimd", "act", cast2_a[NCH - 1])
                stb = tile_dma_out(xb2v, t * T, aux_a, "sxb2")
                rec.use("aux_a", "sxb2", stb)
                rec.use(xkey, "sx2", stx)
                rec.use(xkey, "act", cast2_a[NCH - 1])
                x2st[t] = stx
                xb2st[t] = stb
            rec.use("arena0", "pe", at_pe[NCH - 1])

        pass_F()

        def pass_G():
            wt_pool = load_weight_bf16(arena0, "arena0", wt_s)
            out_last = 0
            for t in range(nt_own):
                xb = xbuf[t % 2]
                xkey = "xbuf%d" % (t % 2)
                lxs = "lx%d" % (t % 2)
                rec.gate("sync", "sin")
                rec.wait("sync", "sxb2", xb2st[nt_own - 1])
                lds = tile_dma_in(xb2v, t * T, sin, "sin")
                rec.gate("sync", xkey)
                rec.wait("sync", "sx2", x2st[nt_own - 1])
                lds_x = tile_dma_in(x2v, t * T, xb, lxs)
                srcg = [(chk(sin, c), lds) for c in range(NCH)]
                for c in range(NCH):
                    rec.wait("scalar", "sin", lds)
                sqg = ln_squares(srcg, aux_a, "aux_a")
                rec.wait("tensor", "sin", lds)
                pe_stg = ln_stats_mms([(chk(sin, c), None) for c in range(NCH)],
                                      sqg, aux_a, "aux_a")
                rec.use("sin", "pe", pe_stg)
                d_r3 = ln_smalls(pe_stg, mul_half_scale=True)
                bc3 = bcasts_45(mrow, rrow, d_r3)
                ffn_pe = [0] * NCH
                out_dv = [0] * NCH

                def ffn2_chunk(e_ix):
                    bank = e_ix % 4
                    rec.gate("tensor", bkey[bank])
                    rec.wait("tensor", "wt", wt_pool)
                    rec.wait("tensor", "sin", lds)
                    for k in range(NCH):
                        ffn_pe[e_ix] = rec.pe(lambda e, k=k, e_ix=e_ix, bank=bank:
                                              e.matmul(
                            psb[bank][:, :], wsl(arena0, k, e_ix * 128, 128),
                            chk(sin, k), start=(k == 0), stop=(k == NCH - 1)),
                            inc=(k == NCH - 1))

                def oev_chunk(e_ix):
                    bank = e_ix % 4
                    rec.wait("vector", "pe", ffn_pe[e_ix])
                    rec.wait("vector", "pe", bc3)
                    rec.wait("vector", lxs, lds_x)
                    rec.dve(lambda e, e_ix=e_ix: e.tensor_scalar(
                        out=dvA[:, :], in0=p4, scalar1=c_wt[:, e_ix:e_ix + 1],
                        scalar2=None, op0=OP.mult))
                    rec.dve(lambda e, bank=bank: e.tensor_tensor(
                        out=dvB[:, :], in0=psb[bank][:, :], in1=dvA[:, :], op=OP.subtract))
                    rec.dve(lambda e: e.tensor_tensor(
                        out=dvA[:, :], in0=dvB[:, :], in1=p5, op=OP.mult))
                    out_dv[e_ix] = rec.dve(lambda e, e_ix=e_ix, xb=xb: e.scalar_tensor_tensor(
                        out=chk(xb, e_ix), in0=dvA[:, :], scalar=hb[:, e_ix:e_ix + 1],
                        in1=chk(xb, e_ix), op0=OP.add, op1=OP.add))
                    rec.use(bkey[bank], "dve", out_dv[e_ix])
                    rec.use("b4", "dve", out_dv[e_ix])
                    rec.use("b5", "dve", out_dv[e_ix])

                for e_ix in range(4):
                    ffn2_chunk(e_ix)
                for e_ix in range(4, NCH):
                    oev_chunk(e_ix - 4)
                    ffn2_chunk(e_ix)
                for e_ix in range(NCH - 4, NCH):
                    oev_chunk(e_ix)
                rec.use("sin", "pe", ffn_pe[NCH - 1])
                rec.wait("gpsimd", "dve", out_dv[NCH - 1])
                st = tile_dma_out(outv, t * T, xb, "sout")
                rec.use(xkey, "sout", st)
                out_last = st
            rec.raw("gpsimd", lambda e, s, v=out_last: e.wait_ge(s["sout"], v))

        pass_G()

        @block.sync
        def _(eng):
            for fn in rec.streams["sync"]:
                fn(eng, sems)

        @block.scalar
        def _(eng):
            for fn in rec.streams["scalar"]:
                fn(eng, sems)

        @block.vector
        def _(eng):
            for fn in rec.streams["vector"]:
                fn(eng, sems)

        @block.tensor
        def _(eng):
            for fn in rec.streams["tensor"]:
                fn(eng, sems)

        @block.gpsimd
        def _(eng):
            for fn in rec.streams["gpsimd"]:
                fn(eng, sems)

    return nc


_NC_CACHE = {}


def _get_nc(debug=False):
    if debug not in _NC_CACHE:
        _NC_CACHE[debug] = build_nc(debug)
    return _NC_CACHE[debug]


def _consts_array():
    c = np.zeros((128, 144), np.float32)
    c[:, 0] = 1.0
    c[:, 1] = 0.5
    c[:, 2] = 1.0 / 2048.0
    c[:, 4:132] = 1.0
    for i, v in enumerate((EPS_LN, ISCALE, 1.0 / 128.0, 0.0, 1.0,
                           1e-12, 0.5, 1.0 / NW, EPS_T)):
        c[:, 132 + i] = v
    return c


def kernel(L0_attn_x, L0_attn_wq, L0_attn_wk, L0_attn_wv, L0_attn_wu,
           L0_attn_wo, L0_ffn_weight, L0_ffn_bias, _debug=False):
    x = np.asarray(L0_attn_x, dtype=np.float32)
    x_fm = np.ascontiguousarray(x.reshape(2 * SEQ, D).T)
    wkT_full = np.asarray(L0_attn_wk, np.float32).T
    wvT_full = np.asarray(L0_attn_wv, np.float32).T
    wts = {
        "wqT": np.ascontiguousarray(np.asarray(L0_attn_wq, np.float32).T),
        "wuT": np.ascontiguousarray(np.asarray(L0_attn_wu, np.float32).T),
        "woT": np.ascontiguousarray(np.asarray(L0_attn_wo, np.float32).T),
        "wfT": np.ascontiguousarray(np.asarray(L0_ffn_weight, np.float32).T),
    }
    bias = np.ascontiguousarray(np.asarray(L0_ffn_bias, np.float32).reshape(16, 128).T)
    consts = _consts_array()

    salt = np.random.default_rng().integers(1, 2 ** 31, size=(1, 4)).astype(np.uint32)
    in_maps = []
    for c in range(8):
        sq = c // 4
        q = c % 4
        par = c % 2
        base = sq * SEQ
        own = x_fm[:, base + q * QTOK: base + (q + 1) * QTOK]
        fq = (q + 2) % 4
        foreign = x_fm[:, base + fq * QTOK: base + (fq + 1) * QTOK]
        xc = np.ascontiguousarray(np.concatenate([own, foreign], axis=1))
        m = {"x": xc, "bias": bias, "consts_in": consts, "salt": salt}
        m["wkT"] = np.ascontiguousarray(wkT_full[:, par * 1024:(par + 1) * 1024])
        m["wvT"] = np.ascontiguousarray(wvT_full[:, par * 1024:(par + 1) * 1024])
        m.update(wts)
        in_maps.append(m)

    nc = _get_nc(_debug)
    res = run_bass_kernel_spmd(nc, in_maps, core_ids=list(range(8)))
    shards = [res.results[c]["out"] for c in range(8)]
    full = np.concatenate(shards, axis=1)
    out = np.ascontiguousarray(full.T).reshape(2, SEQ, D)
    if _debug:
        return out, res
    return out


# version 11 (final)
# speedup vs baseline: 1.8424x; 1.0026x over previous
import contextlib

import numpy as np

import concourse.bass as bass
import concourse.mybir as mybir
from concourse.bass_utils import run_bass_kernel_spmd

F32 = mybir.dt.float32
BF16 = mybir.dt.bfloat16
AF = mybir.ActivationFunctionType
OP = mybir.AluOpType
AX = mybir.AxisListType.X

D = 2048
NCH = 16
T = 512
SEQ = 16384
QTOK = 4096
NT_SEQ = SEQ // T
NT_OWN = QTOK // T
HEADS = 16
ISCALE = 1.0 / float(np.sqrt(128.0))
EPS_LN = 1e-5
EPS_T = 1e-10
NW = float(D * D)

ENGS = ("sync", "scalar", "vector", "tensor", "gpsimd")


class Rec:

    def __init__(self):
        self.streams = {e: [] for e in ENGS}
        self.n = {k: 0 for k in ("lx0", "lx1", "sin", "rows", "w0", "w1",
                                 "wt", "misc", "sy2", "sx1", "so", "sx2",
                                 "sxb2", "srows", "sout",
                                 "pck", "pcv", "pcq", "pcu", "pco",
                                 "sy2s", "kvx", "rdz", "sinp",
                                 "act", "dve", "pe")}
        self.last_wait = {}
        self.buf = {}

    def wait(self, eng, sem, val):
        if not val or val <= 0:
            return
        assert val <= self.n[sem], (eng, sem, val, self.n[sem])
        if self.last_wait.get((eng, sem), 0) >= val:
            return
        self.last_wait[(eng, sem)] = val

        def fn(e, s, sem=sem, val=val):
            e.wait_ge(s[sem], val)
        self.streams[eng].append(fn)

    def raw(self, eng, fn):
        self.streams[eng].append(fn)

    def dma(self, eng, sem, fn):
        self.n[sem] += 16

        def run(e, s, fn=fn, sem=sem):
            fn(e).then_inc(s[sem], 16)
        self.streams[eng].append(run)
        return self.n[sem]

    def act(self, fn):
        self.wait("scalar", "act", self.n["act"])
        self.n["act"] += 1

        def run(e, s, fn=fn):
            fn(e).then_inc(s["act"], 1)
        self.streams["scalar"].append(run)
        return self.n["act"]

    def dve(self, fn):
        self.wait("vector", "dve", self.n["dve"])
        self.n["dve"] += 1

        def run(e, s, fn=fn):
            fn(e).then_inc(s["dve"], 1)
        self.streams["vector"].append(run)
        return self.n["dve"]

    def pe(self, fn, inc=False):
        if inc:
            self.n["pe"] += 1

            def run(e, s, fn=fn):
                fn(e).then_inc(s["pe"], 1)
        else:
            def run(e, s, fn=fn):
                fn(e)
        self.streams["tensor"].append(run)
        return self.n["pe"]

    def use(self, key, sem, cnt):
        d = self.buf.setdefault(key, {})
        d[sem] = max(d.get(sem, 0), cnt)

    def gate(self, eng, key):
        for sem, cnt in sorted(self.buf.get(key, {}).items()):
            self.wait(eng, sem, cnt)

    def merge_use(self, dst, *srcs):
        for s in srcs:
            for sem, cnt in self.buf.get(s, {}).items():
                self.use(dst, sem, cnt)


def build_nc(debug=False, seq=SEQ, qtok=QTOK, race_check=False, no_poll=False):
    nt_seq = seq // T
    nt_own = qtok // T
    nt_a = 2 * nt_own
    half_cols = seq // 2
    nc = bass.Bass("TRN2", target_bir_lowering=False,
                   detect_race_conditions=race_check)

    x_in = nc.dram_tensor("x", [D, 2 * qtok], F32, kind="ExternalInput").ap()
    wT = {}
    for nm in ("wq", "wu", "wo", "wf"):
        wT[nm] = nc.dram_tensor(nm + "T", [D, D], F32, kind="ExternalInput").ap()
    for nm in ("wk", "wv"):
        wT[nm] = nc.dram_tensor(nm + "T", [D, D // 2], F32, kind="ExternalInput").ap()
    salt_in = nc.dram_tensor("salt", [1, 4], mybir.dt.uint32, kind="ExternalInput").ap()
    bias_in = nc.dram_tensor("bias", [128, 16], F32, kind="ExternalInput").ap()
    consts_in = nc.dram_tensor("consts_in", [128, 144], F32, kind="ExternalInput").ap()
    out_d = nc.dram_tensor("out", [D, qtok], F32, kind="ExternalOutput").ap()

    skind = "ExternalOutput" if debug else "Internal"
    y2_s = nc.dram_tensor("y2_s", [D, 2 * qtok], BF16, kind=skind).ap()
    y2_sh = nc.dram_tensor("y2_sh", [D, seq], BF16, addr_space="Shared").ap()
    kv_sh = nc.dram_tensor("kv_sh", [2, 128, D // 2], BF16, addr_space="Shared").ap()
    fl_sh = nc.dram_tensor("fl_sh", [2, 2, 16], mybir.dt.uint32, addr_space="Shared").ap()
    x1_s = nc.dram_tensor("x1_s", [D, qtok], F32, kind=skind).ap()
    o_s = nc.dram_tensor("o_s", [D, qtok], BF16, kind=skind).ap()
    x2_s = nc.dram_tensor("x2_s", [D, qtok], F32, kind=skind).ap()
    xb2_s = nc.dram_tensor("xb2_s", [D, qtok], BF16, kind=skind).ap()
    wt_s = nc.dram_tensor("wt_s", [D, D], BF16, kind=skind).ap()
    w_bf = {}
    for nm in ("wq", "wu", "wo"):
        w_bf[nm] = nc.dram_tensor(nm + "_bf", [D, D], BF16, kind="Internal").ap()
    for nm in ("wk", "wv"):
        w_bf[nm] = nc.dram_tensor(nm + "_bf", [D, D // 2], BF16, kind="Internal").ap()
    m3_s = nc.dram_tensor("m3_s", [1, qtok], F32, kind=skind).ap()
    r3_s = nc.dram_tensor("r3_s", [1, qtok], F32, kind=skind).ap()
    dbg_fl = nc.dram_tensor("dbg_fl", [1, 16], mybir.dt.uint32, kind="ExternalOutput").ap()
    if debug:
        kv_dbg = nc.dram_tensor("kv_dbg", [128, D], F32, kind="ExternalOutput").ap()

    rec = Rec()

    with contextlib.ExitStack() as stack:
        E = stack.enter_context
        arena0 = E(nc.sbuf_tensor("arena0", [128, NCH * D], BF16))
        arena1 = E(nc.sbuf_tensor("arena1", [128, NCH * D], BF16))
        sin = E(nc.sbuf_tensor("sin", [128, NCH * T], BF16))
        aux_a = E(nc.sbuf_tensor("aux_a", [128, NCH * T], BF16))
        aux_b = E(nc.sbuf_tensor("aux_b", [128, NCH * T], BF16))
        stg = E(nc.sbuf_tensor("stg", [128, D], F32))
        sqb = E(nc.sbuf_tensor("sqb", [128, 2 * T], BF16))
        wstg = E(nc.sbuf_tensor("wstg", [128, 1024], BF16))
        kvbf = E(nc.sbuf_tensor("kvbf", [128, D], BF16))
        dvA = E(nc.sbuf_tensor("dvA", [128, T], F32))
        dvB = E(nc.sbuf_tensor("dvB", [128, T], F32))
        qh = E(nc.sbuf_tensor("qh", [128, 2 * T], BF16))
        uh = E(nc.sbuf_tensor("uh", [128, 2 * T], F32))
        consts = E(nc.sbuf_tensor("consts", [128, 144], F32))
        consts2 = E(nc.sbuf_tensor("consts2", [128, 4], F32))
        onesb = E(nc.sbuf_tensor("onesb", [128, 2], BF16))
        c_wt = E(nc.sbuf_tensor("c_wt", [128, 16], F32))
        hb = E(nc.sbuf_tensor("hb", [128, 16], F32))
        rows1 = E(nc.sbuf_tensor("rows1", [65, T], F32))
        rows2 = E(nc.sbuf_tensor("rows2", [65, T], F32))
        salt_sb = E(nc.sbuf_tensor("salt_sb", [1, 4], mybir.dt.uint32))
        flag_rd = E(nc.sbuf_tensor("flag_rd", [1, 4], mybir.dt.uint32))

        psb = [E(nc.psum_tensor("p%d" % i, [128, 512], F32)) for i in range(4)]
        p_kv = E(nc.psum_tensor("p_kv", [128, 2048], F32))

        sems = {}
        for s in ("lx0", "lx1", "sin", "rows", "w0", "w1", "wt", "misc",
                  "sy2", "sx1", "so", "sx2", "sxb2", "srows", "sout",
                  "pck", "pcv", "pcq", "pcu", "pco",
                  "sy2s", "kvx", "rdz", "sinp",
                  "act", "dve", "pe"):
            sems[s] = E(nc.semaphore("s_" + s))
        sems["pol"] = E(nc.semaphore("s_pol"))
        block = E(nc.Block())

        a1f = arena1.ap().bitcast(F32)
        xbuf = [a1f[:, 0:NCH * T], a1f[:, NCH * T:2 * NCH * T]]
        p4 = p_kv[:, 0:512]
        p5 = p_kv[:, 512:1024]
        p6 = p_kv[:, 1024:1536]
        p7 = p_kv[:, 1536:2048]
        bkey = ["b0", "b1", "b2", "b3", "b4", "b5", "b6", "b7"]

        def wsl(arena, k, j0, w):
            return arena[:, k * D + j0:k * D + j0 + w]

        def chk(buf, c, w=T):
            return buf[:, c * w:(c + 1) * w]

        onescol = consts[:, 0:1]
        half11 = consts[0:1, 1:2]
        inv2048 = consts[:, 2:3]
        onesrow = consts[0:1, 4:132]
        thr_col = consts2[:, 0:1]
        half_scale = consts2[0:1, 1:2]
        mean_w = consts2[0:1, 2:3]
        acc_col = consts2[:, 3:4]
        ones_mean = onesb[:, 0:1]
        ones_one = onesb[:, 1:2]

        for _cix, _cval in enumerate((EPS_LN, ISCALE, 1.0 / 128.0, 0.0, 1.0,
                                      1e-12, 0.5, 1.0 / NW, EPS_T)):
            nc.const_aps.register(F32, _cval, consts[:, 132 + _cix:133 + _cix])

        mrow = rows1[0:1, :]
        trow = rows1[32:33, :]
        srow = rows1[64:65, :]
        rrow = rows2[0:1, :]
        r1row = rows2[32:33, :]

        xv = x_in.rearrange("(c p) t -> p c t", p=128)
        y2v = y2_s.rearrange("(c p) t -> p c t", p=128)
        y2shv = y2_sh.rearrange("(c p) t -> p c t", p=128)
        x1v = x1_s.rearrange("(c p) t -> p c t", p=128)
        ov = o_s.rearrange("(c p) t -> p c t", p=128)
        x2v = x2_s.rearrange("(c p) t -> p c t", p=128)
        xb2v = xb2_s.rearrange("(c p) t -> p c t", p=128)
        outv = out_d.rearrange("(c p) t -> p c t", p=128)

        def tile_dma_in(dramv, col0, sbuf_buf, sem):
            for j in range(4):
                dsl = dramv[:, 4 * j:4 * j + 4, col0:col0 + T]
                ssl = sbuf_buf[:, 4 * j * T:(4 * j + 4) * T].rearrange(
                    "p (c t) -> p c t", t=T)
                end = rec.dma("sync", sem,
                              lambda e, a=ssl, b=dsl: e.dma_start(out=a, in_=b))
            return end

        def tile_dma_out(dramv, col0, sbuf_buf, sem, serialize=False):
            if serialize:
                rec.wait("gpsimd", sem, rec.n[sem])
            for j in range(4):
                dsl = dramv[:, 4 * j:4 * j + 4, col0:col0 + T]
                ssl = sbuf_buf[:, 4 * j * T:(4 * j + 4) * T].rearrange(
                    "p (c t) -> p c t", t=T)
                end = rec.dma("gpsimd", sem,
                              lambda e, a=dsl, b=ssl: e.dma_start(out=a, in_=b))
            return end

        c_ld = rec.dma("sync", "rows", lambda e: e.dma_start(out=consts[:, :], in_=consts_in[:, :]))
        rec.dma("sync", "rows", lambda e: e.dma_start(out=salt_sb[0:1, :], in_=salt_in[:, :]))
        c_salt = rec.n["rows"]
        rec.wait("scalar", "rows", c_salt)
        c_fz = rec.act(lambda e: e.memzero(flag_rd[0:1, :]))
        rec.act(lambda e: e.memzero(wstg[:, :]))
        rec.act(lambda e: e.copy(out=ones_mean, in_=inv2048))
        c_ones = rec.act(lambda e: e.copy(out=ones_one, in_=onescol))

        red_dve = []
        for hix in range(2 * NCH):
            k, half = divmod(hix, 2)
            sl = stg[:, half * 1024:(half + 1) * 1024]
            src = wT["wf"][k * 128:(k + 1) * 128, half * 1024:(half + 1) * 1024]
            if hix >= 2:
                rec.wait("gpsimd", "dve", red_dve[hix - 2])
            c = rec.dma("gpsimd", "w%d" % (hix % 2),
                        lambda e, a=sl, b=src: e.dma_start(out=a, in_=b))
            rec.wait("vector", "w%d" % (hix % 2), c)
            if hix == 0:
                rec.dve(lambda e, a=sl: e.tensor_reduce(
                    out=acc_col, in_=a, axis=AX, op=OP.add, apply_absolute_value=True))
            else:
                rec.dve(lambda e, a=sl: e.tensor_reduce(
                    out=dvA[:, 0:1], in_=a, axis=AX, op=OP.add, apply_absolute_value=True))
                rec.dve(lambda e: e.tensor_tensor(
                    out=acc_col, in0=acc_col, in1=dvA[:, 0:1], op=OP.add))
            red_dve.append(rec.n["dve"])
        rec.wait("tensor", "dve", red_dve[-1])
        rec.wait("tensor", "rows", c_salt)
        c_tot = rec.pe(lambda e: e.matmul(psb[0][0:1, 0:1], acc_col, onescol,
                                          start=True, stop=True), inc=True)
        rec.wait("vector", "pe", c_tot)
        rec.dve(lambda e: e.tensor_scalar(out=mean_w, in0=psb[0][0:1, 0:1],
                                          scalar1=1.0 / NW, scalar2=None, op0=OP.mult))
        rec.dve(lambda e: e.tensor_scalar(out=rows1[0:1, 0:1], in0=mean_w,
                                          scalar1=EPS_T, scalar2=0.5,
                                          op0=OP.add, op1=OP.mult))
        c_hs = rec.dve(lambda e: e.tensor_scalar(out=half_scale, in0=mean_w,
                                                 scalar1=0.5, scalar2=None, op0=OP.mult))
        rec.wait("tensor", "dve", c_hs)
        c_thrb = rec.pe(lambda e: e.matmul(psb[1][:, 0:1], onesrow, rows1[0:1, 0:1],
                                           start=True, stop=True), inc=True)
        rec.wait("scalar", "pe", c_thrb)
        c_thr = rec.act(lambda e: e.copy(out=thr_col, in_=psb[1][:, 0:1]))

        rec.wait("gpsimd", "dve", red_dve[-1])
        tern_act = []
        tern_dve = []
        for hix in range(2 * NCH):
            k, half = divmod(hix, 2)
            sl = stg[:, half * 1024:(half + 1) * 1024]
            src = wT["wf"][k * 128:(k + 1) * 128, half * 1024:(half + 1) * 1024]
            if hix >= 2:
                rec.wait("gpsimd", "act", tern_act[hix - 2])
                rec.wait("gpsimd", "dve", tern_dve[hix - 2])
            c = rec.dma("gpsimd", "w%d" % (hix % 2),
                        lambda e, a=sl, b=src: e.dma_start(out=a, in_=b))
            rec.wait("vector", "w%d" % (hix % 2), c)
            rec.wait("vector", "act", c_thr)
            rec.wait("scalar", "w%d" % (hix % 2), c)
            for qx in range(2):
                q = sl[:, qx * 512:(qx + 1) * 512]
                dst = arena0[:, k * D + half * 1024 + qx * 512:
                             k * D + half * 1024 + (qx + 1) * 512]
                c_abs = rec.act(lambda e, q=q: e.activation(
                    out=dvA[:, :], in_=q, func=AF.Abs))
                rec.wait("vector", "act", c_abs)
                rec.dve(lambda e: e.tensor_scalar(
                    out=dvA[:, :], in0=dvA[:, :], scalar1=thr_col, scalar2=None, op0=OP.is_gt))
                c_s = rec.act(lambda e, q=q: e.sign(out=dvB[:, :], in_=q))
                rec.wait("vector", "act", c_s)
                c_w = rec.dve(lambda e: e.tensor_tensor(
                    out=dvB[:, :], in0=dvA[:, :], in1=dvB[:, :], op=OP.mult))
                rec.wait("scalar", "dve", c_w)
                c_cp = rec.act(lambda e, dst=dst: e.copy(out=dst, in_=dvB[:, :]))
            tern_act.append(c_cp)
            tern_dve.append(c_w)
        c_tern = tern_act[-1]
        rec.use("stg", "act", c_tern)
        rec.use("stg", "dve", tern_dve[-1])

        rec.wait("gpsimd", "act", c_tern)
        for k in range(NCH):
            sl = arena0[:, k * D:(k + 1) * D]
            dst = wt_s[k * 128:(k + 1) * 128, :]
            rec.dma("gpsimd", "wt", lambda e, a=dst, b=sl: e.dma_start(out=a, in_=b))
        wt_store_total = rec.n["wt"]
        rec.use("arena0", "wt", wt_store_total)

        rec.wait("tensor", "act", max(c_tern, c_ones))
        for j in range(NCH):
            bank = j % 4
            rec.gate("tensor", bkey[bank])
            for k in range(NCH):
                c_cwt_pe = rec.pe(lambda e, j=j, k=k, bank=bank: e.matmul(
                    psb[bank][:, 0:1], wsl(arena0, k, j * 128, 128), ones_one,
                    start=(k == 0), stop=(k == NCH - 1)),
                    inc=(k == NCH - 1))
            rec.wait("scalar", "pe", c_cwt_pe)
            c_cwt = rec.act(lambda e, j=j, bank=bank: e.copy(
                out=c_wt[:, j:j + 1], in_=psb[bank][:, 0:1]))
            rec.use(bkey[bank], "act", c_cwt)

        c_brow = rec.dma("gpsimd", "misc",
                         lambda e: e.dma_start(out=hb[:, :], in_=bias_in[:, :]))
        rec.wait("scalar", "misc", c_brow)
        c_hb = rec.act(lambda e: e.activation(out=hb[:, :], in_=hb[:, :],
                                              func=AF.Copy, scale=0.5))

        def ln_smalls(pe_stats, mul_half_scale):
            rec.wait("scalar", "pe", pe_stats)
            rec.gate("scalar", "rows")
            a_m = rec.act(lambda e: e.copy(out=mrow, in_=psb[2][0:1, :]))
            rec.use("b2", "act", a_m)
            rec.wait("vector", "act", a_m)
            rec.gate("vector", "rows")
            rec.dve(lambda e: e.tensor_tensor(out=trow, in0=mrow, in1=mrow, op=OP.mult))
            d_var = rec.dve(lambda e: e.tensor_tensor(
                out=trow, in0=psb[3][0:1, :], in1=trow, op=OP.subtract))
            rec.use("b3", "dve", d_var)
            rec.wait("scalar", "dve", d_var)
            a_s = rec.act(lambda e: e.activation(out=srow, in_=trow, func=AF.Sqrt,
                                                 bias=EPS_LN))
            rec.wait("vector", "act", a_s)
            rec.dve(lambda e: e.reciprocal(out=r1row, in_=srow))
            if mul_half_scale:
                d_r = rec.dve(lambda e: e.tensor_scalar(
                    out=rrow, in0=r1row, scalar1=half_scale, scalar2=None, op0=OP.mult))
            else:
                d_r = rec.dve(lambda e: e.tensor_copy(out=rrow, in_=r1row))
            return d_r

        def ln_squares(src_acts, dst_buf, dst_key):
            sq_acts = []
            rec.gate("scalar", dst_key)
            for c, (ap_, cnt) in enumerate(src_acts):
                sq_acts.append(rec.act(lambda e, ap_=ap_, c=c: e.activation(
                    out=chk(dst_buf, c), in_=ap_, func=AF.Square)))
            return sq_acts

        def ln_stats_mms(src_acts, sq_acts, sq_buf, sq_key):
            rec.gate("tensor", "b2")
            rec.gate("tensor", "b3")
            for c, (ap_, cnt) in enumerate(src_acts):
                if cnt is not None:
                    rec.wait("tensor", "act", cnt)
                rec.pe(lambda e, ap_=ap_, c=c: e.matmul(
                    psb[2][0:1, :], ones_mean, ap_,
                    start=(c == 0), stop=(c == NCH - 1)), inc=(c == NCH - 1))
            for c in range(NCH):
                rec.wait("tensor", "act", sq_acts[c])
                pe_c = rec.pe(lambda e, c=c, sq_buf=sq_buf: e.matmul(
                    psb[3][0:1, :], ones_mean, chk(sq_buf, c),
                    start=(c == 0), stop=(c == NCH - 1)), inc=(c == NCH - 1))
            rec.use(sq_key, "pe", pe_c)
            return pe_c

        def bcasts_45(row_a, row_b, dve_ready):
            rec.wait("tensor", "dve", dve_ready)
            rec.gate("tensor", "b4")
            rec.gate("tensor", "b5")
            rec.pe(lambda e: e.matmul(p4, onesrow, row_a, start=True, stop=True))
            ret = rec.pe(lambda e: e.matmul(p5, onesrow, row_b,
                                            start=True, stop=True), inc=True)
            rec.use("rows", "pe", ret)
            return ret

        def load_weight_fp32(arena, arena_key, src, half=False):
            done = [0] * NCH
            first = True
            nh = NCH if half else 2 * NCH
            for hix in range(nh):
                if half:
                    k, hf = hix, 0
                    srcsl = src[k * 128:(k + 1) * 128, 0:1024]
                    dstsl = arena[:, k * 1024:(k + 1) * 1024]
                else:
                    k, hf = divmod(hix, 2)
                    srcsl = src[k * 128:(k + 1) * 128, hf * 1024:(hf + 1) * 1024]
                    dstsl = arena[:, k * D + hf * 1024:k * D + (hf + 1) * 1024]
                slot = hix % 2
                sl = stg[:, slot * 1024:(slot + 1) * 1024]
                skey = "stgh%d" % slot
                rec.gate("gpsimd", skey)
                if first:
                    rec.gate("gpsimd", "stg")
                c = rec.dma("gpsimd", "w%d" % slot,
                            lambda e, a=sl, b=srcsl: e.dma_start(out=a, in_=b))
                if first:
                    rec.gate("scalar", arena_key)
                    first = False
                rec.wait("scalar", "w%d" % slot, c)
                cc = rec.act(lambda e, a=dstsl, b=sl: e.copy(out=a, in_=b))
                rec.use(skey, "act", cc)
                if half or hf == 1:
                    done[k] = cc
            return done

        def load_weight_bf16(arena, arena_key, src, pc_sem=None, half=False):
            rec.gate("gpsimd", arena_key)
            if pc_sem is not None:
                rec.wait("gpsimd", pc_sem, rec.n[pc_sem])
            rec.wait("gpsimd", "wt", rec.n["wt"])
            w = 1024 if half else D
            for k in range(NCH):
                done = rec.dma("gpsimd", "wt",
                               lambda e, a=arena[:, k * w:(k + 1) * w],
                               b=src[k * 128:(k + 1) * 128, :]:
                               e.dma_start(out=a, in_=b))
            return done

        POOL_ENG = [mybir.EngineType.Pool]

        def emit_switch_pool(arm_fn, sem_incs):
            def closure(e, s):
                for arm in nc.switch_core_id(engines=[e], n=128):
                    arm_fn(int(arm) % 2, e, s)
            rec.raw("gpsimd", closure)
            for sem, inc in sem_incs.items():
                rec.n[sem] += inc

        _pollid = [0]

        def emit_poll(phase, dep_sem, dep_val):
            _cfz = c_fz

            def closure(e, s, phase=phase, _cfz=_cfz):
                e.wait_ge(s[dep_sem], dep_val)
                e.wait_ge(s["act"], _cfz)
                pid = _pollid[0]
                _pollid[0] += 1
                lbl = "poll_%d" % pid
                c2 = lbl + "_c2"
                done = lbl + "_done"
                e.br(lbl)
                with nc.body(lbl, valid_engines=POOL_ENG):
                    e.dma_start(out=flag_rd[0:1, 2 * phase:2 * phase + 1],
                                in_=fl_sh[0:1, phase, 0:1]).then_inc(s["pol"], 16)
                    e.dma_start(out=flag_rd[0:1, 2 * phase + 1:2 * phase + 2],
                                in_=fl_sh[1:2, phase, 0:1]).then_inc(s["pol"], 16)
                    e.tensor_copy(out=dvB[:, 0:256], in_=dvA[:, 0:256])
                    v0 = e.value_load(flag_rd[0:1, 2 * phase:2 * phase + 1])
                    vs = e.value_load(salt_sb[0:1, 0:1])
                    e.br_ne(v0, vs, on_true=lbl, on_false=c2)
                with nc.body(c2, valid_engines=POOL_ENG):
                    v1 = e.value_load(flag_rd[0:1, 2 * phase + 1:2 * phase + 2])
                    vs2 = e.value_load(salt_sb[0:1, 0:1])
                    e.br_ne(v1, vs2, on_true=lbl, on_false=done)
                nc.switch_bb(done)
                e.sem_inc(s["rdz"], 1)
            rec.raw("gpsimd", closure)
            rec.n["rdz"] += 1
            return rec.n["rdz"]

        wt_cast = [c_tern] * NCH

        y2st = [0] * nt_a
        x1st = [0] * nt_own

        pc_jobs = []
        for nm, sem in (("wk", "pck"), ("wv", "pcv"), ("wq", "pcq"),
                        ("wu", "pcu"), ("wo", "pco")):
            nh = 1 if nm in ("wk", "wv") else 2
            for k in range(NCH):
                for half in range(nh):
                    pc_jobs.append((nm, sem, k, half))
        pc_issued = []

        def pc_tick(nissue):
            for (nm, sem, k, half), slot, c in pc_issued:
                sl = stg[:, slot * 1024:(slot + 1) * 1024]
                bsl = (sqb[:, :], wstg[:, :], qh[:, :])[slot]
                bkeyn = "pcs%d" % slot
                rec.wait("scalar", "w%d" % slot, c)
                rec.gate("scalar", bkeyn)
                cc = rec.act(lambda e, a=bsl, b=sl: e.copy(out=a, in_=b))
                rec.use("stgh%d" % slot, "act", cc)
                rec.wait("gpsimd", "act", cc)
                st = rec.dma("gpsimd", sem,
                             lambda e, a=w_bf[nm][k * 128:(k + 1) * 128,
                                                  half * 1024:(half + 1) * 1024],
                             b=bsl: e.dma_start(out=a, in_=b))
                rec.use(bkeyn, sem, st)
                if slot == 0:
                    rec.use("sq0", sem, st)
                    rec.use("sq1", sem, st)
                elif slot == 2:
                    rec.use("qh0", sem, st)
                    rec.use("qh1", sem, st)
            pc_issued.clear()
            for _ in range(nissue):
                if not pc_jobs:
                    return
                nm, sem, k, half = pc_jobs.pop(0)
                if nm == "wk" and k == 0 and half == 0:
                    rec.gate("gpsimd", "stg")
                slot = len(pc_issued)
                assert slot < 2
                sl = stg[:, slot * 1024:(slot + 1) * 1024]
                rec.gate("gpsimd", "stgh%d" % slot)
                c = rec.dma("gpsimd", "w%d" % slot,
                            lambda e, a=sl, b=wT[nm][k * 128:(k + 1) * 128,
                                                     half * 1024:(half + 1) * 1024]:
                            e.dma_start(out=a, in_=b))
                pc_issued.append(((nm, sem, k, half), slot, c))

        def pc_flush(sem):
            while any(j[1] == sem for j in pc_jobs) or \
                    any(j[0][1] == sem for j in pc_issued):
                pc_tick(2)

        def pass_A():
            for i in range(nt_a):
                pc_tick(2)
                xb = xbuf[i % 2]
                xkey = "xbuf%d" % (i % 2)
                lxs = "lx%d" % (i % 2)
                rec.gate("sync", xkey)
                lds = tile_dma_in(xv, i * T, xb, lxs)
                cast_a = []
                for c in range(NCH):
                    rec.wait("scalar", lxs, lds)
                    rec.gate("scalar", "sin")
                    cast_a.append(rec.act(lambda e, c=c, xb=xb: e.copy(
                        out=chk(sin, c), in_=chk(xb, c))))
                rec.use(xkey, "act", cast_a[-1])
                src1 = [(chk(sin, c), cast_a[c]) for c in range(NCH)]
                sq1 = ln_squares(src1, aux_a, "aux_a")
                pe_st1 = ln_stats_mms(src1, sq1, aux_a, "aux_a")
                rec.use("sin", "pe", pe_st1)
                rec.wait("vector", "pe", pe_st1)
                d_r1 = ln_smalls(pe_st1, mul_half_scale=True)
                ffn_pe = [0] * NCH
                bc1 = None

                def ffn_chunk(e_ix):
                    bank = e_ix % 4
                    rec.gate("tensor", bkey[bank])
                    for k in range(NCH):
                        rec.wait("tensor", "act", wt_cast[k])
                        rec.wait("tensor", "act", cast_a[k])
                        ffn_pe[e_ix] = rec.pe(
                            lambda e, k=k, e_ix=e_ix, bank=bank: e.matmul(
                                psb[bank][:, :], wsl(arena0, k, e_ix * 128, 128),
                                chk(sin, k), start=(k == 0), stop=(k == NCH - 1)),
                            inc=(k == NCH - 1))

                evac_d = [0] * NCH

                def evac_chunk(e_ix):
                    bank = e_ix % 4
                    rec.wait("vector", "pe", ffn_pe[e_ix])
                    rec.wait("vector", "pe", bc1)
                    rec.dve(lambda e, e_ix=e_ix: e.tensor_scalar(
                        out=dvA[:, :], in0=p4, scalar1=c_wt[:, e_ix:e_ix + 1],
                        scalar2=None, op0=OP.mult))
                    rec.dve(lambda e, bank=bank: e.tensor_tensor(
                        out=dvB[:, :], in0=psb[bank][:, :], in1=dvA[:, :], op=OP.subtract))
                    rec.dve(lambda e: e.tensor_tensor(
                        out=dvA[:, :], in0=dvB[:, :], in1=p5, op=OP.mult))
                    evac_d[e_ix] = rec.dve(lambda e, e_ix=e_ix, xb=xb: e.scalar_tensor_tensor(
                        out=chk(xb, e_ix), in0=dvA[:, :], scalar=hb[:, e_ix:e_ix + 1],
                        in1=chk(xb, e_ix), op0=OP.add, op1=OP.add))
                    rec.use(bkey[bank], "dve", evac_d[e_ix])
                    rec.use("b4", "dve", evac_d[e_ix])
                    rec.use("b5", "dve", evac_d[e_ix])

                for e_ix in range(4):
                    ffn_chunk(e_ix)
                bc1 = bcasts_45(mrow, rrow, d_r1)
                for e_ix in range(4, NCH):
                    evac_chunk(e_ix - 4)
                    ffn_chunk(e_ix)
                for e_ix in range(NCH - 4, NCH):
                    evac_chunk(e_ix)
                rec.use("sin", "pe", ffn_pe[NCH - 1])
                if i < nt_own:
                    rec.wait("gpsimd", "dve", evac_d[NCH - 1])
                    st = tile_dma_out(x1v, i * T, xb, "sx1", serialize=True)
                    rec.use(xkey, "sx1", st)
                    x1st[i] = st
                cast1_a = []
                for c in range(NCH):
                    rec.wait("scalar", "dve", evac_d[c])
                    rec.gate("scalar", "aux_a")
                    cast1_a.append(rec.act(lambda e, c=c, xb=xb: e.copy(
                        out=chk(aux_a, c), in_=chk(xb, c))))
                rec.use(xkey, "act", cast1_a[-1])
                src2 = [(chk(aux_a, c), cast1_a[c]) for c in range(NCH)]
                sq2 = ln_squares(src2, aux_b, "aux_b")
                pe_st2 = ln_stats_mms(src2, sq2, aux_b, "aux_b")
                rec.use("aux_a", "pe", pe_st2)
                rec.wait("vector", "pe", pe_st2)
                d_r2 = ln_smalls(pe_st2, mul_half_scale=False)
                bc2 = bcasts_45(mrow, rrow, d_r2)
                app_d = [0] * NCH
                for c in range(NCH):
                    rec.wait("vector", "pe", bc2)
                    rec.gate("vector", "aux_b")
                    rec.dve(lambda e, c=c, xb=xb: e.tensor_tensor(
                        out=dvB[:, :], in0=chk(xb, c), in1=p4, op=OP.subtract))
                    app_d[c] = rec.dve(lambda e, c=c: e.tensor_tensor(
                        out=chk(aux_b, c), in0=dvB[:, :], in1=p5, op=OP.mult))
                rec.use("b4", "dve", app_d[-1])
                rec.use("b5", "dve", app_d[-1])
                rec.use(xkey, "dve", app_d[-1])
                rec.wait("gpsimd", "dve", app_d[NCH - 1])
                st2 = tile_dma_out(y2v, i * T, aux_b, "sy2")
                rec.use("aux_b", "sy2", st2)
                y2st[i] = st2

                def arm_sh(par, e, s, i=i):
                    for j in range(4):
                        ssl = aux_b[:, 4 * j * T:(4 * j + 4) * T].rearrange(
                            "p (c t) -> p c t", t=T)
                        dsl = y2shv[:, 4 * j:4 * j + 4,
                                    par * half_cols + i * T:
                                    par * half_cols + (i + 1) * T]
                        e.dma_start(out=dsl, in_=ssl).then_inc(s["sy2s"], 16)
                emit_switch_pool(arm_sh, {"sy2s": 64})
                rec.use("aux_b", "sy2s", rec.n["sy2s"])
            rec.use("arena0", "pe", ffn_pe[NCH - 1])
            rec.merge_use("arena1", "xbuf0", "xbuf1")
            return

        pass_A()

        def pass_BC():
            rec.wait("gpsimd", "sy2", y2st[nt_a - 1])
            rec.wait("gpsimd", "rows", c_salt)

            rec.wait("gpsimd", "sy2s", rec.n["sy2s"])

            def arm_flag1(par, e, s):
                e.dma_start(out=fl_sh[par:par + 1, 0, 0:1],
                            in_=salt_sb[0:1, 0:1]).then_inc(s["sy2s"], 16)
            emit_switch_pool(arm_flag1, {"sy2s": 16})
            pc_flush("pck")
            pc_flush("pcv")
            wk_pool = load_weight_bf16(arena0, "arena0", w_bf["wk"], "pck", half=True)
            wv_pool = load_weight_bf16(arena1, "arena1", w_bf["wv"], "pcv", half=True)
            rdz1 = emit_poll(0, "sy2s", rec.n["sy2s"])
            for b in range(4, 8):
                rec.gate("scalar", bkey[b])
            c_zkv = rec.act(lambda e: e.memzero(p_kv[:, :]))
            rec.wait("tensor", "act", c_zkv)
            kv_last = 0
            for i in range(nt_seq):
                pc_tick(2)
                rec.gate("sync", "sin")
                if i < nt_a:
                    rec.wait("sync", "sy2", y2st[i])
                    lds = ("sin", tile_dma_in(y2v, i * T, sin, "sin"))
                else:
                    rec.wait("gpsimd", "rdz", rdz1)
                    rec.gate("gpsimd", "sin")

                    def arm_ld(par, e, s, i=i):
                        sib = 1 - par
                        for j in range(4):
                            ssl = sin[:, 4 * j * T:(4 * j + 4) * T].rearrange(
                                "p (c t) -> p c t", t=T)
                            dsl = y2shv[:, 4 * j:4 * j + 4,
                                        sib * half_cols + (i - nt_a) * T:
                                        sib * half_cols + (i - nt_a + 1) * T]
                            e.dma_start(out=ssl, in_=dsl).then_inc(s["sinp"], 16)
                    emit_switch_pool(arm_ld, {"sinp": 64})
                    lds = ("sinp", rec.n["sinp"])
                proj_pe = {}
                evac_a = {}
                for pj, (arena, wpool, dstbuf, func) in enumerate((
                        (arena0, wk_pool, aux_a, "K"),
                        (arena1, wv_pool, aux_b, "V"))):
                    for s in range(4):
                        for strip in range(2):
                            bank = 2 * pj + (strip % 2)
                            rec.gate("tensor", bkey[bank])
                            rec.wait("tensor", lds[0], lds[1])
                            rec.wait("tensor", "wt", wpool)
                            for k in range(NCH):
                                pc = rec.pe(lambda e, k=k, s=s, strip=strip, bank=bank,
                                            arena=arena: e.matmul(
                                    psb[bank][:, :],
                                    sin[:, k * T + s * 128:k * T + (s + 1) * 128],
                                    arena[:, k * 1024 + strip * 512:
                                          k * 1024 + (strip + 1) * 512],
                                    start=(k == 0), stop=(k == NCH - 1)),
                                    inc=(k == NCH - 1))
                            proj_pe[(pj, s, strip)] = pc
                            rec.wait("scalar", "pe", pc)
                            rec.gate("scalar", "aux_a" if pj == 0 else "aux_b")
                            dst = dstbuf[:, s * 1024 + strip * 512:
                                         s * 1024 + (strip + 1) * 512]
                            if func == "K":
                                ca = rec.act(lambda e, dst=dst, bank=bank: e.activation(
                                    out=dst, in_=psb[bank][:, :], func=AF.Relu,
                                    scale=ISCALE))
                            else:
                                ca = rec.act(lambda e, dst=dst, bank=bank: e.copy(
                                    out=dst, in_=psb[bank][:, :]))
                            evac_a[(pj, s, strip)] = ca
                            rec.use(bkey[bank], "act", ca)
                rec.use("sin", "pe", pc)
                for s in range(4):
                    for strip in range(2):
                        rec.wait("tensor", "act", evac_a[(0, s, strip)])
                        rec.wait("tensor", "act", evac_a[(1, s, strip)])
                    for h in range(HEADS // 2):
                        kv_last = rec.pe(lambda e, s=s, h=h: e.matmul(
                            p_kv[:, h * 128:(h + 1) * 128],
                            aux_a[:, s * 1024 + h * 128:s * 1024 + (h + 1) * 128],
                            aux_b[:, s * 1024 + h * 128:s * 1024 + (h + 1) * 128],
                            start=False, stop=True, skip_group_check=True),
                            inc=(h == HEADS // 2 - 1))
                rec.use("aux_a", "pe", kv_last)
                rec.use("aux_b", "pe", kv_last)
            while pc_jobs or pc_issued:
                pc_tick(2)
            rec.use("arena0", "pe", kv_last)
            rec.use("arena1", "pe", kv_last)
            rec.wait("scalar", "pe", kv_last)
            kv_acts = []
            for j in range(2):
                kv_acts.append(rec.act(lambda e, j=j: e.copy(
                    out=kvbf[:, j * 512:(j + 1) * 512], in_=p_kv[:, j * 512:(j + 1) * 512])))
                rec.use(bkey[4 + j], "act", kv_acts[-1])
            rec.wait("gpsimd", "act", kv_acts[-1])

            def arm_kv(par, e, s):
                e.dma_start(out=kv_sh[par, :, :],
                            in_=kvbf[:, 0:1024]).then_inc(s["kvx"], 16)
            emit_switch_pool(arm_kv, {"kvx": 16})
            rec.wait("gpsimd", "kvx", rec.n["kvx"])

            def arm_flag2(par, e, s):
                e.dma_start(out=fl_sh[par:par + 1, 1, 0:1],
                            in_=salt_sb[0:1, 0:1]).then_inc(s["kvx"], 16)
            emit_switch_pool(arm_flag2, {"kvx": 16})
            rec.use("kvbf", "kvx", rec.n["kvx"])
            rdz2 = emit_poll(1, "kvx", rec.n["kvx"])
            rec.wait("gpsimd", "rdz", rdz2)
            rec.gate("gpsimd", "kvbf")
            for j in range(2):
                kvl = rec.dma("gpsimd", "kvx", lambda e, j=j: e.dma_start(
                    out=kvbf[:, j * 1024:(j + 1) * 1024], in_=kv_sh[j, :, :]))
            kv_acts = [("kvx", kvl)]
            if debug:
                rec.gate("gpsimd", "stg")
                for j in range(4):
                    rec.wait("vector", "pe", kv_last)
                    cd = rec.dve(lambda e, j=j: e.tensor_copy(
                        out=stg[:, j * 512:(j + 1) * 512], in_=p_kv[:, j * 512:(j + 1) * 512]))
                rec.wait("gpsimd", "dve", cd)
                c_kvd = rec.dma("gpsimd", "misc", lambda e: e.dma_start(out=kv_dbg[:, :], in_=stg[:, :]))
                rec.use("stg", "misc", c_kvd)
                rec.use("stg", "dve", cd)
            return kv_acts[-1]

        pass_BC()

        def pass_DE():
            pc_flush("pcq")
            pc_flush("pcu")
            wq_pool = load_weight_bf16(arena0, "arena0", w_bf["wq"], "pcq")
            wu_pool = load_weight_bf16(arena1, "arena1", w_bf["wu"], "pcu")
            rdz2 = emit_poll(1, "kvx", rec.n["kvx"])
            rec.wait("gpsimd", "rdz", rdz2)
            rec.gate("gpsimd", "kvbf")
            for j in range(2):
                kvl = rec.dma("gpsimd", "kvx", lambda e, j=j: e.dma_start(
                    out=kvbf[:, j * 1024:(j + 1) * 1024], in_=kv_sh[j, :, :]))
            kv_ready = ("kvx", kvl)
            last_pe = 0
            ost = [0] * nt_own
            for t in range(nt_own):
                rec.gate("sync", "sin")
                lds = tile_dma_in(y2v, t * T, sin, "sin")
                q_pe = [0] * HEADS
                u_pe = [0] * HEADS
                o_pe = [0] * HEADS
                ss_pe = [0] * HEADS
                bc_pe = [0] * HEADS
                qev = [0] * HEADS
                uev = [0] * HEADS
                sqo = [0] * HEADS
                of_d = [0] * HEADS

                def emit_q(h):
                    bank = h % 2
                    rec.gate("tensor", bkey[bank])
                    rec.wait("tensor", "sin", lds)
                    rec.wait("tensor", "wt", wq_pool)
                    for k in range(NCH):
                        q_pe[h] = rec.pe(lambda e, k=k, h=h, bank=bank: e.matmul(
                            psb[bank][:, :], wsl(arena0, k, h * 128, 128),
                            chk(sin, k), start=(k == 0), stop=(k == NCH - 1)),
                            inc=(k == NCH - 1))
                    rec.wait("scalar", "pe", q_pe[h])
                    rec.gate("scalar", "qh%d" % (h % 2))
                    qev[h] = rec.act(lambda e, h=h, bank=bank: e.activation(
                        out=chk(qh, h % 2), in_=psb[bank][:, :], func=AF.Relu,
                        scale=ISCALE))
                    rec.use(bkey[bank], "act", qev[h])

                def emit_u(h):
                    bank = 2 + h % 2
                    rec.gate("tensor", bkey[bank])
                    rec.wait("tensor", "sin", lds)
                    rec.wait("tensor", "wt", wu_pool)
                    for k in range(NCH):
                        u_pe[h] = rec.pe(lambda e, k=k, h=h, bank=bank: e.matmul(
                            psb[bank][:, :], wsl(arena1, k, h * 128, 128),
                            chk(sin, k), start=(k == 0), stop=(k == NCH - 1)),
                            inc=(k == NCH - 1))
                    rec.wait("scalar", "pe", u_pe[h])
                    rec.gate("scalar", "uh%d" % (h % 2))
                    uev[h] = rec.act(lambda e, h=h, bank=bank: e.activation(
                        out=chk(uh, h % 2), in_=psb[bank][:, :], func=AF.Silu))
                    rec.use(bkey[bank], "act", uev[h])

                def emit_o(h):
                    pb = p4 if h % 2 == 0 else p5
                    bk = "b4" if h % 2 == 0 else "b5"
                    rec.wait("tensor", "act", qev[h])
                    rec.wait("tensor", kv_ready[0], kv_ready[1])
                    rec.gate("tensor", bk)
                    o_pe[h] = rec.pe(lambda e, h=h, pb=pb: e.matmul(
                        pb, kvbf[:, h * 128:(h + 1) * 128], chk(qh, h % 2),
                        start=True, stop=True), inc=True)
                    rec.use("qh%d" % (h % 2), "pe", o_pe[h])
                    rec.wait("scalar", "pe", o_pe[h])
                    rec.gate("scalar", "sq%d" % (h % 2))
                    sqo[h] = rec.act(lambda e, h=h, pb=pb: e.activation(
                        out=chk(sqb, h % 2), in_=pb, func=AF.Square))

                def emit_ss(h):
                    rec.wait("tensor", "act", sqo[h])
                    rec.gate("tensor", "b6")
                    ss_pe[h] = rec.pe(lambda e, h=h: e.matmul(
                        p6[0:1, :], ones_one, chk(sqb, h % 2),
                        start=True, stop=True), inc=True)
                    rec.use("sq%d" % (h % 2), "pe", ss_pe[h])
                    rec.wait("scalar", "pe", ss_pe[h])
                    rec.gate("scalar", "rows")
                    a_a = rec.act(lambda e: e.activation(
                        out=srow, in_=p6[0:1, :], func=AF.Sqrt, scale=1.0 / 128.0))
                    rec.use("b6", "act", a_a)
                    rec.wait("vector", "act", a_a)
                    rec.gate("vector", "rows")
                    rec.dve(lambda e: e.tensor_scalar(
                        out=trow, in0=srow, scalar1=1e-12, scalar2=None, op0=OP.max))
                    return rec.dve(lambda e: e.reciprocal(out=rrow, in_=trow))

                def emit_bc(h, d_recip):
                    rec.wait("tensor", "dve", d_recip)
                    rec.gate("tensor", "b7")
                    bc_pe[h] = rec.pe(lambda e: e.matmul(
                        p7, onesrow, rrow, start=True, stop=True), inc=True)
                    rec.use("rows", "pe", bc_pe[h])

                def emit_gate(h):
                    pb = p4 if h % 2 == 0 else p5
                    bk = "b4" if h % 2 == 0 else "b5"
                    g1buf = dvA[:, :] if h % 2 == 0 else dvB[:, :]
                    rec.wait("vector", "pe", o_pe[h])
                    rec.wait("vector", "act", uev[h])
                    rec.dve(lambda e, pb=pb, g1buf=g1buf, h=h: e.tensor_tensor(
                        out=g1buf, in0=pb, in1=chk(uh, h % 2), op=OP.mult))
                    rec.use("uh%d" % (h % 2), "dve", rec.n["dve"])
                    rec.wait("vector", "pe", bc_pe[h])
                    of_d[h] = rec.dve(lambda e, h=h, g1buf=g1buf: e.tensor_tensor(
                        out=chk(aux_a, h), in0=g1buf, in1=p7, op=OP.mult))
                    rec.use(bk, "dve", of_d[h])
                    rec.use("b7", "dve", of_d[h])

                d_recip = [0] * HEADS
                for h in range(HEADS):
                    emit_q(h)
                    if h >= 2:
                        emit_bc(h - 2, d_recip[h - 2])
                        emit_gate(h - 2)
                    if h >= 1:
                        d_recip[h - 1] = emit_ss(h - 1)
                    if h == 0:
                        rec.gate("vector", "aux_a")
                    emit_u(h)
                    emit_o(h)
                emit_bc(HEADS - 2, d_recip[HEADS - 2])
                emit_gate(HEADS - 2)
                d_recip[HEADS - 1] = emit_ss(HEADS - 1)
                emit_bc(HEADS - 1, d_recip[HEADS - 1])
                emit_gate(HEADS - 1)
                rec.wait("gpsimd", "dve", of_d[HEADS - 1])
                st = tile_dma_out(ov, t * T, aux_a, "so")
                rec.use("aux_a", "so", st)
                ost[t] = st
                rec.use("sin", "pe", max(q_pe[HEADS - 1], u_pe[HEADS - 1]))
                last_pe = max(q_pe[HEADS - 1], u_pe[HEADS - 1])
            rec.use("arena0", "pe", last_pe)
            rec.use("arena1", "pe", last_pe)
            rec.merge_use("xbuf0", "arena1")
            rec.merge_use("xbuf1", "arena1")
            return ost

        ost = pass_DE()

        x2st = [0] * nt_own
        xb2st = [0] * nt_own
        rowst = [0] * nt_own

        def pass_F():
            pc_flush("pco")
            wo_pool = load_weight_bf16(arena0, "arena0", w_bf["wo"], "pco")
            for t in range(nt_own):
                xb = xbuf[t % 2]
                xkey = "xbuf%d" % (t % 2)
                lxs = "lx%d" % (t % 2)
                rec.gate("sync", "sin")
                rec.wait("sync", "so", ost[nt_own - 1])
                lds = tile_dma_in(ov, t * T, sin, "sin")
                rec.gate("sync", xkey)
                rec.wait("sync", "sx1", x1st[nt_own - 1])
                lds_x = tile_dma_in(x1v, t * T, xb, lxs)
                at_pe = [0] * NCH
                x2_d = [0] * NCH

                def attn_chunk(e_ix):
                    bank = e_ix % 4
                    rec.gate("tensor", bkey[bank])
                    rec.wait("tensor", "sin", lds)
                    rec.wait("tensor", "wt", wo_pool)
                    for k in range(NCH):
                        at_pe[e_ix] = rec.pe(lambda e, k=k, e_ix=e_ix, bank=bank:
                                             e.matmul(
                            psb[bank][:, :], wsl(arena0, k, e_ix * 128, 128),
                            chk(sin, k), start=(k == 0), stop=(k == NCH - 1)),
                            inc=(k == NCH - 1))

                def x2_chunk(e_ix):
                    bank = e_ix % 4
                    rec.wait("vector", "pe", at_pe[e_ix])
                    rec.wait("vector", lxs, lds_x)
                    x2_d[e_ix] = rec.dve(lambda e, e_ix=e_ix, bank=bank, xb=xb: e.tensor_tensor(
                        out=chk(xb, e_ix), in0=psb[bank][:, :], in1=chk(xb, e_ix),
                        op=OP.add))
                    rec.use(bkey[bank], "dve", x2_d[e_ix])

                for e_ix in range(4):
                    attn_chunk(e_ix)
                for e_ix in range(4, NCH):
                    x2_chunk(e_ix - 4)
                    attn_chunk(e_ix)
                for e_ix in range(NCH - 4, NCH):
                    x2_chunk(e_ix)
                rec.use("sin", "pe", at_pe[NCH - 1])
                rec.wait("gpsimd", "dve", x2_d[NCH - 1])
                stx = tile_dma_out(x2v, t * T, xb, "sx2", serialize=True)
                cast2_a = []
                for c in range(NCH):
                    rec.wait("scalar", "dve", x2_d[c])
                    rec.gate("scalar", "aux_a")
                    cast2_a.append(rec.act(lambda e, c=c, xb=xb: e.copy(
                        out=chk(aux_a, c), in_=chk(xb, c))))
                rec.wait("gpsimd", "act", cast2_a[NCH - 1])
                stb = tile_dma_out(xb2v, t * T, aux_a, "sxb2")
                rec.use("aux_a", "sxb2", stb)
                rec.use(xkey, "sx2", stx)
                rec.use(xkey, "act", cast2_a[NCH - 1])
                x2st[t] = stx
                xb2st[t] = stb
            rec.use("arena0", "pe", at_pe[NCH - 1])

        pass_F()

        def pass_G():
            wt_pool = load_weight_bf16(arena0, "arena0", wt_s)
            out_last = 0
            for t in range(nt_own):
                xb = xbuf[t % 2]
                xkey = "xbuf%d" % (t % 2)
                lxs = "lx%d" % (t % 2)
                rec.gate("sync", "sin")
                rec.wait("sync", "sxb2", xb2st[nt_own - 1])
                lds = tile_dma_in(xb2v, t * T, sin, "sin")
                rec.gate("sync", xkey)
                rec.wait("sync", "sx2", x2st[nt_own - 1])
                lds_x = tile_dma_in(x2v, t * T, xb, lxs)
                srcg = [(chk(sin, c), lds) for c in range(NCH)]
                for c in range(NCH):
                    rec.wait("scalar", "sin", lds)
                sqg = ln_squares(srcg, aux_a, "aux_a")
                rec.wait("tensor", "sin", lds)
                pe_stg = ln_stats_mms([(chk(sin, c), None) for c in range(NCH)],
                                      sqg, aux_a, "aux_a")
                rec.use("sin", "pe", pe_stg)
                d_r3 = ln_smalls(pe_stg, mul_half_scale=True)
                bc3 = bcasts_45(mrow, rrow, d_r3)
                ffn_pe = [0] * NCH
                out_dv = [0] * NCH

                def ffn2_chunk(e_ix):
                    bank = e_ix % 4
                    rec.gate("tensor", bkey[bank])
                    rec.wait("tensor", "wt", wt_pool)
                    rec.wait("tensor", "sin", lds)
                    for k in range(NCH):
                        ffn_pe[e_ix] = rec.pe(lambda e, k=k, e_ix=e_ix, bank=bank:
                                              e.matmul(
                            psb[bank][:, :], wsl(arena0, k, e_ix * 128, 128),
                            chk(sin, k), start=(k == 0), stop=(k == NCH - 1)),
                            inc=(k == NCH - 1))

                def oev_chunk(e_ix):
                    bank = e_ix % 4
                    rec.wait("vector", "pe", ffn_pe[e_ix])
                    rec.wait("vector", "pe", bc3)
                    rec.wait("vector", lxs, lds_x)
                    rec.dve(lambda e, e_ix=e_ix: e.tensor_scalar(
                        out=dvA[:, :], in0=p4, scalar1=c_wt[:, e_ix:e_ix + 1],
                        scalar2=None, op0=OP.mult))
                    rec.dve(lambda e, bank=bank: e.tensor_tensor(
                        out=dvB[:, :], in0=psb[bank][:, :], in1=dvA[:, :], op=OP.subtract))
                    rec.dve(lambda e: e.tensor_tensor(
                        out=dvA[:, :], in0=dvB[:, :], in1=p5, op=OP.mult))
                    out_dv[e_ix] = rec.dve(lambda e, e_ix=e_ix, xb=xb: e.scalar_tensor_tensor(
                        out=chk(xb, e_ix), in0=dvA[:, :], scalar=hb[:, e_ix:e_ix + 1],
                        in1=chk(xb, e_ix), op0=OP.add, op1=OP.add))
                    rec.use(bkey[bank], "dve", out_dv[e_ix])
                    rec.use("b4", "dve", out_dv[e_ix])
                    rec.use("b5", "dve", out_dv[e_ix])

                for e_ix in range(4):
                    ffn2_chunk(e_ix)
                for e_ix in range(4, NCH):
                    oev_chunk(e_ix - 4)
                    ffn2_chunk(e_ix)
                for e_ix in range(NCH - 4, NCH):
                    oev_chunk(e_ix)
                rec.use("sin", "pe", ffn_pe[NCH - 1])
                rec.wait("gpsimd", "dve", out_dv[NCH - 1])
                st = tile_dma_out(outv, t * T, xb, "sout")
                rec.use(xkey, "sout", st)
                out_last = st
            rec.raw("gpsimd", lambda e, s, v=out_last: e.wait_ge(s["sout"], v))

        pass_G()

        @block.sync
        def _(eng):
            for fn in rec.streams["sync"]:
                fn(eng, sems)

        @block.scalar
        def _(eng):
            for fn in rec.streams["scalar"]:
                fn(eng, sems)

        @block.vector
        def _(eng):
            for fn in rec.streams["vector"]:
                fn(eng, sems)

        @block.tensor
        def _(eng):
            for fn in rec.streams["tensor"]:
                fn(eng, sems)

        @block.gpsimd
        def _(eng):
            for fn in rec.streams["gpsimd"]:
                fn(eng, sems)

    return nc


_NC_CACHE = {}


def _get_nc(debug=False):
    if debug not in _NC_CACHE:
        _NC_CACHE[debug] = build_nc(debug)
    return _NC_CACHE[debug]


def _consts_array():
    c = np.zeros((128, 144), np.float32)
    c[:, 0] = 1.0
    c[:, 1] = 0.5
    c[:, 2] = 1.0 / 2048.0
    c[:, 4:132] = 1.0
    for i, v in enumerate((EPS_LN, ISCALE, 1.0 / 128.0, 0.0, 1.0,
                           1e-12, 0.5, 1.0 / NW, EPS_T)):
        c[:, 132 + i] = v
    return c


def kernel(L0_attn_x, L0_attn_wq, L0_attn_wk, L0_attn_wv, L0_attn_wu,
           L0_attn_wo, L0_ffn_weight, L0_ffn_bias, _debug=False):
    x = np.asarray(L0_attn_x, dtype=np.float32)
    x_fm = np.ascontiguousarray(x.reshape(2 * SEQ, D).T)
    wkT_full = np.asarray(L0_attn_wk, np.float32).T
    wvT_full = np.asarray(L0_attn_wv, np.float32).T
    wts = {
        "wqT": np.ascontiguousarray(np.asarray(L0_attn_wq, np.float32).T),
        "wuT": np.ascontiguousarray(np.asarray(L0_attn_wu, np.float32).T),
        "woT": np.ascontiguousarray(np.asarray(L0_attn_wo, np.float32).T),
        "wfT": np.ascontiguousarray(np.asarray(L0_ffn_weight, np.float32).T),
    }
    bias = np.ascontiguousarray(np.asarray(L0_ffn_bias, np.float32).reshape(16, 128).T)
    consts = _consts_array()

    salt = np.random.default_rng().integers(1, 2 ** 31, size=(1, 4)).astype(np.uint32)
    in_maps = []
    for c in range(8):
        sq = c // 4
        q = c % 4
        par = c % 2
        base = sq * SEQ
        own = x_fm[:, base + q * QTOK: base + (q + 1) * QTOK]
        fq = (q + 2) % 4
        foreign = x_fm[:, base + fq * QTOK: base + (fq + 1) * QTOK]
        xc = np.ascontiguousarray(np.concatenate([own, foreign], axis=1))
        m = {"x": xc, "bias": bias, "consts_in": consts, "salt": salt}
        m["wkT"] = np.ascontiguousarray(wkT_full[:, par * 1024:(par + 1) * 1024])
        m["wvT"] = np.ascontiguousarray(wvT_full[:, par * 1024:(par + 1) * 1024])
        m.update(wts)
        in_maps.append(m)

    nc = _get_nc(_debug)
    res = run_bass_kernel_spmd(nc, in_maps, core_ids=list(range(8)))
    shards = [res.results[c]["out"] for c in range(8)]
    full = np.concatenate(shards, axis=1)
    out = np.ascontiguousarray(full.T).reshape(2, SEQ, D)
    if _debug:
        return out, res
    return out
